# version 01
# baseline (speedup 1.0000x reference)
import numpy as np
import ml_dtypes
from contextlib import ExitStack

import concourse.bass as bass
import concourse.mybir as mybir
from concourse.bass_utils import run_bass_kernel_spmd

F32 = mybir.dt.float32
BF16 = mybir.dt.bfloat16
I32 = mybir.dt.int32
AF = mybir.ActivationFunctionType
ALU = mybir.AluOpType
AX = mybir.AxisListType

D = 1024
LK = 8192
NQ = 2048
CC = 8
KB = 16
QKB = 4
PAIRS = 8
HEADS = 16
QB = 4
NKT = 64
GRP = 2
CH = 8
GPC = CH // GRP
NCH = NKT // CH
NPQ = PAIRS * QB
NGRP = NPQ * NKT // GRP
CSH = 10.0
EPS = 1e-5
FC = 4
NE = 9
TQ = 4
VSLOT = 65

LNA = 5 * KB
TPA = 8 * 4 * KB
XNA = 8 * 4 * KB
RIA = KB
RIB = RIA + NPQ


def build_program(dump=None):
    nc = bass.Bass("TRN2", target_bir_lowering=False,
                   detect_race_conditions=False)

    xb = nc.dram_tensor("xb", [LK, D], F32, kind="ExternalInput")
    WqT = nc.dram_tensor("WqT", [D, D], F32, kind="ExternalInput")
    WkT = nc.dram_tensor("WkT", [D, D], F32, kind="ExternalInput")
    WvT = nc.dram_tensor("WvT", [D, D], F32, kind="ExternalInput")
    WoT = nc.dram_tensor("WoT", [D, D], F32, kind="ExternalInput")
    bo = nc.dram_tensor("bo", [D], F32, kind="ExternalInput")
    Ws1 = nc.dram_tensor("Ws1", [D, 512], F32, kind="ExternalInput")
    Ws2 = nc.dram_tensor("Ws2", [512, D], F32, kind="ExternalInput")
    W1 = nc.dram_tensor("W1", [8, D, 512], F32, kind="ExternalInput")
    W2 = nc.dram_tensor("W2", [8, 512, D], F32, kind="ExternalInput")
    Wr = nc.dram_tensor("Wr", [D, 8], F32, kind="ExternalInput")
    out_d = nc.dram_tensor("out", [NQ, D], F32, kind="ExternalOutput")

    KT_dram = nc.dram_tensor("KT_scr", [128, PAIRS, LK], BF16)
    V_dram = nc.dram_tensor("V_scr", [128, HEADS, NKT * VSLOT], BF16)
    OT_dram = nc.dram_tensor("OT_scr", [HEADS, 64, NQ], BF16)

    ident_dram = nc.inline_tensor(np.eye(128, dtype=ml_dtypes.bfloat16), "ident_c")

    dumps = {}
    if dump == "A":
        dumps["d_qt"] = nc.dram_tensor("d_qt", [128, PAIRS * NQ], BF16, kind="ExternalOutput")
        dumps["d_kt"] = nc.dram_tensor("d_kt", [128, PAIRS, LK], BF16, kind="ExternalOutput")
        dumps["d_v"] = nc.dram_tensor("d_v", [128, HEADS, NKT * VSLOT], BF16, kind="ExternalOutput")
    if dump == "B":
        dumps["d_ot"] = nc.dram_tensor("d_ot", [HEADS, 64, NQ], BF16, kind="ExternalOutput")
    if dump == "C":
        dumps["d_x1"] = nc.dram_tensor("d_x1", [128, 16 * D], F32, kind="ExternalOutput")
    if dump == "D":
        dumps["d_xn2"] = nc.dram_tensor("d_xn2", [128, CC * NQ], BF16, kind="ExternalOutput")
        dumps["d_prob"] = nc.dram_tensor("d_prob", [128, 16 * 8], BF16, kind="ExternalOutput")
        dumps["d_probT"] = nc.dram_tensor("d_probT", [8, NQ], BF16, kind="ExternalOutput")

    g = ExitStack()
    with g:
        sem = lambda n: g.enter_context(nc.semaphore(n))

        ident = g.enter_context(nc.sbuf_tensor("ident", [128, 128], BF16))
        biasC = g.enter_context(nc.sbuf_tensor("biasC", [128, 1], F32))
        epsc = g.enter_context(nc.sbuf_tensor("epsc", [128, 1], F32))
        ones64 = g.enter_context(nc.sbuf_tensor("ones64", [128, 64], F32))
        onesbc = g.enter_context(nc.sbuf_tensor("onesbc", [128, 512], BF16))
        scrq = g.enter_context(nc.sbuf_tensor("scrq", [128, 4 * D], F32))
        junk0 = g.enter_context(nc.sbuf_tensor("junk0", [128, 8], F32))

        s_ph = sem("s_ph")
        s_x = sem("s_x")
        s_wc = sem("s_wc")
        s_ln = sem("s_ln")
        s_tp = sem("s_tp")
        s_xnb = sem("s_xnb")
        s_kmm = sem("s_kmm")
        s_kev = sem("s_kev")
        s_stg = sem("s_stg")
        s_kvld = sem("s_kvld")
        s_sc = sem("s_sc")
        s_exp = sem("s_exp")
        s_avc = sem("s_avc")
        s_oadd = sem("s_oadd")
        s_rinv = sem("s_rinv")
        s_rb = sem("s_rb")
        s_nrm = sem("s_nrm")
        s_ot = sem("s_ot")
        s_cld = sem("s_cld")
        s_cmm = sem("s_cmm")
        s_cev = sem("s_cev")
        s_lg = sem("s_lg")
        s_lge = sem("s_lge")
        s_rt = sem("s_rt")
        s_wld = sem("s_wld")
        s_pb = sem("s_pb")
        s_pbe = sem("s_pbe")
        s_hmm = sem("s_hmm")
        s_hact = sem("s_hact")
        s_he = sem("s_he")
        s_eomm = sem("s_eomm")
        s_eoadd = sem("s_eoadd")
        s_oev = sem("s_oev")
        s_od = sem("s_od")

        PH = [0]
        block = g.enter_context(nc.Block())

        s_ab = ExitStack()
        QT_all = s_ab.enter_context(nc.sbuf_tensor("QT_all", [128, PAIRS * NQ], BF16))

        a = ExitStack()
        asb = lambda n, s, d: a.enter_context(nc.sbuf_tensor(n, s, d))
        aps = lambda n, s, d: a.enter_context(nc.psum_tensor(n, s, d))
        WqT_sb = asb("WqT_sb", [128, CC * D], BF16)
        WkT_sb = asb("WkT_sb", [128, CC * D], BF16)
        WvT_sb = asb("WvT_sb", [128, CC * D], BF16)
        xblk = [asb(f"xblk{i}", [128, 4 * D], F32) for i in range(2)]
        xn_t = [asb(f"xn_t{i}", [128, D], BF16) for i in range(4)]
        st_s1 = asb("st_s1", [128, 4], F32)
        st_s2 = asb("st_s2", [128, 4], F32)
        st_m = asb("st_m", [128, 4], F32)
        st_q = asb("st_q", [128, 4], F32)
        st_lt = asb("st_lt", [128, 4], F32)
        st_t = asb("st_t", [128, 4], F32)
        st_v = asb("st_v", [128, 4], F32)
        st_r = asb("st_r", [128, 4], F32)
        xnT = [asb(f"xnT{i}", [128, CC * 512], BF16) for i in range(2)]
        KT_stage = [asb(f"KT_st{i}", [128, PAIRS * 512], BF16) for i in range(2)]
        V_stage = [asb(f"V_st{i}", [128, HEADS * 260], BF16) for i in range(2)]
        ps_T = [aps(f"ps_T{i}", [128, 128], BF16) for i in range(4)]
        ps_M = [aps(f"ps_M{i}", [128, 512], F32) for i in range(2)]

        kvq_sched = []
        for kb in range(KB):
            for pr in range(PAIRS):
                kvq_sched.append((kb, "K", pr))
            for kt in range(4):
                for half in range(2):
                    kvq_sched.append((kb, "V", (kt, half)))
            if kb < QKB:
                for pr in range(PAIRS):
                    kvq_sched.append((kb, "Q", pr))
        kb_end_count = {}
        for i, (kb, kind, argv) in enumerate(kvq_sched):
            kb_end_count[kb] = i + 1
        kb_start = {kb: (kb_end_count[kb - 1] if kb else 0) for kb in range(KB)}

        @block.sync
        def _(sync):
            sync.dma_start(out=ident[:, :], in_=ident_dram.ap()).then_inc(s_x, 16)
            for kb in range(KB):
                if kb >= 2:
                    sync.wait_ge(s_ln, 5 * (kb - 1))
                sync.dma_start(
                    out=xblk[kb % 2][:, :].rearrange("p (s d) -> p s d", s=4),
                    in_=xb.ap().rearrange("(kb s p) d -> kb p s d", p=128, s=4)[kb],
                ).then_inc(s_x, 16)
            sync.sem_inc(s_ph, 1)

        @block.gpsimd
        def _(gp):
            for W, Wsb in ((WqT, WqT_sb), (WkT, WkT_sb), (WvT, WvT_sb)):
                gp.dma_start(
                    out=Wsb[:, :].rearrange("p (cc d) -> p cc d", cc=CC),
                    in_=W.ap().rearrange("(cc p) d -> p cc d", p=128),
                ).then_inc(s_wc, 16)
            for i in range(2):
                vst = V_stage[i][:, :].rearrange("p (h kt c) -> p h kt c", h=HEADS, kt=4)
                gp.memset(vst[:, :, :, 64:65], 1.0)
            for kb in range(KB):
                gp.wait_ge(s_kev, kb_end_count[kb])
                gp.dma_start(
                    out=KT_dram.ap()[:, :, kb * 512:(kb + 1) * 512],
                    in_=KT_stage[kb % 2][:, :].rearrange("p (pr c) -> p pr c", pr=PAIRS),
                ).then_inc(s_stg, 16)
                gp.dma_start(
                    out=V_dram.ap()[:, :, kb * 260:(kb + 1) * 260],
                    in_=V_stage[kb % 2][:, :].rearrange("p (h c) -> p h c", h=HEADS),
                ).then_inc(s_stg, 16)
            gp.wait_ge(s_stg, 32 * KB)
            gp.sem_inc(s_ph, 1)

        @block.vector
        def _(v):
            v.memset(biasC[:, :], -CSH)
            v.memset(epsc[:, :], EPS)
            v.memset(ones64[64:65, :], 1.0)
            v.memset(onesbc[:, :], 1.0)
            v.memset(scrq[:, :], 1.0)
            v.memset(junk0[:, :], 1.0)
            for _w in range(3):
                v.reduce_sum(junk0[:, 0:1], scrq[:, 0:D], axis=AX.X)
                v.tensor_tensor(scrq[:, 0:D], scrq[:, 0:D], scrq[:, 0:D], ALU.mult)
                v.tensor_tensor(junk0[:, 4:8], junk0[:, 0:4], junk0[:, 0:4],
                                ALU.subtract)
                v.tensor_scalar(junk0[:, 0:4], junk0[:, 4:8], 0.5, None, ALU.mult)
                v.tensor_copy(junk0[:, 4:8], junk0[:, 0:4])
                v.reciprocal(junk0[:, 0:4], scrq[:, 0:4])
            for kb in range(KB):
                v.wait_ge(s_x, 16 * (kb + 2))
                for t in range(4):
                    xt = xblk[kb % 2][:, t * D:(t + 1) * D]
                    sq = scrq[:, t * D:(t + 1) * D]
                    v.reduce_sum(st_s1[:, t:t + 1], xt, axis=AX.X)
                    v.tensor_tensor(sq, xt, xt, ALU.mult)
                    v.reduce_sum(st_s2[:, t:t + 1], sq, axis=AX.X)
                v.tensor_scalar(st_m[:, :], st_s1[:, :], 1.0 / D, None, ALU.mult)
                v.tensor_scalar(st_q[:, :], st_s2[:, :], 1.0 / D, None, ALU.mult)
                v.tensor_copy(junk0[:, 0:4], st_m[:, :])
                v.tensor_copy(junk0[:, 4:8], st_m[:, :])
                v.tensor_tensor(st_t[:, :], st_m[:, :], st_m[:, :], ALU.mult)
                v.tensor_copy(junk0[:, 0:4], st_t[:, :])
                v.tensor_copy(junk0[:, 4:8], st_t[:, :])
                v.tensor_tensor(st_v[:, :], st_q[:, :], st_t[:, :],
                                ALU.subtract).then_inc(s_ln, 1)
                v.wait_ge(s_rinv, kb + 1)
                for t in range(4):
                    T = 4 * kb + t
                    if T >= 4:
                        v.wait_ge(s_tp, 8 * (T - 3))
                    xt = xblk[kb % 2][:, t * D:(t + 1) * D]
                    v.tensor_scalar(xn_t[t][:, :], xt, st_m[:, t:t + 1],
                                    st_r[:, t:t + 1], ALU.subtract,
                                    ALU.mult).then_inc(s_ln, 1)
                for t in range(4):
                    T = 4 * kb + t
                    if kb >= 2 and t == 0:
                        v.wait_ge(s_kmm, kb_end_count[kb - 2])
                    dst = xnT[kb % 2][:, :].rearrange("p (cc c) -> p cc c", cc=CC)
                    for c in range(CC):
                        gt = T * 8 + c
                        v.wait_ge(s_tp, gt + 1)
                        v.tensor_copy(dst[:, c, t * 128:(t + 1) * 128],
                                      ps_T[gt % 4][:, :]).then_inc(s_xnb, 1)
            v.sem_inc(s_ph, 1)

        @block.scalar
        def _(sc):
            gidx = 0
            for kb in range(KB):
                sc.wait_ge(s_ln, 5 * kb + 1)
                sc.activation(st_lt[:, :], st_v[:, :], AF.Ln, bias=epsc[:, :], scale=1.0)
                sc.activation(st_r[:, :], st_lt[:, :], AF.Exp, bias=0.0,
                              scale=-0.5).then_inc(s_rinv, 1)
                for _ in range(kb_end_count[kb] - kb_start[kb]):
                    kbx, kind, argv = kvq_sched[gidx]
                    sc.wait_ge(s_kmm, gidx + 1)
                    if kb >= 2 and gidx == kb_start[kb]:
                        sc.wait_ge(s_stg, 32 * (kb - 1))
                    src = ps_M[gidx % 2]
                    if kind == "K":
                        sc.copy(KT_stage[kb % 2][:, argv * 512:(argv + 1) * 512],
                                src[:, :]).then_inc(s_kev, 1)
                    elif kind == "V":
                        kt, half = argv
                        dst = V_stage[kb % 2][:, :].rearrange(
                            "p (h c) -> p h c", h=HEADS)
                        sc.copy(
                            dst[:, half * 8:(half + 1) * 8, kt * 65:kt * 65 + 64],
                            src[:, :].rearrange("p (h c) -> p h c", h=8),
                        ).then_inc(s_kev, 1)
                    else:
                        sc.copy(QT_all[:, argv * NQ + kb * 512: argv * NQ + (kb + 1) * 512],
                                src[:, :]).then_inc(s_kev, 1)
                    gidx += 1
            sc.sem_inc(s_ph, 1)

        @block.tensor
        def _(te):
            te.wait_ge(s_x, 16)
            te.wait_ge(s_wc, 48)
            gidx = 0
            for kb in range(KB):
                for t in range(4):
                    T = 4 * kb + t
                    te.wait_ge(s_ln, 5 * kb + t + 2)
                    for c in range(CC):
                        gt = T * 8 + c
                        if gt >= 4:
                            te.wait_ge(s_xnb, gt - 3)
                        te.transpose(ps_T[gt % 4][:, :],
                                     xn_t[t][:, c * 128:(c + 1) * 128],
                                     ident[:, :]).then_inc(s_tp, 1)
                te.wait_ge(s_xnb, 32 * (kb + 1))
                xn = xnT[kb % 2][:, :].rearrange("p (cc c) -> p cc c", cc=CC)
                for _ in range(kb_end_count[kb] - kb_start[kb]):
                    kbx, kind, argv = kvq_sched[gidx]
                    if gidx >= 2:
                        te.wait_ge(s_kev, gidx - 1)
                    dstp = ps_M[gidx % 2]
                    for c in range(CC):
                        if kind == "K":
                            ins = te.matmul(dstp[:, :],
                                            WkT_sb[:, c * D + argv * 128: c * D + (argv + 1) * 128],
                                            xn[:, c, :], start=(c == 0), stop=(c == CC - 1))
                        elif kind == "V":
                            kt, half = argv
                            ins = te.matmul(dstp[:, :],
                                            xn[:, c, kt * 128:(kt + 1) * 128],
                                            WvT_sb[:, c * D + half * 512: c * D + (half + 1) * 512],
                                            start=(c == 0), stop=(c == CC - 1))
                        else:
                            ins = te.matmul(dstp[:, :],
                                            WqT_sb[:, c * D + argv * 128: c * D + (argv + 1) * 128],
                                            xn[:, c, :], start=(c == 0), stop=(c == CC - 1))
                        if c == CC - 1:
                            ins.then_inc(s_kmm, 1)
                    gidx += 1
            te.wait_ge(s_kev, len(kvq_sched))
            te.sem_inc(s_ph, 1)

        PH[0] += 5

        if dump == "A":
            _emit_dump(block, PH[0], s_ph, s_od, [
                (dumps["d_qt"].ap(), QT_all[:, :]),
                (dumps["d_kt"].ap(), KT_dram.ap()),
                (dumps["d_v"].ap(), V_dram.ap()),
            ])
            a.close()
            s_ab.close()
            return nc, dumps

        a.close()
        b = ExitStack()
        bsb = lambda n, s, d: b.enter_context(nc.sbuf_tensor(n, s, d))
        bps = lambda n, s, d: b.enter_context(nc.psum_tensor(n, s, d))
        KT_buf = [bsb(f"KT_b{i}", [128, LK], BF16) for i in range(2)]
        V_buf = [bsb(f"V_b{i}", [128, 2 * NKT * VSLOT], BF16) for i in range(2)]
        P_buf = [bsb(f"P_b{i}", [128, CH * 1024], BF16) for i in range(2)]
        OT_stage = [bsb(f"OT_st{i}", [64, 2 * NQ], BF16) for i in range(2)]
        OA_acc = bsb("OA_acc", [VSLOT, 512], F32)
        OB_acc = bsb("OB_acc", [VSLOT, 512], F32)
        rinv = bsb("rinv", [128, 1024], F32)
        rb_sbA = bsb("rb_sbA", [64, 512], F32)
        rb_sbB = bsb("rb_sbB", [64, 512], F32)
        ps_S = bps("ps_S", [128, GRP * 1024], F32)
        ps_OA = [bps(f"ps_OA{i}", [VSLOT, 512], F32) for i in range(2)]
        ps_OB = [bps(f"ps_OB{i}", [VSLOT, 512], F32) for i in range(2)]

        @block.gpsimd
        def _(gp):
            gp.wait_ge(s_ph, PH[0])

            def load(p):
                if p >= 2:
                    gp.wait_ge(s_rb, QB * (p - 1))
                gp.dma_start(out=KT_buf[p % 2][:, :],
                             in_=KT_dram.ap()[:, p, :]).then_inc(s_kvld, 16)
                gp.dma_start(
                    out=V_buf[p % 2][:, :].rearrange("p (h c) -> p h c", h=2),
                    in_=V_dram.ap()[:, 2 * p:2 * p + 2, :],
                ).then_inc(s_kvld, 16)

            def spill(p):
                gp.wait_ge(s_nrm, 2 * QB * (p + 1))
                gp.dma_start(out=OT_dram.ap()[2 * p, :, :],
                             in_=OT_stage[p % 2][:, 0:NQ]).then_inc(s_ot, 16)
                gp.dma_start(out=OT_dram.ap()[2 * p + 1, :, :],
                             in_=OT_stage[p % 2][:, NQ:2 * NQ]).then_inc(s_ot, 16)

            load(0)
            load(1)
            for p in range(2, PAIRS):
                load(p)
                spill(p - 2)
            spill(PAIRS - 2)
            spill(PAIRS - 1)
            gp.wait_ge(s_ot, 32 * PAIRS)
            gp.sem_inc(s_ph, 1)

        @block.tensor
        def _(te):
            te.wait_ge(s_ph, PH[0])
            for p in range(PAIRS):
                te.wait_ge(s_kvld, 32 * (p + 1))
                kt_b = KT_buf[p % 2]
                v_b = V_buf[p % 2][:, :].rearrange(
                    "p (h kt c) -> p h kt c", h=2, kt=NKT)
                for q in range(QB):
                    pq = p * QB + q
                    for ch in range(NCH):
                        cix = pq * NCH + ch
                        for gi in range(GPC):
                            idx = cix * GPC + gi
                            if idx > 0:
                                te.wait_ge(s_exp, idx)
                            for j in range(GRP):
                                kt = (ch * GPC + gi) * GRP + j
                                te.matmul(
                                    ps_S[:, j * 1024: j * 1024 + 512],
                                    kt_b[0:64, kt * 128:(kt + 1) * 128],
                                    QT_all[0:64, p * NQ + q * 512: p * NQ + (q + 1) * 512],
                                    start=True, stop=True, tile_position=(0, 0))
                                ins = te.matmul(
                                    ps_S[:, j * 1024 + 512: (j + 1) * 1024],
                                    kt_b[64:128, kt * 128:(kt + 1) * 128],
                                    QT_all[64:128, p * NQ + q * 512: p * NQ + (q + 1) * 512],
                                    start=True, stop=True, tile_position=(64, 0))
                                if j == GRP - 1:
                                    ins.then_inc(s_sc, 1)
                        te.wait_ge(s_exp, (cix + 1) * GPC)
                        if cix >= 2:
                            te.wait_ge(s_oadd, 2 * (cix - 1))
                        if ch == 0:
                            te.wait_ge(s_nrm, 2 * pq)
                        pb = P_buf[cix % 2][:, :].rearrange(
                            "p (kt c) -> p kt c", kt=CH)
                        for j in range(CH):
                            kt = ch * CH + j
                            te.matmul(ps_OA[cix % 2][:, :], v_b[:, 0, kt, :],
                                      pb[:, j, 0:512],
                                      start=(j == 0), stop=(j == CH - 1))
                        for j in range(CH):
                            kt = ch * CH + j
                            ins = te.matmul(ps_OB[cix % 2][:, :], v_b[:, 1, kt, :],
                                            pb[:, j, 512:1024],
                                            start=(j == 0), stop=(j == CH - 1))
                            if j == CH - 1:
                                ins.then_inc(s_avc, 1)
                    te.wait_ge(s_rinv, RIA + pq + 1)
                    te.wait_ge(s_oadd, 2 * (pq + 1) * NCH)
                    te.matmul(ps_OA[0][0:64, :], ones64[64:65, :],
                              rinv[64:65, 0:512], start=True, stop=True)
                    ins = te.matmul(ps_OB[0][0:64, :], ones64[64:65, :],
                                    rinv[64:65, 512:1024], start=True, stop=True)
                    ins.then_inc(s_rb, 1)
            te.wait_ge(s_nrm, 2 * NPQ)
            te.sem_inc(s_ph, 1)

        @block.scalar
        def _(sc):
            sc.wait_ge(s_ph, PH[0])
            for idx in range(NGRP):
                cix = idx // GPC
                if idx % GPC == 0 and cix >= 2:
                    sc.wait_ge(s_avc, cix - 1)
                sc.wait_ge(s_sc, idx + 1)
                gi = idx % GPC
                sc.activation(P_buf[cix % 2][:, gi * GRP * 1024:(gi + 1) * GRP * 1024],
                              ps_S[:, :], AF.Exp,
                              bias=biasC[:, :], scale=0.125).then_inc(s_exp, 1)
            sc.sem_inc(s_ph, 1)

        @block.vector
        def _(v):
            v.wait_ge(s_ph, PH[0])
            for pq in range(NPQ):
                p, q = divmod(pq, QB)
                for ch in range(NCH):
                    cix = pq * NCH + ch
                    v.wait_ge(s_avc, cix + 1)
                    if ch == 0:
                        v.tensor_copy(OA_acc[:, :], ps_OA[cix % 2][:, :])
                        v.tensor_copy(OB_acc[:, :],
                                      ps_OB[cix % 2][:, :]).then_inc(s_oadd, 2)
                    else:
                        v.tensor_tensor(OA_acc[:, :], ps_OA[cix % 2][:, :],
                                        OA_acc[:, :], ALU.add)
                        v.tensor_tensor(OB_acc[:, :], ps_OB[cix % 2][:, :],
                                        OB_acc[:, :], ALU.add).then_inc(s_oadd, 2)
                if p >= 2 and q == 0:
                    v.wait_ge(s_ot, 32 * (p - 1))
                v.reciprocal(rinv[64:65, 0:512], OA_acc[64:65, :])
                v.reciprocal(rinv[64:65, 512:1024],
                             OB_acc[64:65, :]).then_inc(s_rinv, 1)
                v.wait_ge(s_rb, pq + 1)
                v.tensor_copy(rb_sbA[:, :], ps_OA[0][0:64, :])
                v.tensor_copy(rb_sbB[:, :], ps_OB[0][0:64, :]).then_inc(s_nrm, 1)
                v.tensor_tensor(OT_stage[p % 2][:, q * 512:(q + 1) * 512],
                                OA_acc[0:64, :], rb_sbA[:, :], ALU.mult)
                v.tensor_tensor(OT_stage[p % 2][:, NQ + q * 512: NQ + (q + 1) * 512],
                                OB_acc[0:64, :], rb_sbB[:, :],
                                ALU.mult).then_inc(s_nrm, 1)
            v.sem_inc(s_ph, 1)

        @block.sync
        def _(sync):
            sync.sem_inc(s_ph, 1)

        PH[0] += 5

        if dump == "B":
            _emit_dump(block, PH[0], s_ph, s_od, [
                (dumps["d_ot"].ap(), OT_dram.ap()),
            ])
            b.close()
            s_ab.close()
            return nc, dumps

        b.close()
        s_ab.close()
        c_x = ExitStack()
        x1 = c_x.enter_context(nc.sbuf_tensor("x1", [128, 16 * D], F32))
        c = ExitStack()
        OT_pack = c.enter_context(nc.sbuf_tensor("OT_pack", [128, PAIRS * NQ], BF16))
        WoT_sb = c.enter_context(nc.sbuf_tensor("WoT_sb", [128, PAIRS * D], BF16))
        bobc = c.enter_context(nc.sbuf_tensor("bobc", [128, D], F32))
        ps_OP = [c.enter_context(nc.psum_tensor(f"ps_OP{i}", [128, 512], F32))
                 for i in range(2)]
        NCLD = 19 * 16

        @block.sync
        def _(sync):
            sync.wait_ge(s_ph, PH[0])
            sync.dma_start(
                out=x1[:, :].rearrange("p (a d) -> p a d", a=16),
                in_=xb.ap().rearrange("(a p) d -> p a d", p=128)[:, 0:16, :],
            ).then_inc(s_cld, 16)
            sync.sem_inc(s_ph, 1)

        @block.gpsimd
        def _(gp):
            gp.wait_ge(s_ph, PH[0])
            gp.dma_start(out=bobc[:, :],
                         in_=bass.AP(bo, 0, [[0, 128], [1, D]])).then_inc(s_cld, 16)
            gp.dma_start(
                out=WoT_sb[:, :].rearrange("p (pr d) -> p pr d", pr=PAIRS),
                in_=WoT.ap().rearrange("(pr p) d -> p pr d", p=128),
            ).then_inc(s_cld, 16)
            for pr in range(PAIRS):
                gp.dma_start(out=OT_pack[0:64, pr * NQ:(pr + 1) * NQ],
                             in_=OT_dram.ap()[2 * pr, :, :]).then_inc(s_cld, 16)
                gp.dma_start(out=OT_pack[64:128, pr * NQ:(pr + 1) * NQ],
                             in_=OT_dram.ap()[2 * pr + 1, :, :]).then_inc(s_cld, 16)
            gp.sem_inc(s_ph, 1)

        @block.vector
        def _(v):
            v.wait_ge(s_ph, PH[0])
            v.wait_ge(s_cld, NCLD)
            for qt in range(16):
                ins = v.tensor_tensor(x1[:, qt * D:(qt + 1) * D],
                                      x1[:, qt * D:(qt + 1) * D], bobc[:, :], ALU.add)
            ins.then_inc(s_cev, 1)
            for blk in range(32):
                qt, dmb = divmod(blk, 2)
                v.wait_ge(s_cmm, blk + 1)
                sl = x1[:, qt * D + dmb * 512: qt * D + (dmb + 1) * 512]
                v.tensor_tensor(sl, ps_OP[blk % 2][:, :], sl,
                                ALU.add).then_inc(s_cev, 1)
            v.sem_inc(s_ph, 1)

        @block.tensor
        def _(te):
            te.wait_ge(s_ph, PH[0])
            te.wait_ge(s_cld, NCLD)
            for blk in range(32):
                qt, dmb = divmod(blk, 2)
                if blk >= 2:
                    te.wait_ge(s_cev, blk)
                for pr in range(PAIRS):
                    ins = te.matmul(ps_OP[blk % 2][:, :],
                                    OT_pack[:, pr * NQ + qt * 128: pr * NQ + (qt + 1) * 128],
                                    WoT_sb[:, pr * D + dmb * 512: pr * D + (dmb + 1) * 512],
                                    start=(pr == 0), stop=(pr == PAIRS - 1))
                    if pr == PAIRS - 1:
                        ins.then_inc(s_cmm, 1)
            te.wait_ge(s_cev, 33)
            te.sem_inc(s_ph, 1)

        @block.scalar
        def _(sc):
            sc.wait_ge(s_ph, PH[0])
            sc.sem_inc(s_ph, 1)

        PH[0] += 5

        if dump == "C":
            _emit_dump(block, PH[0], s_ph, s_od, [
                (dumps["d_x1"].ap(), x1[:, :]),
            ])
            c.close()
            c_x.close()
            return nc, dumps

        c.close()
        dk = ExitStack()
        xn2T = dk.enter_context(nc.sbuf_tensor("xn2T", [128, CC * NQ], BF16))
        probT = dk.enter_context(nc.sbuf_tensor("probT", [8, NQ], BF16))
        sel_bf = dk.enter_context(nc.sbuf_tensor("sel_bf", [8, 1024], BF16))
        dt_ = ExitStack()
        dsb = lambda n, s, ddt: dt_.enter_context(nc.sbuf_tensor(n, s, ddt))
        dps = lambda n, s, ddt: dt_.enter_context(nc.psum_tensor(n, s, ddt))
        xn2_t = [dsb(f"xn2_t{i}", [128, D], BF16) for i in range(2)]
        d2_s1 = dsb("d2_s1", [128, 4], F32)
        d2_s2 = dsb("d2_s2", [128, 4], F32)
        d2_m = dsb("d2_m", [128, 4], F32)
        d2_q = dsb("d2_q", [128, 4], F32)
        d2_lt = dsb("d2_lt", [128, 4], F32)
        d2_t = dsb("d2_t", [128, 4], F32)
        d2_v = dsb("d2_v", [128, 4], F32)
        d2_r = dsb("d2_r", [128, 4], F32)
        Wr_sb = dsb("Wr_sb", [128, CC * 8], BF16)
        logits = dsb("logits", [128, 128], F32)
        ismax = dsb("ismax", [128, 16 * 8], F32)
        maskd = dsb("maskd", [128, 16 * 8], F32)
        nmax = dsb("nmax", [128, 16], F32)
        m1c = dsb("m1c", [128, 16], F32)
        v2c = dsb("v2c", [128, 16], F32)
        dcol = dsb("dcol", [128, 16], F32)
        ecol = dsb("ecol", [128, 16], F32)
        e2col = dsb("e2col", [128, 16], F32)
        e3col = dsb("e3col", [128, 16], F32)
        gcol = dsb("gcol", [128, 16], F32)
        g2col = dsb("g2col", [128, 16], F32)
        g3col = dsb("g3col", [128, 16], F32)
        prob_all = dsb("prob_all", [128, 16 * 8], BF16)
        sel_i = dsb("sel_i", [8, 1024], I32)
        ps_T2 = [dps(f"ps_T2{i}", [128, 128], BF16) for i in range(4)]
        ps_LG = [dps(f"ps_LG{i}", [128, 8], F32) for i in range(2)]
        ps_PT = dps("ps_PT", [8, 128], BF16)

        x1r = x1[:, :].rearrange("p (qt d) -> p qt d", qt=16)

        @block.gpsimd
        def _(gp):
            gp.wait_ge(s_ph, PH[0])
            gp.dma_start(
                out=Wr_sb[:, :].rearrange("p (cc e) -> p cc e", cc=CC),
                in_=Wr.ap().rearrange("(cc p) e -> p cc e", p=128),
            ).then_inc(s_wc, 16)
            gp.iota(sel_i[:, :], [[1, 8], [0, 128]], channel_multiplier=-1,
                    allow_small_or_imprecise_dtypes=True).then_inc(s_wc, 1)
            gp.sem_inc(s_ph, 1)

        @block.sync
        def _(sync):
            sync.wait_ge(s_ph, PH[0])
            sync.sem_inc(s_ph, 1)

        @block.vector
        def _(v):
            v.wait_ge(s_ph, PH[0])
            v.wait_ge(s_wc, 65)
            v.tensor_scalar(sel_bf[:, :], sel_i[:, :], 0.0, None, ALU.is_equal)
            for g4 in range(4):
                for t in range(4):
                    xt = x1r[:, g4 * 4 + t, :]
                    sq = scrq[:, t * D:(t + 1) * D]
                    v.reduce_sum(d2_s1[:, t:t + 1], xt, axis=AX.X)
                    v.tensor_tensor(sq, xt, xt, ALU.mult)
                    v.reduce_sum(d2_s2[:, t:t + 1], sq, axis=AX.X)
                v.tensor_scalar(d2_m[:, :], d2_s1[:, :], 1.0 / D, None, ALU.mult)
                v.tensor_scalar(d2_q[:, :], d2_s2[:, :], 1.0 / D, None, ALU.mult)
                v.tensor_copy(junk0[:, 0:4], d2_m[:, :])
                v.tensor_copy(junk0[:, 4:8], d2_m[:, :])
                v.tensor_tensor(d2_t[:, :], d2_m[:, :], d2_m[:, :], ALU.mult)
                v.tensor_copy(junk0[:, 0:4], d2_t[:, :])
                v.tensor_copy(junk0[:, 4:8], d2_t[:, :])
                v.tensor_tensor(d2_v[:, :], d2_q[:, :], d2_t[:, :],
                                ALU.subtract).then_inc(s_ln, 1)
                v.wait_ge(s_rinv, RIB + g4 + 1)
                for t in range(4):
                    ti = g4 * 4 + t
                    if ti >= 2:
                        v.wait_ge(s_tp, TPA + 8 * (ti - 1))
                    v.tensor_scalar(xn2_t[ti % 2][:, :], x1r[:, ti, :],
                                    d2_m[:, t:t + 1], d2_r[:, t:t + 1],
                                    ALU.subtract, ALU.mult).then_inc(s_ln, 1)
                    dst = xn2T[:, :].rearrange("p (cc c) -> p cc c", cc=CC)
                    for cc in range(CC):
                        gt = ti * 8 + cc
                        v.wait_ge(s_tp, TPA + gt + 1)
                        v.tensor_copy(dst[:, cc, ti * 128:(ti + 1) * 128],
                                      ps_T2[gt % 4][:, :]).then_inc(s_xnb, 1)
            for qt in range(16):
                v.wait_ge(s_lg, qt + 1)
                v.tensor_copy(logits[:, qt * 8:(qt + 1) * 8],
                              ps_LG[qt % 2][:, :]).then_inc(s_lge, 1)
            for qt in range(16):
                v.tensor_reduce(m1c[:, qt:qt + 1],
                                logits[:, qt * 8:(qt + 1) * 8], AX.X, ALU.max)
            for qt in range(16):
                v.tensor_scalar(ismax[:, qt * 8:(qt + 1) * 8],
                                logits[:, qt * 8:(qt + 1) * 8],
                                m1c[:, qt:qt + 1], None, ALU.is_equal)
            for qt in range(16):
                v.tensor_reduce(nmax[:, qt:qt + 1],
                                ismax[:, qt * 8:(qt + 1) * 8], AX.X, ALU.add)
            for qt in range(16):
                v.tensor_scalar(maskd[:, qt * 8:(qt + 1) * 8],
                                ismax[:, qt * 8:(qt + 1) * 8], -1e30, None,
                                ALU.mult)
            for qt in range(16):
                msl = maskd[:, qt * 8:(qt + 1) * 8]
                v.tensor_tensor(msl, msl, logits[:, qt * 8:(qt + 1) * 8], ALU.add)
            for qt in range(16):
                v.tensor_reduce(v2c[:, qt:qt + 1],
                                maskd[:, qt * 8:(qt + 1) * 8], AX.X, ALU.max)
            v.tensor_tensor(dcol[:, :], v2c[:, :], m1c[:, :],
                            ALU.subtract).then_inc(s_rt, 1)
            v.wait_ge(s_rt, 2)
            v.tensor_scalar(e2col[:, :], ecol[:, :], 1.0, None, ALU.add)
            v.tensor_scalar(g2col[:, :], nmax[:, :], 1.5, None, ALU.is_le)
            v.tensor_copy(junk0[:, 0:8], e2col[:, 0:8])
            v.reciprocal(e3col[:, :], e2col[:, :])
            v.tensor_copy(junk0[:, 0:8], e3col[:, 0:8])
            v.tensor_copy(junk0[:, 0:8], e3col[:, 8:16])
            v.tensor_scalar(gcol[:, :], e3col[:, :], 1.0, None, ALU.is_ge)
            v.tensor_copy(junk0[:, 0:8], gcol[:, 0:8])
            v.tensor_copy(junk0[:, 0:8], gcol[:, 8:16])
            v.tensor_tensor(g3col[:, :], gcol[:, :], g2col[:, :], ALU.mult)
            v.tensor_copy(junk0[:, 0:8], g3col[:, 0:8])
            v.tensor_copy(junk0[:, 0:8], g3col[:, 8:16])
            for qt in range(16):
                ins = v.tensor_scalar(prob_all[:, qt * 8:(qt + 1) * 8],
                                      ismax[:, qt * 8:(qt + 1) * 8],
                                      g3col[:, qt:qt + 1], None, ALU.mult)
            ins.then_inc(s_rt, 1)
            for qt in range(16):
                v.wait_ge(s_lg, 16 + qt + 1)
                v.tensor_copy(probT[:, qt * 128:(qt + 1) * 128],
                              ps_PT[:, :]).then_inc(s_xnb, 1)
            v.sem_inc(s_ph, 1)

        @block.scalar
        def _(sc):
            sc.wait_ge(s_ph, PH[0])
            for g4 in range(4):
                sc.wait_ge(s_ln, LNA + 5 * g4 + 1)
                sc.activation(d2_lt[:, :], d2_v[:, :], AF.Ln, bias=epsc[:, :], scale=1.0)
                sc.activation(d2_r[:, :], d2_lt[:, :], AF.Exp, bias=0.0,
                              scale=-0.5).then_inc(s_rinv, 1)
            sc.wait_ge(s_rt, 1)
            sc.activation(ecol[:, :], dcol[:, :], AF.Exp, bias=0.0,
                          scale=1.0).then_inc(s_rt, 1)
            sc.sem_inc(s_ph, 1)

        @block.tensor
        def _(te):
            te.wait_ge(s_ph, PH[0])
            for ti in range(16):
                te.wait_ge(s_ln, LNA + 5 * (ti // 4) + (ti % 4) + 2)
                for cc in range(CC):
                    gt = ti * 8 + cc
                    if gt >= 4:
                        te.wait_ge(s_xnb, XNA + gt - 3)
                    te.transpose(ps_T2[gt % 4][:, :],
                                 xn2_t[ti % 2][:, cc * 128:(cc + 1) * 128],
                                 ident[:, :]).then_inc(s_tp, 1)
            te.wait_ge(s_xnb, XNA + 128)
            te.wait_ge(s_wc, 65)
            xn2 = xn2T[:, :].rearrange("p (cc c) -> p cc c", cc=CC)
            for qt in range(16):
                if qt >= 2:
                    te.wait_ge(s_lge, qt - 1)
                for cc in range(CC):
                    ins = te.matmul(ps_LG[qt % 2][:, :],
                                    xn2[:, cc, qt * 128:(qt + 1) * 128],
                                    Wr_sb[:, cc * 8:(cc + 1) * 8],
                                    start=(cc == 0), stop=(cc == CC - 1))
                    if cc == CC - 1:
                        ins.then_inc(s_lg, 1)
            te.wait_ge(s_rt, 3)
            for qt in range(16):
                if qt >= 1:
                    te.wait_ge(s_xnb, XNA + 128 + qt)
                te.transpose(ps_PT[:, :], prob_all[:, qt * 8:(qt + 1) * 8],
                             ident[:, :]).then_inc(s_lg, 1)
            te.wait_ge(s_xnb, XNA + 128 + 16)
            te.sem_inc(s_ph, 1)

        PH[0] += 5

        if dump == "D":
            _emit_dump(block, PH[0], s_ph, s_od, [
                (dumps["d_xn2"].ap(), xn2T[:, :]),
                (dumps["d_prob"].ap(), prob_all[:, :]),
                (dumps["d_probT"].ap(), probT[:, :]),
            ])
            dt_.close()
            dk.close()
            c_x.close()
            return nc, dumps

        dt_.close()
        e = ExitStack()
        esb = lambda n, s, ddt: e.enter_context(nc.sbuf_tensor(n, s, ddt))
        W1_sb = [esb(f"W1_sb{i}", [128, CC * 512], BF16) for i in range(2)]
        W2_sb = [esb(f"W2_sb{i}", [128, FC * D], BF16) for i in range(2)]
        H_sb = [esb(f"H_sb{i}", [128, FC * 512], BF16) for i in range(2)]
        acc_sb = esb("acc_sb", [128, 4 * D], F32)
        pb_sb = esb("pb_sb", [128, 8 * 512], BF16)
        r_f = [esb(f"r_f{i}", [128, 512], F32) for i in range(4)]
        r2_f = [esb(f"r2_f{i}", [128, 512], F32) for i in range(2)]
        out_stage = [esb(f"out_st{i}", [128, 512], F32) for i in range(2)]
        ps_H = [e.enter_context(nc.psum_tensor(f"ps_H{i}", [128, 512], F32))
                for i in range(2)]
        ps_O = [e.enter_context(nc.psum_tensor(f"ps_O{i}", [128, 512], F32))
                for i in range(2)]
        ps_PB = e.enter_context(nc.psum_tensor("ps_PB", [128, 512], F32))

        out_r = out_d.ap().rearrange("(tt p) d -> p tt d", p=128)
        W1src = [W1.ap()[ei] for ei in range(8)] + [Ws1.ap()]
        W2src = [W2.ap()[ei] for ei in range(8)] + [Ws2.ap()]

        @block.gpsimd
        def _(gp):
            gp.wait_ge(s_ph, PH[0])
            for tq in range(TQ):
                for ei in range(NE):
                    gidx = tq * NE + ei
                    if gidx >= 2:
                        gp.wait_ge(s_hmm, (gidx - 1) * FC)
                    gp.dma_start(
                        out=W1_sb[gidx % 2][:, :].rearrange(
                            "p (cc f) -> p cc f", cc=CC),
                        in_=W1src[ei].rearrange("(cc p) f -> p cc f", p=128),
                    ).then_inc(s_wld, 16)
                    if gidx >= 2:
                        gp.wait_ge(s_eomm, (gidx - 1) * 8)
                    gp.dma_start(
                        out=W2_sb[gidx % 2][:, :].rearrange(
                            "p (fc d) -> p fc d", fc=FC),
                        in_=W2src[ei].rearrange("(fc p) d -> p fc d", p=128),
                    ).then_inc(s_wld, 16)
                for blk in range(8):
                    tb, dmb = divmod(blk, 2)
                    gp.wait_ge(s_oev, tq * 8 + blk + 1)
                    gp.dma_start(
                        out=out_r[:, tq * 4 + tb, dmb * 512:(dmb + 1) * 512],
                        in_=out_stage[blk % 2][:, :]).then_inc(s_od, 16)
            gp.wait_ge(s_od, 16 * 32)

        @block.tensor
        def _(te):
            te.wait_ge(s_ph, PH[0])
            xn2 = xn2T[:, :].rearrange("p (cc c) -> p cc c", cc=CC)
            for tq in range(TQ):
                for ei in range(8):
                    te.wait_ge(s_pbe, 8 * tq + ei)
                    te.matmul(ps_PB[:, :], sel_bf[:, ei * 128:(ei + 1) * 128],
                              probT[:, tq * 512:(tq + 1) * 512],
                              start=True, stop=True).then_inc(s_pb, 1)
                for ei in range(NE):
                    gidx = tq * NE + ei
                    te.wait_ge(s_wld, 16 * (2 * gidx + 1))
                    for fc in range(FC):
                        hg = gidx * FC + fc
                        if hg >= 2:
                            te.wait_ge(s_hact, hg - 1)
                        for cc in range(CC):
                            ins = te.matmul(ps_H[hg % 2][:, :],
                                            W1_sb[gidx % 2][:, cc * 512 + fc * 128:
                                                            cc * 512 + (fc + 1) * 128],
                                            xn2[:, cc, tq * 512:(tq + 1) * 512],
                                            start=(cc == 0), stop=(cc == CC - 1))
                            if cc == CC - 1:
                                ins.then_inc(s_hmm, 1)
                    te.wait_ge(s_wld, 16 * (2 * gidx + 2))
                    te.wait_ge(s_he, gidx * FC + FC)
                    for blk in range(8):
                        og = gidx * 8 + blk
                        tb, dmb = divmod(blk, 2)
                        if og >= 2:
                            te.wait_ge(s_eoadd, og - 1)
                        for fc in range(FC):
                            ins = te.matmul(ps_O[og % 2][:, :],
                                            H_sb[gidx % 2][:, fc * 512 + tb * 128:
                                                           fc * 512 + (tb + 1) * 128],
                                            W2_sb[gidx % 2][:, fc * D + dmb * 512:
                                                            fc * D + (dmb + 1) * 512],
                                            start=(fc == 0), stop=(fc == FC - 1))
                            if fc == FC - 1:
                                ins.then_inc(s_eomm, 1)

        @block.scalar
        def _(sc):
            sc.wait_ge(s_ph, PH[0])
            for hg in range(TQ * NE * FC):
                sc.wait_ge(s_hmm, hg + 1)
                if hg >= 2:
                    sc.wait_ge(s_he, hg - 1)
                sc.activation(r_f[hg % 4][:, :], ps_H[hg % 2][:, :],
                              AF.Relu, bias=0.0, scale=1.0)
                sc.activation(r2_f[hg % 2][:, :], r_f[hg % 4][:, :],
                              AF.Square, bias=0.0, scale=1.0).then_inc(s_hact, 1)

        @block.vector
        def _(v):
            v.wait_ge(s_ph, PH[0])
            for tq in range(TQ):
                for ei in range(8):
                    v.wait_ge(s_pb, 8 * tq + ei + 1)
                    v.tensor_copy(pb_sb[:, ei * 512:(ei + 1) * 512],
                                  ps_PB[:, :]).then_inc(s_pbe, 1)
                for ei in range(NE):
                    gidx = tq * NE + ei
                    pbs = onesbc[:, :] if ei == 8 else pb_sb[:, ei * 512:(ei + 1) * 512]
                    for fc in range(FC):
                        hg = gidx * FC + fc
                        v.wait_ge(s_hact, hg + 1)
                        if gidx >= 2 and fc == 0:
                            v.wait_ge(s_eomm, (gidx - 2) * 8 + 8)
                        v.tensor_tensor(H_sb[gidx % 2][:, fc * 512:(fc + 1) * 512],
                                        r2_f[hg % 2][:, :], pbs,
                                        ALU.mult).then_inc(s_he, 1)
                    for blk in range(8):
                        og = gidx * 8 + blk
                        tb, dmb = divmod(blk, 2)
                        v.wait_ge(s_eomm, og + 1)
                        sl = acc_sb[:, tb * D + dmb * 512: tb * D + (dmb + 1) * 512]
                        if ei == 0:
                            v.tensor_copy(sl, ps_O[og % 2][:, :]).then_inc(s_eoadd, 1)
                        else:
                            v.tensor_tensor(sl, ps_O[og % 2][:, :], sl,
                                            ALU.add).then_inc(s_eoadd, 1)
                for blk in range(8):
                    tb, dmb = divmod(blk, 2)
                    if tq * 8 + blk >= 2:
                        v.wait_ge(s_od, 16 * (tq * 8 + blk - 1))
                    tt = tq * 4 + tb
                    v.tensor_tensor(out_stage[blk % 2][:, :],
                                    acc_sb[:, tb * D + dmb * 512: tb * D + (dmb + 1) * 512],
                                    x1[:, tt * D + dmb * 512: tt * D + (dmb + 1) * 512],
                                    ALU.add).then_inc(s_oev, 1)

        e.close()
        dk.close()
        c_x.close()
    return nc, dumps


def _emit_dump(block, ph_val, s_ph, s_od, pairs):
    @block.gpsimd
    def _(gp):
        gp.wait_ge(s_ph, ph_val)
        n = 0
        for dst, src in pairs:
            gp.dma_start(out=dst, in_=src).then_inc(s_od, 16)
            n += 16
        gp.wait_ge(s_od, n)


_PROG_CACHE = {}


def _get_program(dump=None):
    if dump not in _PROG_CACHE:
        _PROG_CACHE[dump] = build_program(dump)
    return _PROG_CACHE[dump]


def make_in_maps(x, WqT, WkT, WvT, WoT, bo, Ws1, Ws2, W1, W2, Wr):
    in_maps = []
    for core in range(8):
        bidx, qidx = divmod(core, 4)
        xb_c = np.ascontiguousarray(np.roll(x[bidx], -qidx * NQ, axis=0))
        in_maps.append({
            "xb": xb_c, "WqT": WqT, "WkT": WkT, "WvT": WvT, "WoT": WoT,
            "bo": bo, "Ws1": Ws1, "Ws2": Ws2, "W1": W1, "W2": W2, "Wr": Wr,
        })
    return in_maps


def prep_inputs(L0_attn_x, L0_attn_W_q, L0_attn_W_k, L0_attn_W_v, L0_attn_W_o,
                L0_attn_b_o, L0_ffn_Ws1, L0_ffn_Ws2, L0_ffn_W1, L0_ffn_W2,
                L0_ffn_Wr):
    f = lambda v: np.ascontiguousarray(np.asarray(v, dtype=np.float32))
    ft = lambda v: np.ascontiguousarray(np.asarray(v, dtype=np.float32).T)
    return (f(L0_attn_x), ft(L0_attn_W_q), ft(L0_attn_W_k), ft(L0_attn_W_v),
            ft(L0_attn_W_o), f(L0_attn_b_o), f(L0_ffn_Ws1), f(L0_ffn_Ws2),
            f(L0_ffn_W1), f(L0_ffn_W2), f(L0_ffn_Wr))


def kernel(**inputs):
    arrs = prep_inputs(**inputs)
    nc, _ = _get_program()
    in_maps = make_in_maps(*arrs)
    res = run_bass_kernel_spmd(nc, in_maps, core_ids=list(range(8)))
    out = np.empty((2, LK, D), dtype=np.float32)
    for core in range(8):
        bidx, qidx = divmod(core, 4)
        out[bidx, qidx * NQ:(qidx + 1) * NQ] = np.asarray(
            res.results[core]["out"], dtype=np.float32)
    return out


# version 02
# speedup vs baseline: 1.6797x; 1.6797x over previous
import numpy as np
import ml_dtypes
from contextlib import ExitStack

import concourse.bass as bass
import concourse.mybir as mybir
from concourse.bass_utils import run_bass_kernel_spmd

F32 = mybir.dt.float32
BF16 = mybir.dt.bfloat16
I32 = mybir.dt.int32
AF = mybir.ActivationFunctionType
ALU = mybir.AluOpType
AX = mybir.AxisListType

D = 1024
LK = 8192
NQ = 2048
CC = 8
KB = 16
QKB = 4
PAIRS = 8
HEADS = 16
QB = 4
NKT = 64
GRP = 2
CH = 8
GPC = CH // GRP
NCH = NKT // CH
NPQ = PAIRS * QB
NGRP = NPQ * NKT // GRP
CSH = 10.0
EPS = 1e-5
FC = 4
NE = 9
TQ = 4
VSLOT = 65

LNA = 5 * KB
TPA = 8 * 4 * KB
XNA = 8 * 4 * KB
RIA = KB
RIB = RIA + NPQ


def build_program(dump=None):
    nc = bass.Bass("TRN2", target_bir_lowering=False,
                   detect_race_conditions=False)

    xb = nc.dram_tensor("xb", [LK, D], F32, kind="ExternalInput")
    WqT = nc.dram_tensor("WqT", [D, D], F32, kind="ExternalInput")
    WkT = nc.dram_tensor("WkT", [D, D], F32, kind="ExternalInput")
    WvT = nc.dram_tensor("WvT", [D, D], F32, kind="ExternalInput")
    WoT = nc.dram_tensor("WoT", [D, D], F32, kind="ExternalInput")
    bo = nc.dram_tensor("bo", [D], F32, kind="ExternalInput")
    Ws1 = nc.dram_tensor("Ws1", [D, 512], F32, kind="ExternalInput")
    Ws2 = nc.dram_tensor("Ws2", [512, D], F32, kind="ExternalInput")
    W1 = nc.dram_tensor("W1", [8, D, 512], F32, kind="ExternalInput")
    W2 = nc.dram_tensor("W2", [8, 512, D], F32, kind="ExternalInput")
    Wr = nc.dram_tensor("Wr", [D, 8], F32, kind="ExternalInput")
    out_d = nc.dram_tensor("out", [NQ, D], F32, kind="ExternalOutput")

    KT_dram = nc.dram_tensor("KT_scr", [128, PAIRS, LK], BF16)
    V_dram = nc.dram_tensor("V_scr", [128, HEADS, NKT * VSLOT], BF16)
    OT_dram = nc.dram_tensor("OT_scr", [HEADS, 64, NQ], BF16)

    ident_dram = nc.inline_tensor(np.eye(128, dtype=ml_dtypes.bfloat16), "ident_c")

    dumps = {}
    if dump == "A":
        dumps["d_qt"] = nc.dram_tensor("d_qt", [128, PAIRS * NQ], BF16, kind="ExternalOutput")
        dumps["d_kt"] = nc.dram_tensor("d_kt", [128, PAIRS, LK], BF16, kind="ExternalOutput")
        dumps["d_v"] = nc.dram_tensor("d_v", [128, HEADS, NKT * VSLOT], BF16, kind="ExternalOutput")
    if dump == "B":
        dumps["d_ot"] = nc.dram_tensor("d_ot", [HEADS, 64, NQ], BF16, kind="ExternalOutput")
    if dump == "C":
        dumps["d_x1"] = nc.dram_tensor("d_x1", [128, 16 * D], F32, kind="ExternalOutput")
    if dump == "D":
        dumps["d_xn2"] = nc.dram_tensor("d_xn2", [128, CC * NQ], BF16, kind="ExternalOutput")
        dumps["d_prob"] = nc.dram_tensor("d_prob", [128, 16 * 8], BF16, kind="ExternalOutput")
        dumps["d_probT"] = nc.dram_tensor("d_probT", [8, NQ], BF16, kind="ExternalOutput")

    g = ExitStack()
    with g:
        sem = lambda n: g.enter_context(nc.semaphore(n))

        ident = g.enter_context(nc.sbuf_tensor("ident", [128, 128], BF16))
        biasC = g.enter_context(nc.sbuf_tensor("biasC", [128, 1], F32))
        epsc = g.enter_context(nc.sbuf_tensor("epsc", [128, 1], F32))
        ones64 = g.enter_context(nc.sbuf_tensor("ones64", [128, 64], F32))
        onesbc = g.enter_context(nc.sbuf_tensor("onesbc", [128, 512], BF16))
        scrq = g.enter_context(nc.sbuf_tensor("scrq", [128, 4 * D], F32))
        junk0 = g.enter_context(nc.sbuf_tensor("junk0", [128, 8], F32))

        s_ph = sem("s_ph")
        s_x = sem("s_x")
        s_wc = sem("s_wc")
        s_ln = sem("s_ln")
        s_tp = sem("s_tp")
        s_xnb = sem("s_xnb")
        s_kmm = sem("s_kmm")
        s_kev = sem("s_kev")
        s_stg = sem("s_stg")
        s_kvld = sem("s_kvld")
        s_sc = sem("s_sc")
        s_exp = sem("s_exp")
        s_av = sem("s_av")
        s_rinv = sem("s_rinv")
        s_rb = sem("s_rb")
        s_nrm = sem("s_nrm")
        s_ot = sem("s_ot")
        s_cld = sem("s_cld")
        s_cmm = sem("s_cmm")
        s_cev = sem("s_cev")
        s_lg = sem("s_lg")
        s_lge = sem("s_lge")
        s_rt = sem("s_rt")
        s_wld = sem("s_wld")
        s_pb = sem("s_pb")
        s_pbe = sem("s_pbe")
        s_hmm = sem("s_hmm")
        s_hact = sem("s_hact")
        s_he = sem("s_he")
        s_eomm = sem("s_eomm")
        s_eoadd = sem("s_eoadd")
        s_oev = sem("s_oev")
        s_od = sem("s_od")

        PH = [0]
        block = g.enter_context(nc.Block())

        s_ab = ExitStack()
        QT_all = s_ab.enter_context(nc.sbuf_tensor("QT_all", [128, PAIRS * NQ], BF16))

        a = ExitStack()
        asb = lambda n, s, d: a.enter_context(nc.sbuf_tensor(n, s, d))
        aps = lambda n, s, d: a.enter_context(nc.psum_tensor(n, s, d))
        WqT_sb = asb("WqT_sb", [128, CC * D], BF16)
        WkT_sb = asb("WkT_sb", [128, CC * D], BF16)
        WvT_sb = asb("WvT_sb", [128, CC * D], BF16)
        xblk = [asb(f"xblk{i}", [128, 4 * D], F32) for i in range(2)]
        xn_t = [asb(f"xn_t{i}", [128, D], BF16) for i in range(4)]
        st_s1 = asb("st_s1", [128, 4], F32)
        st_s2 = asb("st_s2", [128, 4], F32)
        st_m = asb("st_m", [128, 4], F32)
        st_q = asb("st_q", [128, 4], F32)
        st_lt = asb("st_lt", [128, 4], F32)
        st_t = asb("st_t", [128, 4], F32)
        st_v = asb("st_v", [128, 4], F32)
        st_r = asb("st_r", [128, 4], F32)
        xnT = [asb(f"xnT{i}", [128, CC * 512], BF16) for i in range(2)]
        KT_stage = [asb(f"KT_st{i}", [128, PAIRS * 512], BF16) for i in range(2)]
        V_stage = [asb(f"V_st{i}", [128, HEADS * 260], BF16) for i in range(2)]
        ps_T = [aps(f"ps_T{i}", [128, 128], BF16) for i in range(4)]
        ps_M = [aps(f"ps_M{i}", [128, 512], F32) for i in range(2)]

        kvq_sched = []
        for kb in range(KB):
            for pr in range(PAIRS):
                kvq_sched.append((kb, "K", pr))
            for kt in range(4):
                for half in range(2):
                    kvq_sched.append((kb, "V", (kt, half)))
            if kb < QKB:
                for pr in range(PAIRS):
                    kvq_sched.append((kb, "Q", pr))
        kb_end_count = {}
        for i, (kb, kind, argv) in enumerate(kvq_sched):
            kb_end_count[kb] = i + 1
        kb_start = {kb: (kb_end_count[kb - 1] if kb else 0) for kb in range(KB)}

        @block.sync
        def _(sync):
            sync.dma_start(out=ident[:, :], in_=ident_dram.ap()).then_inc(s_x, 16)
            for kb in range(KB):
                if kb >= 2:
                    sync.wait_ge(s_ln, 5 * (kb - 1))
                sync.dma_start(
                    out=xblk[kb % 2][:, :].rearrange("p (s d) -> p s d", s=4),
                    in_=xb.ap().rearrange("(kb s p) d -> kb p s d", p=128, s=4)[kb],
                ).then_inc(s_x, 16)
            sync.sem_inc(s_ph, 1)

        @block.gpsimd
        def _(gp):
            for W, Wsb in ((WqT, WqT_sb), (WkT, WkT_sb), (WvT, WvT_sb)):
                gp.dma_start(
                    out=Wsb[:, :].rearrange("p (cc d) -> p cc d", cc=CC),
                    in_=W.ap().rearrange("(cc p) d -> p cc d", p=128),
                ).then_inc(s_wc, 16)
            for i in range(2):
                vst = V_stage[i][:, :].rearrange("p (h kt c) -> p h kt c", h=HEADS, kt=4)
                gp.memset(vst[:, :, :, 64:65], 1.0)
            for kb in range(KB):
                gp.wait_ge(s_kev, kb_end_count[kb])
                gp.dma_start(
                    out=KT_dram.ap()[:, :, kb * 512:(kb + 1) * 512],
                    in_=KT_stage[kb % 2][:, :].rearrange("p (pr c) -> p pr c", pr=PAIRS),
                ).then_inc(s_stg, 16)
                gp.dma_start(
                    out=V_dram.ap()[:, :, kb * 260:(kb + 1) * 260],
                    in_=V_stage[kb % 2][:, :].rearrange("p (h c) -> p h c", h=HEADS),
                ).then_inc(s_stg, 16)
            gp.wait_ge(s_stg, 32 * KB)
            gp.sem_inc(s_ph, 1)

        @block.vector
        def _(v):
            v.memset(biasC[:, :], -CSH)
            v.memset(epsc[:, :], EPS)
            v.memset(ones64[64:65, :], 1.0)
            v.memset(onesbc[:, :], 1.0)
            v.memset(scrq[:, :], 1.0)
            v.memset(junk0[:, :], 1.0)
            for _w in range(3):
                v.reduce_sum(junk0[:, 0:1], scrq[:, 0:D], axis=AX.X)
                v.tensor_tensor(scrq[:, 0:D], scrq[:, 0:D], scrq[:, 0:D], ALU.mult)
                v.tensor_tensor(junk0[:, 4:8], junk0[:, 0:4], junk0[:, 0:4],
                                ALU.subtract)
                v.tensor_scalar(junk0[:, 0:4], junk0[:, 4:8], 0.5, None, ALU.mult)
                v.tensor_copy(junk0[:, 4:8], junk0[:, 0:4])
                v.reciprocal(junk0[:, 0:4], scrq[:, 0:4])
            for kb in range(KB):
                v.wait_ge(s_x, 16 * (kb + 2))
                for t in range(4):
                    xt = xblk[kb % 2][:, t * D:(t + 1) * D]
                    sq = scrq[:, t * D:(t + 1) * D]
                    v.reduce_sum(st_s1[:, t:t + 1], xt, axis=AX.X)
                    v.tensor_tensor(sq, xt, xt, ALU.mult)
                    v.reduce_sum(st_s2[:, t:t + 1], sq, axis=AX.X)
                v.tensor_scalar(st_m[:, :], st_s1[:, :], 1.0 / D, None, ALU.mult)
                v.tensor_scalar(st_q[:, :], st_s2[:, :], 1.0 / D, None, ALU.mult)
                v.tensor_copy(junk0[:, 0:4], st_m[:, :])
                v.tensor_copy(junk0[:, 4:8], st_m[:, :])
                v.tensor_tensor(st_t[:, :], st_m[:, :], st_m[:, :], ALU.mult)
                v.tensor_copy(junk0[:, 0:4], st_t[:, :])
                v.tensor_copy(junk0[:, 4:8], st_t[:, :])
                v.tensor_tensor(st_v[:, :], st_q[:, :], st_t[:, :],
                                ALU.subtract).then_inc(s_ln, 1)
                v.wait_ge(s_rinv, kb + 1)
                for t in range(4):
                    T = 4 * kb + t
                    if T >= 4:
                        v.wait_ge(s_tp, 8 * (T - 3))
                    xt = xblk[kb % 2][:, t * D:(t + 1) * D]
                    v.tensor_scalar(xn_t[t][:, :], xt, st_m[:, t:t + 1],
                                    st_r[:, t:t + 1], ALU.subtract,
                                    ALU.mult).then_inc(s_ln, 1)
                for t in range(4):
                    T = 4 * kb + t
                    if kb >= 2 and t == 0:
                        v.wait_ge(s_kmm, kb_end_count[kb - 2])
                    dst = xnT[kb % 2][:, :].rearrange("p (cc c) -> p cc c", cc=CC)
                    for c in range(CC):
                        gt = T * 8 + c
                        v.wait_ge(s_tp, gt + 1)
                        v.tensor_copy(dst[:, c, t * 128:(t + 1) * 128],
                                      ps_T[gt % 4][:, :]).then_inc(s_xnb, 1)
            v.sem_inc(s_ph, 1)

        @block.scalar
        def _(sc):
            gidx = 0
            for kb in range(KB):
                sc.wait_ge(s_ln, 5 * kb + 1)
                sc.activation(st_lt[:, :], st_v[:, :], AF.Ln, bias=epsc[:, :], scale=1.0)
                sc.activation(st_r[:, :], st_lt[:, :], AF.Exp, bias=0.0,
                              scale=-0.5).then_inc(s_rinv, 1)
                for _ in range(kb_end_count[kb] - kb_start[kb]):
                    kbx, kind, argv = kvq_sched[gidx]
                    sc.wait_ge(s_kmm, gidx + 1)
                    if kb >= 2 and gidx == kb_start[kb]:
                        sc.wait_ge(s_stg, 32 * (kb - 1))
                    src = ps_M[gidx % 2]
                    if kind == "K":
                        sc.copy(KT_stage[kb % 2][:, argv * 512:(argv + 1) * 512],
                                src[:, :]).then_inc(s_kev, 1)
                    elif kind == "V":
                        kt, half = argv
                        dst = V_stage[kb % 2][:, :].rearrange(
                            "p (h c) -> p h c", h=HEADS)
                        sc.copy(
                            dst[:, half * 8:(half + 1) * 8, kt * 65:kt * 65 + 64],
                            src[:, :].rearrange("p (h c) -> p h c", h=8),
                        ).then_inc(s_kev, 1)
                    else:
                        sc.copy(QT_all[:, argv * NQ + kb * 512: argv * NQ + (kb + 1) * 512],
                                src[:, :]).then_inc(s_kev, 1)
                    gidx += 1
            sc.sem_inc(s_ph, 1)

        @block.tensor
        def _(te):
            te.wait_ge(s_x, 16)
            te.wait_ge(s_wc, 48)
            gidx = 0
            for kb in range(KB):
                for t in range(4):
                    T = 4 * kb + t
                    te.wait_ge(s_ln, 5 * kb + t + 2)
                    for c in range(CC):
                        gt = T * 8 + c
                        if gt >= 4:
                            te.wait_ge(s_xnb, gt - 3)
                        te.transpose(ps_T[gt % 4][:, :],
                                     xn_t[t][:, c * 128:(c + 1) * 128],
                                     ident[:, :]).then_inc(s_tp, 1)
                te.wait_ge(s_xnb, 32 * (kb + 1))
                xn = xnT[kb % 2][:, :].rearrange("p (cc c) -> p cc c", cc=CC)
                for _ in range(kb_end_count[kb] - kb_start[kb]):
                    kbx, kind, argv = kvq_sched[gidx]
                    if gidx >= 2:
                        te.wait_ge(s_kev, gidx - 1)
                    dstp = ps_M[gidx % 2]
                    for c in range(CC):
                        if kind == "K":
                            ins = te.matmul(dstp[:, :],
                                            WkT_sb[:, c * D + argv * 128: c * D + (argv + 1) * 128],
                                            xn[:, c, :], start=(c == 0), stop=(c == CC - 1))
                        elif kind == "V":
                            kt, half = argv
                            ins = te.matmul(dstp[:, :],
                                            xn[:, c, kt * 128:(kt + 1) * 128],
                                            WvT_sb[:, c * D + half * 512: c * D + (half + 1) * 512],
                                            start=(c == 0), stop=(c == CC - 1))
                        else:
                            ins = te.matmul(dstp[:, :],
                                            WqT_sb[:, c * D + argv * 128: c * D + (argv + 1) * 128],
                                            xn[:, c, :], start=(c == 0), stop=(c == CC - 1))
                        if c == CC - 1:
                            ins.then_inc(s_kmm, 1)
                    gidx += 1
            te.wait_ge(s_kev, len(kvq_sched))
            te.sem_inc(s_ph, 1)

        PH[0] += 5

        if dump == "A":
            _emit_dump(block, PH[0], s_ph, s_od, [
                (dumps["d_qt"].ap(), QT_all[:, :]),
                (dumps["d_kt"].ap(), KT_dram.ap()),
                (dumps["d_v"].ap(), V_dram.ap()),
            ])
            a.close()
            s_ab.close()
            return nc, dumps

        a.close()
        b = ExitStack()
        bsb = lambda n, sh, d: b.enter_context(nc.sbuf_tensor(n, sh, d))
        bps = lambda n, sh, d: b.enter_context(nc.psum_tensor(n, sh, d))
        KT_buf = [bsb(f"KT_b{i}", [128, LK], BF16) for i in range(2)]
        V_buf = [bsb(f"V_b{i}", [128, 2 * NKT * VSLOT], BF16) for i in range(2)]
        P_buf = bsb("P_b", [128, 4 * 1024], BF16)
        OT_stage = [bsb(f"OT_st{i}", [64, 2 * NQ], BF16) for i in range(2)]
        rinv = bsb("rinv", [128, 1024], F32)
        rb_sbA = bsb("rb_sbA", [64, 512], F32)
        rb_sbB = bsb("rb_sbB", [64, 512], F32)
        ps_S = [bps(f"ps_S{i}", [128, 1024], F32) for i in range(2)]
        ps_OA = bps("ps_OA", [VSLOT, 512], F32)
        ps_OB = bps("ps_OB", [VSLOT, 512], F32)
        ps_RA = bps("ps_RA", [64, 512], F32)
        ps_RB = bps("ps_RB", [64, 512], F32)
        NKTG = NPQ * NKT

        @block.gpsimd
        def _(gp):
            gp.wait_ge(s_ph, PH[0])

            def load(p):
                if p >= 2:
                    gp.wait_ge(s_rb, QB * (p - 1))
                gp.dma_start(out=KT_buf[p % 2][:, :],
                             in_=KT_dram.ap()[:, p, :]).then_inc(s_kvld, 16)
                gp.dma_start(
                    out=V_buf[p % 2][:, :].rearrange("p (h c) -> p h c", h=2),
                    in_=V_dram.ap()[:, 2 * p:2 * p + 2, :],
                ).then_inc(s_kvld, 16)

            def spill(p):
                gp.wait_ge(s_nrm, 2 * QB * (p + 1))
                gp.dma_start(out=OT_dram.ap()[2 * p, :, :],
                             in_=OT_stage[p % 2][:, 0:NQ]).then_inc(s_ot, 16)
                gp.dma_start(out=OT_dram.ap()[2 * p + 1, :, :],
                             in_=OT_stage[p % 2][:, NQ:2 * NQ]).then_inc(s_ot, 16)

            load(0)
            load(1)
            for p in range(2, PAIRS):
                load(p)
                spill(p - 2)
            spill(PAIRS - 2)
            spill(PAIRS - 1)
            gp.wait_ge(s_ot, 32 * PAIRS)
            gp.sem_inc(s_ph, 1)

        def _score(te, g, kt_b):
            p, rem = divmod(g, QB * NKT)
            q, kt = divmod(rem, NKT)
            te.matmul(ps_S[g % 2][:, 0:512],
                      kt_b[0:64, kt * 128:(kt + 1) * 128],
                      QT_all[0:64, p * NQ + q * 512: p * NQ + (q + 1) * 512],
                      start=True, stop=True, tile_position=(0, 0))
            te.matmul(ps_S[g % 2][:, 512:1024],
                      kt_b[64:128, kt * 128:(kt + 1) * 128],
                      QT_all[64:128, p * NQ + q * 512: p * NQ + (q + 1) * 512],
                      start=True, stop=True,
                      tile_position=(64, 0)).then_inc(s_sc, 1)

        def _av(te, g, v_b):
            p, rem = divmod(g, QB * NKT)
            q, kt = divmod(rem, NKT)
            pq = p * QB + q
            if kt == 0:
                te.wait_ge(s_nrm, 2 * pq)
            sl = P_buf[:, (g % 4) * 1024:(g % 4 + 1) * 1024]
            te.matmul(ps_OA[:, :], v_b[:, 0, kt, :], sl[:, 0:512],
                      start=(kt == 0), stop=(kt == NKT - 1))
            te.matmul(ps_OB[:, :], v_b[:, 1, kt, :], sl[:, 512:1024],
                      start=(kt == 0), stop=(kt == NKT - 1)).then_inc(s_av, 1)
            if kt == NKT - 1:
                te.wait_ge(s_rinv, RIA + pq + 1)
                te.matmul(ps_RA[:, :], ones64[64:65, :], rinv[64:65, 0:512],
                          start=True, stop=True)
                te.matmul(ps_RB[:, :], ones64[64:65, :], rinv[64:65, 512:1024],
                          start=True, stop=True).then_inc(s_rb, 1)

        @block.tensor
        def _(te):
            te.wait_ge(s_ph, PH[0])
            for p in range(PAIRS):
                te.wait_ge(s_kvld, 32 * (p + 1))
                kt_b = KT_buf[p % 2]
                v_b = V_buf[p % 2][:, :].rearrange(
                    "p (h kt c) -> p h kt c", h=2, kt=NKT)
                for gl in range(QB * NKT):
                    g = p * QB * NKT + gl
                    if g >= 2:
                        te.wait_ge(s_exp, g - 1)
                    _score(te, g, kt_b)
                    if gl > 0:
                        te.wait_ge(s_exp, g)
                        _av(te, g - 1, v_b)
                g = (p + 1) * QB * NKT
                te.wait_ge(s_exp, g)
                _av(te, g - 1, v_b)
            te.wait_ge(s_nrm, 2 * NPQ)
            te.sem_inc(s_ph, 1)

        @block.scalar
        def _(sc):
            sc.wait_ge(s_ph, PH[0])
            for g in range(NKTG):
                sc.wait_ge(s_sc, g + 1)
                if g >= 4:
                    sc.wait_ge(s_av, g - 3)
                sc.activation(P_buf[:, (g % 4) * 1024:(g % 4 + 1) * 1024],
                              ps_S[g % 2][:, :], AF.Exp,
                              bias=biasC[:, :], scale=0.125).then_inc(s_exp, 1)
            sc.sem_inc(s_ph, 1)

        @block.vector
        def _(v):
            v.wait_ge(s_ph, PH[0])
            for pq in range(NPQ):
                p, q = divmod(pq, QB)
                v.wait_ge(s_av, NKT * (pq + 1))
                if p >= 2 and q == 0:
                    v.wait_ge(s_ot, 32 * (p - 1))
                v.reciprocal(rinv[64:65, 0:512], ps_OA[64:65, :])
                v.reciprocal(rinv[64:65, 512:1024],
                             ps_OB[64:65, :]).then_inc(s_rinv, 1)
                v.wait_ge(s_rb, pq + 1)
                v.tensor_copy(rb_sbA[:, :], ps_RA[:, :])
                v.tensor_copy(rb_sbB[:, :], ps_RB[:, :]).then_inc(s_nrm, 1)
                v.tensor_tensor(OT_stage[p % 2][:, q * 512:(q + 1) * 512],
                                ps_OA[0:64, :], rb_sbA[:, :], ALU.mult)
                v.tensor_tensor(OT_stage[p % 2][:, NQ + q * 512: NQ + (q + 1) * 512],
                                ps_OB[0:64, :], rb_sbB[:, :],
                                ALU.mult).then_inc(s_nrm, 1)
            v.sem_inc(s_ph, 1)

        @block.sync
        def _(sync):
            sync.sem_inc(s_ph, 1)

        PH[0] += 5

        if dump == "B":
            _emit_dump(block, PH[0], s_ph, s_od, [
                (dumps["d_ot"].ap(), OT_dram.ap()),
            ])
            b.close()
            s_ab.close()
            return nc, dumps

        b.close()
        s_ab.close()
        c_x = ExitStack()
        x1 = c_x.enter_context(nc.sbuf_tensor("x1", [128, 16 * D], F32))
        c = ExitStack()
        OT_pack = c.enter_context(nc.sbuf_tensor("OT_pack", [128, PAIRS * NQ], BF16))
        WoT_sb = c.enter_context(nc.sbuf_tensor("WoT_sb", [128, PAIRS * D], BF16))
        bobc = c.enter_context(nc.sbuf_tensor("bobc", [128, D], F32))
        ps_OP = [c.enter_context(nc.psum_tensor(f"ps_OP{i}", [128, 512], F32))
                 for i in range(2)]
        NCLD = 19 * 16

        @block.sync
        def _(sync):
            sync.wait_ge(s_ph, PH[0])
            sync.dma_start(
                out=x1[:, :].rearrange("p (a d) -> p a d", a=16),
                in_=xb.ap().rearrange("(a p) d -> p a d", p=128)[:, 0:16, :],
            ).then_inc(s_cld, 16)
            sync.sem_inc(s_ph, 1)

        @block.gpsimd
        def _(gp):
            gp.wait_ge(s_ph, PH[0])
            gp.dma_start(out=bobc[:, :],
                         in_=bass.AP(bo, 0, [[0, 128], [1, D]])).then_inc(s_cld, 16)
            gp.dma_start(
                out=WoT_sb[:, :].rearrange("p (pr d) -> p pr d", pr=PAIRS),
                in_=WoT.ap().rearrange("(pr p) d -> p pr d", p=128),
            ).then_inc(s_cld, 16)
            for pr in range(PAIRS):
                gp.dma_start(out=OT_pack[0:64, pr * NQ:(pr + 1) * NQ],
                             in_=OT_dram.ap()[2 * pr, :, :]).then_inc(s_cld, 16)
                gp.dma_start(out=OT_pack[64:128, pr * NQ:(pr + 1) * NQ],
                             in_=OT_dram.ap()[2 * pr + 1, :, :]).then_inc(s_cld, 16)
            gp.sem_inc(s_ph, 1)

        @block.vector
        def _(v):
            v.wait_ge(s_ph, PH[0])
            v.wait_ge(s_cld, NCLD)
            for qt in range(16):
                ins = v.tensor_tensor(x1[:, qt * D:(qt + 1) * D],
                                      x1[:, qt * D:(qt + 1) * D], bobc[:, :], ALU.add)
            ins.then_inc(s_cev, 1)
            for blk in range(32):
                qt, dmb = divmod(blk, 2)
                v.wait_ge(s_cmm, blk + 1)
                sl = x1[:, qt * D + dmb * 512: qt * D + (dmb + 1) * 512]
                v.tensor_tensor(sl, ps_OP[blk % 2][:, :], sl,
                                ALU.add).then_inc(s_cev, 1)
            v.sem_inc(s_ph, 1)

        @block.tensor
        def _(te):
            te.wait_ge(s_ph, PH[0])
            te.wait_ge(s_cld, NCLD)
            for blk in range(32):
                qt, dmb = divmod(blk, 2)
                if blk >= 2:
                    te.wait_ge(s_cev, blk)
                for pr in range(PAIRS):
                    ins = te.matmul(ps_OP[blk % 2][:, :],
                                    OT_pack[:, pr * NQ + qt * 128: pr * NQ + (qt + 1) * 128],
                                    WoT_sb[:, pr * D + dmb * 512: pr * D + (dmb + 1) * 512],
                                    start=(pr == 0), stop=(pr == PAIRS - 1))
                    if pr == PAIRS - 1:
                        ins.then_inc(s_cmm, 1)
            te.wait_ge(s_cev, 33)
            te.sem_inc(s_ph, 1)

        @block.scalar
        def _(sc):
            sc.wait_ge(s_ph, PH[0])
            sc.sem_inc(s_ph, 1)

        PH[0] += 5

        if dump == "C":
            _emit_dump(block, PH[0], s_ph, s_od, [
                (dumps["d_x1"].ap(), x1[:, :]),
            ])
            c.close()
            c_x.close()
            return nc, dumps

        c.close()
        dk = ExitStack()
        xn2T = dk.enter_context(nc.sbuf_tensor("xn2T", [128, CC * NQ], BF16))
        probT = dk.enter_context(nc.sbuf_tensor("probT", [8, NQ], BF16))
        sel_bf = dk.enter_context(nc.sbuf_tensor("sel_bf", [8, 1024], BF16))
        dt_ = ExitStack()
        dsb = lambda n, s, ddt: dt_.enter_context(nc.sbuf_tensor(n, s, ddt))
        dps = lambda n, s, ddt: dt_.enter_context(nc.psum_tensor(n, s, ddt))
        xn2_t = [dsb(f"xn2_t{i}", [128, D], BF16) for i in range(2)]
        d2_s1 = dsb("d2_s1", [128, 4], F32)
        d2_s2 = dsb("d2_s2", [128, 4], F32)
        d2_m = dsb("d2_m", [128, 4], F32)
        d2_q = dsb("d2_q", [128, 4], F32)
        d2_lt = dsb("d2_lt", [128, 4], F32)
        d2_t = dsb("d2_t", [128, 4], F32)
        d2_v = dsb("d2_v", [128, 4], F32)
        d2_r = dsb("d2_r", [128, 4], F32)
        Wr_sb = dsb("Wr_sb", [128, CC * 8], BF16)
        logits = dsb("logits", [128, 128], F32)
        ismax = dsb("ismax", [128, 16 * 8], F32)
        maskd = dsb("maskd", [128, 16 * 8], F32)
        nmax = dsb("nmax", [128, 16], F32)
        m1c = dsb("m1c", [128, 16], F32)
        v2c = dsb("v2c", [128, 16], F32)
        dcol = dsb("dcol", [128, 16], F32)
        ecol = dsb("ecol", [128, 16], F32)
        e2col = dsb("e2col", [128, 16], F32)
        e3col = dsb("e3col", [128, 16], F32)
        gcol = dsb("gcol", [128, 16], F32)
        g2col = dsb("g2col", [128, 16], F32)
        g3col = dsb("g3col", [128, 16], F32)
        prob_all = dsb("prob_all", [128, 16 * 8], BF16)
        sel_i = dsb("sel_i", [8, 1024], I32)
        ps_T2 = [dps(f"ps_T2{i}", [128, 128], BF16) for i in range(4)]
        ps_LG = [dps(f"ps_LG{i}", [128, 8], F32) for i in range(2)]
        ps_PT = dps("ps_PT", [8, 128], BF16)

        x1r = x1[:, :].rearrange("p (qt d) -> p qt d", qt=16)

        @block.gpsimd
        def _(gp):
            gp.wait_ge(s_ph, PH[0])
            gp.dma_start(
                out=Wr_sb[:, :].rearrange("p (cc e) -> p cc e", cc=CC),
                in_=Wr.ap().rearrange("(cc p) e -> p cc e", p=128),
            ).then_inc(s_wc, 16)
            gp.iota(sel_i[:, :], [[1, 8], [0, 128]], channel_multiplier=-1,
                    allow_small_or_imprecise_dtypes=True).then_inc(s_wc, 1)
            gp.sem_inc(s_ph, 1)

        @block.sync
        def _(sync):
            sync.wait_ge(s_ph, PH[0])
            sync.sem_inc(s_ph, 1)

        @block.vector
        def _(v):
            v.wait_ge(s_ph, PH[0])
            v.wait_ge(s_wc, 65)
            v.tensor_scalar(sel_bf[:, :], sel_i[:, :], 0.0, None, ALU.is_equal)
            for g4 in range(4):
                for t in range(4):
                    xt = x1r[:, g4 * 4 + t, :]
                    sq = scrq[:, t * D:(t + 1) * D]
                    v.reduce_sum(d2_s1[:, t:t + 1], xt, axis=AX.X)
                    v.tensor_tensor(sq, xt, xt, ALU.mult)
                    v.reduce_sum(d2_s2[:, t:t + 1], sq, axis=AX.X)
                v.tensor_scalar(d2_m[:, :], d2_s1[:, :], 1.0 / D, None, ALU.mult)
                v.tensor_scalar(d2_q[:, :], d2_s2[:, :], 1.0 / D, None, ALU.mult)
                v.tensor_copy(junk0[:, 0:4], d2_m[:, :])
                v.tensor_copy(junk0[:, 4:8], d2_m[:, :])
                v.tensor_tensor(d2_t[:, :], d2_m[:, :], d2_m[:, :], ALU.mult)
                v.tensor_copy(junk0[:, 0:4], d2_t[:, :])
                v.tensor_copy(junk0[:, 4:8], d2_t[:, :])
                v.tensor_tensor(d2_v[:, :], d2_q[:, :], d2_t[:, :],
                                ALU.subtract).then_inc(s_ln, 1)
                v.wait_ge(s_rinv, RIB + g4 + 1)
                for t in range(4):
                    ti = g4 * 4 + t
                    if ti >= 2:
                        v.wait_ge(s_tp, TPA + 8 * (ti - 1))
                    v.tensor_scalar(xn2_t[ti % 2][:, :], x1r[:, ti, :],
                                    d2_m[:, t:t + 1], d2_r[:, t:t + 1],
                                    ALU.subtract, ALU.mult).then_inc(s_ln, 1)
                    dst = xn2T[:, :].rearrange("p (cc c) -> p cc c", cc=CC)
                    for cc in range(CC):
                        gt = ti * 8 + cc
                        v.wait_ge(s_tp, TPA + gt + 1)
                        v.tensor_copy(dst[:, cc, ti * 128:(ti + 1) * 128],
                                      ps_T2[gt % 4][:, :]).then_inc(s_xnb, 1)
            for qt in range(16):
                v.wait_ge(s_lg, qt + 1)
                v.tensor_copy(logits[:, qt * 8:(qt + 1) * 8],
                              ps_LG[qt % 2][:, :]).then_inc(s_lge, 1)
            for qt in range(16):
                v.tensor_reduce(m1c[:, qt:qt + 1],
                                logits[:, qt * 8:(qt + 1) * 8], AX.X, ALU.max)
            for qt in range(16):
                v.tensor_scalar(ismax[:, qt * 8:(qt + 1) * 8],
                                logits[:, qt * 8:(qt + 1) * 8],
                                m1c[:, qt:qt + 1], None, ALU.is_equal)
            for qt in range(16):
                v.tensor_reduce(nmax[:, qt:qt + 1],
                                ismax[:, qt * 8:(qt + 1) * 8], AX.X, ALU.add)
            for qt in range(16):
                v.tensor_scalar(maskd[:, qt * 8:(qt + 1) * 8],
                                ismax[:, qt * 8:(qt + 1) * 8], -1e30, None,
                                ALU.mult)
            for qt in range(16):
                msl = maskd[:, qt * 8:(qt + 1) * 8]
                v.tensor_tensor(msl, msl, logits[:, qt * 8:(qt + 1) * 8], ALU.add)
            for qt in range(16):
                v.tensor_reduce(v2c[:, qt:qt + 1],
                                maskd[:, qt * 8:(qt + 1) * 8], AX.X, ALU.max)
            v.tensor_tensor(dcol[:, :], v2c[:, :], m1c[:, :],
                            ALU.subtract).then_inc(s_rt, 1)
            v.wait_ge(s_rt, 2)
            v.tensor_scalar(e2col[:, :], ecol[:, :], 1.0, None, ALU.add)
            v.tensor_scalar(g2col[:, :], nmax[:, :], 1.5, None, ALU.is_le)
            v.tensor_copy(junk0[:, 0:8], e2col[:, 0:8])
            v.reciprocal(e3col[:, :], e2col[:, :])
            v.tensor_copy(junk0[:, 0:8], e3col[:, 0:8])
            v.tensor_copy(junk0[:, 0:8], e3col[:, 8:16])
            v.tensor_scalar(gcol[:, :], e3col[:, :], 1.0, None, ALU.is_ge)
            v.tensor_copy(junk0[:, 0:8], gcol[:, 0:8])
            v.tensor_copy(junk0[:, 0:8], gcol[:, 8:16])
            v.tensor_tensor(g3col[:, :], gcol[:, :], g2col[:, :], ALU.mult)
            v.tensor_copy(junk0[:, 0:8], g3col[:, 0:8])
            v.tensor_copy(junk0[:, 0:8], g3col[:, 8:16])
            for qt in range(16):
                ins = v.tensor_scalar(prob_all[:, qt * 8:(qt + 1) * 8],
                                      ismax[:, qt * 8:(qt + 1) * 8],
                                      g3col[:, qt:qt + 1], None, ALU.mult)
            ins.then_inc(s_rt, 1)
            for qt in range(16):
                v.wait_ge(s_lg, 16 + qt + 1)
                v.tensor_copy(probT[:, qt * 128:(qt + 1) * 128],
                              ps_PT[:, :]).then_inc(s_xnb, 1)
            v.sem_inc(s_ph, 1)

        @block.scalar
        def _(sc):
            sc.wait_ge(s_ph, PH[0])
            for g4 in range(4):
                sc.wait_ge(s_ln, LNA + 5 * g4 + 1)
                sc.activation(d2_lt[:, :], d2_v[:, :], AF.Ln, bias=epsc[:, :], scale=1.0)
                sc.activation(d2_r[:, :], d2_lt[:, :], AF.Exp, bias=0.0,
                              scale=-0.5).then_inc(s_rinv, 1)
            sc.wait_ge(s_rt, 1)
            sc.activation(ecol[:, :], dcol[:, :], AF.Exp, bias=0.0,
                          scale=1.0).then_inc(s_rt, 1)
            sc.sem_inc(s_ph, 1)

        @block.tensor
        def _(te):
            te.wait_ge(s_ph, PH[0])
            for ti in range(16):
                te.wait_ge(s_ln, LNA + 5 * (ti // 4) + (ti % 4) + 2)
                for cc in range(CC):
                    gt = ti * 8 + cc
                    if gt >= 4:
                        te.wait_ge(s_xnb, XNA + gt - 3)
                    te.transpose(ps_T2[gt % 4][:, :],
                                 xn2_t[ti % 2][:, cc * 128:(cc + 1) * 128],
                                 ident[:, :]).then_inc(s_tp, 1)
            te.wait_ge(s_xnb, XNA + 128)
            te.wait_ge(s_wc, 65)
            xn2 = xn2T[:, :].rearrange("p (cc c) -> p cc c", cc=CC)
            for qt in range(16):
                if qt >= 2:
                    te.wait_ge(s_lge, qt - 1)
                for cc in range(CC):
                    ins = te.matmul(ps_LG[qt % 2][:, :],
                                    xn2[:, cc, qt * 128:(qt + 1) * 128],
                                    Wr_sb[:, cc * 8:(cc + 1) * 8],
                                    start=(cc == 0), stop=(cc == CC - 1))
                    if cc == CC - 1:
                        ins.then_inc(s_lg, 1)
            te.wait_ge(s_rt, 3)
            for qt in range(16):
                if qt >= 1:
                    te.wait_ge(s_xnb, XNA + 128 + qt)
                te.transpose(ps_PT[:, :], prob_all[:, qt * 8:(qt + 1) * 8],
                             ident[:, :]).then_inc(s_lg, 1)
            te.wait_ge(s_xnb, XNA + 128 + 16)
            te.sem_inc(s_ph, 1)

        PH[0] += 5

        if dump == "D":
            _emit_dump(block, PH[0], s_ph, s_od, [
                (dumps["d_xn2"].ap(), xn2T[:, :]),
                (dumps["d_prob"].ap(), prob_all[:, :]),
                (dumps["d_probT"].ap(), probT[:, :]),
            ])
            dt_.close()
            dk.close()
            c_x.close()
            return nc, dumps

        dt_.close()
        e = ExitStack()
        esb = lambda n, s, ddt: e.enter_context(nc.sbuf_tensor(n, s, ddt))
        W1_sb = [esb(f"W1_sb{i}", [128, CC * 512], BF16) for i in range(2)]
        W2_sb = [esb(f"W2_sb{i}", [128, FC * D], BF16) for i in range(2)]
        H_sb = [esb(f"H_sb{i}", [128, FC * 512], BF16) for i in range(2)]
        acc_sb = esb("acc_sb", [128, 4 * D], F32)
        pb_sb = esb("pb_sb", [128, 8 * 512], BF16)
        r_f = [esb(f"r_f{i}", [128, 512], F32) for i in range(4)]
        r2_f = [esb(f"r2_f{i}", [128, 512], F32) for i in range(2)]
        out_stage = [esb(f"out_st{i}", [128, 512], F32) for i in range(2)]
        ps_H = [e.enter_context(nc.psum_tensor(f"ps_H{i}", [128, 512], F32))
                for i in range(2)]
        ps_O = [e.enter_context(nc.psum_tensor(f"ps_O{i}", [128, 512], F32))
                for i in range(2)]
        ps_PB = e.enter_context(nc.psum_tensor("ps_PB", [128, 512], F32))

        out_r = out_d.ap().rearrange("(tt p) d -> p tt d", p=128)
        W1src = [W1.ap()[ei] for ei in range(8)] + [Ws1.ap()]
        W2src = [W2.ap()[ei] for ei in range(8)] + [Ws2.ap()]

        @block.gpsimd
        def _(gp):
            gp.wait_ge(s_ph, PH[0])
            for tq in range(TQ):
                for ei in range(NE):
                    gidx = tq * NE + ei
                    if gidx >= 2:
                        gp.wait_ge(s_hmm, (gidx - 1) * FC)
                    gp.dma_start(
                        out=W1_sb[gidx % 2][:, :].rearrange(
                            "p (cc f) -> p cc f", cc=CC),
                        in_=W1src[ei].rearrange("(cc p) f -> p cc f", p=128),
                    ).then_inc(s_wld, 16)
                    if gidx >= 2:
                        gp.wait_ge(s_eomm, (gidx - 1) * 8)
                    gp.dma_start(
                        out=W2_sb[gidx % 2][:, :].rearrange(
                            "p (fc d) -> p fc d", fc=FC),
                        in_=W2src[ei].rearrange("(fc p) d -> p fc d", p=128),
                    ).then_inc(s_wld, 16)
                for blk in range(8):
                    tb, dmb = divmod(blk, 2)
                    gp.wait_ge(s_oev, tq * 8 + blk + 1)
                    gp.dma_start(
                        out=out_r[:, tq * 4 + tb, dmb * 512:(dmb + 1) * 512],
                        in_=out_stage[blk % 2][:, :]).then_inc(s_od, 16)
            gp.wait_ge(s_od, 16 * 32)

        @block.tensor
        def _(te):
            te.wait_ge(s_ph, PH[0])
            xn2 = xn2T[:, :].rearrange("p (cc c) -> p cc c", cc=CC)
            for tq in range(TQ):
                for ei in range(8):
                    te.wait_ge(s_pbe, 8 * tq + ei)
                    te.matmul(ps_PB[:, :], sel_bf[:, ei * 128:(ei + 1) * 128],
                              probT[:, tq * 512:(tq + 1) * 512],
                              start=True, stop=True).then_inc(s_pb, 1)
                for ei in range(NE):
                    gidx = tq * NE + ei
                    te.wait_ge(s_wld, 16 * (2 * gidx + 1))
                    for fc in range(FC):
                        hg = gidx * FC + fc
                        if hg >= 2:
                            te.wait_ge(s_hact, hg - 1)
                        for cc in range(CC):
                            ins = te.matmul(ps_H[hg % 2][:, :],
                                            W1_sb[gidx % 2][:, cc * 512 + fc * 128:
                                                            cc * 512 + (fc + 1) * 128],
                                            xn2[:, cc, tq * 512:(tq + 1) * 512],
                                            start=(cc == 0), stop=(cc == CC - 1))
                            if cc == CC - 1:
                                ins.then_inc(s_hmm, 1)
                    te.wait_ge(s_wld, 16 * (2 * gidx + 2))
                    te.wait_ge(s_he, gidx * FC + FC)
                    for blk in range(8):
                        og = gidx * 8 + blk
                        tb, dmb = divmod(blk, 2)
                        if og >= 2:
                            te.wait_ge(s_eoadd, og - 1)
                        for fc in range(FC):
                            ins = te.matmul(ps_O[og % 2][:, :],
                                            H_sb[gidx % 2][:, fc * 512 + tb * 128:
                                                           fc * 512 + (tb + 1) * 128],
                                            W2_sb[gidx % 2][:, fc * D + dmb * 512:
                                                            fc * D + (dmb + 1) * 512],
                                            start=(fc == 0), stop=(fc == FC - 1))
                            if fc == FC - 1:
                                ins.then_inc(s_eomm, 1)

        @block.scalar
        def _(sc):
            sc.wait_ge(s_ph, PH[0])
            for hg in range(TQ * NE * FC):
                sc.wait_ge(s_hmm, hg + 1)
                if hg >= 2:
                    sc.wait_ge(s_he, hg - 1)
                sc.activation(r_f[hg % 4][:, :], ps_H[hg % 2][:, :],
                              AF.Relu, bias=0.0, scale=1.0)
                sc.activation(r2_f[hg % 2][:, :], r_f[hg % 4][:, :],
                              AF.Square, bias=0.0, scale=1.0).then_inc(s_hact, 1)

        @block.vector
        def _(v):
            v.wait_ge(s_ph, PH[0])
            for tq in range(TQ):
                for ei in range(8):
                    v.wait_ge(s_pb, 8 * tq + ei + 1)
                    v.tensor_copy(pb_sb[:, ei * 512:(ei + 1) * 512],
                                  ps_PB[:, :]).then_inc(s_pbe, 1)
                for ei in range(NE):
                    gidx = tq * NE + ei
                    pbs = onesbc[:, :] if ei == 8 else pb_sb[:, ei * 512:(ei + 1) * 512]
                    for fc in range(FC):
                        hg = gidx * FC + fc
                        v.wait_ge(s_hact, hg + 1)
                        if gidx >= 2 and fc == 0:
                            v.wait_ge(s_eomm, (gidx - 2) * 8 + 8)
                        v.tensor_tensor(H_sb[gidx % 2][:, fc * 512:(fc + 1) * 512],
                                        r2_f[hg % 2][:, :], pbs,
                                        ALU.mult).then_inc(s_he, 1)
                    for blk in range(8):
                        og = gidx * 8 + blk
                        tb, dmb = divmod(blk, 2)
                        v.wait_ge(s_eomm, og + 1)
                        sl = acc_sb[:, tb * D + dmb * 512: tb * D + (dmb + 1) * 512]
                        if ei == 0:
                            v.tensor_copy(sl, ps_O[og % 2][:, :]).then_inc(s_eoadd, 1)
                        else:
                            v.tensor_tensor(sl, ps_O[og % 2][:, :], sl,
                                            ALU.add).then_inc(s_eoadd, 1)
                for blk in range(8):
                    tb, dmb = divmod(blk, 2)
                    if tq * 8 + blk >= 2:
                        v.wait_ge(s_od, 16 * (tq * 8 + blk - 1))
                    tt = tq * 4 + tb
                    v.tensor_tensor(out_stage[blk % 2][:, :],
                                    acc_sb[:, tb * D + dmb * 512: tb * D + (dmb + 1) * 512],
                                    x1[:, tt * D + dmb * 512: tt * D + (dmb + 1) * 512],
                                    ALU.add).then_inc(s_oev, 1)

        e.close()
        dk.close()
        c_x.close()
    return nc, dumps


def _emit_dump(block, ph_val, s_ph, s_od, pairs):
    @block.gpsimd
    def _(gp):
        gp.wait_ge(s_ph, ph_val)
        n = 0
        for dst, src in pairs:
            gp.dma_start(out=dst, in_=src).then_inc(s_od, 16)
            n += 16
        gp.wait_ge(s_od, n)


_PROG_CACHE = {}


def _get_program(dump=None):
    if dump not in _PROG_CACHE:
        _PROG_CACHE[dump] = build_program(dump)
    return _PROG_CACHE[dump]


def make_in_maps(x, WqT, WkT, WvT, WoT, bo, Ws1, Ws2, W1, W2, Wr):
    in_maps = []
    for core in range(8):
        bidx, qidx = divmod(core, 4)
        xb_c = np.ascontiguousarray(np.roll(x[bidx], -qidx * NQ, axis=0))
        in_maps.append({
            "xb": xb_c, "WqT": WqT, "WkT": WkT, "WvT": WvT, "WoT": WoT,
            "bo": bo, "Ws1": Ws1, "Ws2": Ws2, "W1": W1, "W2": W2, "Wr": Wr,
        })
    return in_maps


def prep_inputs(L0_attn_x, L0_attn_W_q, L0_attn_W_k, L0_attn_W_v, L0_attn_W_o,
                L0_attn_b_o, L0_ffn_Ws1, L0_ffn_Ws2, L0_ffn_W1, L0_ffn_W2,
                L0_ffn_Wr):
    f = lambda v: np.ascontiguousarray(np.asarray(v, dtype=np.float32))
    ft = lambda v: np.ascontiguousarray(np.asarray(v, dtype=np.float32).T)
    return (f(L0_attn_x), ft(L0_attn_W_q), ft(L0_attn_W_k), ft(L0_attn_W_v),
            ft(L0_attn_W_o), f(L0_attn_b_o), f(L0_ffn_Ws1), f(L0_ffn_Ws2),
            f(L0_ffn_W1), f(L0_ffn_W2), f(L0_ffn_Wr))


def kernel(**inputs):
    arrs = prep_inputs(**inputs)
    nc, _ = _get_program()
    in_maps = make_in_maps(*arrs)
    res = run_bass_kernel_spmd(nc, in_maps, core_ids=list(range(8)))
    out = np.empty((2, LK, D), dtype=np.float32)
    for core in range(8):
        bidx, qidx = divmod(core, 4)
        out[bidx, qidx * NQ:(qidx + 1) * NQ] = np.asarray(
            res.results[core]["out"], dtype=np.float32)
    return out


# version 03
# speedup vs baseline: 1.6970x; 1.0103x over previous
import numpy as np
import ml_dtypes
from contextlib import ExitStack

import concourse.bass as bass
import concourse.mybir as mybir
from concourse.bass_utils import run_bass_kernel_spmd

F32 = mybir.dt.float32
BF16 = mybir.dt.bfloat16
I32 = mybir.dt.int32
AF = mybir.ActivationFunctionType
ALU = mybir.AluOpType
AX = mybir.AxisListType

D = 1024
LK = 8192
NQ = 2048
CC = 8
KB = 16
QKB = 4
PAIRS = 8
HEADS = 16
QB = 4
NKT = 64
GRP = 2
CH = 8
GPC = CH // GRP
NCH = NKT // CH
NPQ = PAIRS * QB
NGRP = NPQ * NKT // GRP
CSH = 10.0
EPS = 1e-5
FC = 4
NE = 9
TQ = 4
VSLOT = 65

LNA = 5 * KB
TPA = 8 * 4 * KB
XNA = 8 * 4 * KB
RIA = KB
RIB = RIA + NPQ


def build_program(dump=None):
    nc = bass.Bass("TRN2", target_bir_lowering=False,
                   detect_race_conditions=False)

    xb = nc.dram_tensor("xb", [LK, D], F32, kind="ExternalInput")
    WqT = nc.dram_tensor("WqT", [D, D], F32, kind="ExternalInput")
    WkT = nc.dram_tensor("WkT", [D, D], F32, kind="ExternalInput")
    WvT = nc.dram_tensor("WvT", [D, D], F32, kind="ExternalInput")
    WoT = nc.dram_tensor("WoT", [D, D], F32, kind="ExternalInput")
    bo = nc.dram_tensor("bo", [D], F32, kind="ExternalInput")
    Ws1 = nc.dram_tensor("Ws1", [D, 512], F32, kind="ExternalInput")
    Ws2 = nc.dram_tensor("Ws2", [512, D], F32, kind="ExternalInput")
    W1 = nc.dram_tensor("W1", [8, D, 512], F32, kind="ExternalInput")
    W2 = nc.dram_tensor("W2", [8, 512, D], F32, kind="ExternalInput")
    Wr = nc.dram_tensor("Wr", [D, 8], F32, kind="ExternalInput")
    out_d = nc.dram_tensor("out", [NQ, D], F32, kind="ExternalOutput")

    KT_dram = nc.dram_tensor("KT_scr", [128, PAIRS, LK], BF16)
    V_dram = nc.dram_tensor("V_scr", [128, HEADS, NKT * VSLOT], BF16)
    OT_dram = nc.dram_tensor("OT_scr", [HEADS, 64, NQ], BF16)
    Wf1_dram = nc.dram_tensor("Wf1_scr", [NE, 128, CC * 512], BF16)
    Wf2_dram = nc.dram_tensor("Wf2_scr", [NE, 128, FC * D], BF16)

    ident_dram = nc.inline_tensor(np.eye(128, dtype=ml_dtypes.bfloat16), "ident_c")

    dumps = {}
    if dump == "A":
        dumps["d_qt"] = nc.dram_tensor("d_qt", [128, PAIRS * NQ], BF16, kind="ExternalOutput")
        dumps["d_kt"] = nc.dram_tensor("d_kt", [128, PAIRS, LK], BF16, kind="ExternalOutput")
        dumps["d_v"] = nc.dram_tensor("d_v", [128, HEADS, NKT * VSLOT], BF16, kind="ExternalOutput")
    if dump == "B":
        dumps["d_ot"] = nc.dram_tensor("d_ot", [HEADS, 64, NQ], BF16, kind="ExternalOutput")
    if dump == "C":
        dumps["d_x1"] = nc.dram_tensor("d_x1", [128, 16 * D], F32, kind="ExternalOutput")
    if dump == "D":
        dumps["d_xn2"] = nc.dram_tensor("d_xn2", [128, CC * NQ], BF16, kind="ExternalOutput")
        dumps["d_prob"] = nc.dram_tensor("d_prob", [128, 16 * 8], BF16, kind="ExternalOutput")
        dumps["d_probT"] = nc.dram_tensor("d_probT", [8, NQ], BF16, kind="ExternalOutput")

    g = ExitStack()
    with g:
        sem = lambda n: g.enter_context(nc.semaphore(n))

        ident = g.enter_context(nc.sbuf_tensor("ident", [128, 128], BF16))
        biasC = g.enter_context(nc.sbuf_tensor("biasC", [128, 1], F32))
        epsc = g.enter_context(nc.sbuf_tensor("epsc", [128, 1], F32))
        ones64 = g.enter_context(nc.sbuf_tensor("ones64", [128, 64], F32))
        onesbc = g.enter_context(nc.sbuf_tensor("onesbc", [128, 512], BF16))
        scrq = g.enter_context(nc.sbuf_tensor("scrq", [128, 4 * D], F32))
        junk0 = g.enter_context(nc.sbuf_tensor("junk0", [128, 8], F32))

        s_ph = sem("s_ph")
        s_x = sem("s_x")
        s_wc = sem("s_wc")
        s_ln = sem("s_ln")
        s_tp = sem("s_tp")
        s_xnb = sem("s_xnb")
        s_kmm = sem("s_kmm")
        s_kev = sem("s_kev")
        s_stg = sem("s_stg")
        s_kvld = sem("s_kvld")
        s_sc = sem("s_sc")
        s_exp = sem("s_exp")
        s_av = sem("s_av")
        s_rinv = sem("s_rinv")
        s_rb = sem("s_rb")
        s_nrm = sem("s_nrm")
        s_ot = sem("s_ot")
        s_cld = sem("s_cld")
        s_cmm = sem("s_cmm")
        s_cev = sem("s_cev")
        s_lg = sem("s_lg")
        s_lge = sem("s_lge")
        s_rt = sem("s_rt")
        s_wld = sem("s_wld")
        s_w2l = sem("s_w2l")
        s_pb = sem("s_pb")
        s_pbe = sem("s_pbe")
        s_hmm = sem("s_hmm")
        s_hact = sem("s_hact")
        s_he = sem("s_he")
        s_eomm = sem("s_eomm")
        s_eoadd = sem("s_eoadd")
        s_oev = sem("s_oev")
        s_od = sem("s_od")

        PH = [0]
        block = g.enter_context(nc.Block())

        s_ab = ExitStack()
        QT_all = s_ab.enter_context(nc.sbuf_tensor("QT_all", [128, PAIRS * NQ], BF16))

        a = ExitStack()
        asb = lambda n, s, d: a.enter_context(nc.sbuf_tensor(n, s, d))
        aps = lambda n, s, d: a.enter_context(nc.psum_tensor(n, s, d))
        WqT_sb = asb("WqT_sb", [128, CC * D], BF16)
        WkT_sb = asb("WkT_sb", [128, CC * D], BF16)
        WvT_sb = asb("WvT_sb", [128, CC * D], BF16)
        xblk = [asb(f"xblk{i}", [128, 4 * D], F32) for i in range(2)]
        xn_t = [asb(f"xn_t{i}", [128, D], BF16) for i in range(4)]
        st_s1 = asb("st_s1", [128, 4], F32)
        st_s2 = asb("st_s2", [128, 4], F32)
        st_m = asb("st_m", [128, 4], F32)
        st_q = asb("st_q", [128, 4], F32)
        st_lt = asb("st_lt", [128, 4], F32)
        st_t = asb("st_t", [128, 4], F32)
        st_v = asb("st_v", [128, 4], F32)
        st_r = asb("st_r", [128, 4], F32)
        xnT = [asb(f"xnT{i}", [128, CC * 512], BF16) for i in range(2)]
        KT_stage = [asb(f"KT_st{i}", [128, PAIRS * 512], BF16) for i in range(2)]
        V_stage = [asb(f"V_st{i}", [128, HEADS * 260], BF16) for i in range(2)]
        ps_T = [aps(f"ps_T{i}", [128, 128], BF16) for i in range(4)]
        ps_M = [aps(f"ps_M{i}", [128, 512], F32) for i in range(4)]

        kvq_sched = []
        for kb in range(KB):
            for pr in range(PAIRS):
                kvq_sched.append((kb, "K", pr))
            for kt in range(4):
                for half in range(2):
                    kvq_sched.append((kb, "V", (kt, half)))
            if kb < QKB:
                for pr in range(PAIRS):
                    kvq_sched.append((kb, "Q", pr))
        kb_end_count = {}
        for i, (kb, kind, argv) in enumerate(kvq_sched):
            kb_end_count[kb] = i + 1
        kb_start = {kb: (kb_end_count[kb - 1] if kb else 0) for kb in range(KB)}

        @block.sync
        def _(sync):
            sync.dma_start(out=ident[:, :], in_=ident_dram.ap()).then_inc(s_x, 16)
            for kb in range(KB):
                if kb >= 2:
                    sync.wait_ge(s_ln, 5 * (kb - 1))
                sync.dma_start(
                    out=xblk[kb % 2][:, :].rearrange("p (s d) -> p s d", s=4),
                    in_=xb.ap().rearrange("(kb s p) d -> kb p s d", p=128, s=4)[kb],
                ).then_inc(s_x, 16)
            sync.sem_inc(s_ph, 1)

        @block.gpsimd
        def _(gp):
            for W, Wsb in ((WqT, WqT_sb), (WkT, WkT_sb), (WvT, WvT_sb)):
                gp.dma_start(
                    out=Wsb[:, :].rearrange("p (cc d) -> p cc d", cc=CC),
                    in_=W.ap().rearrange("(cc p) d -> p cc d", p=128),
                ).then_inc(s_wc, 16)
            for i in range(2):
                vst = V_stage[i][:, :].rearrange("p (h kt c) -> p h kt c", h=HEADS, kt=4)
                gp.memset(vst[:, :, :, 64:65], 1.0)
            for kb in range(KB):
                gp.wait_ge(s_kev, kb_end_count[kb])
                gp.dma_start(
                    out=KT_dram.ap()[:, :, kb * 512:(kb + 1) * 512],
                    in_=KT_stage[kb % 2][:, :].rearrange("p (pr c) -> p pr c", pr=PAIRS),
                ).then_inc(s_stg, 16)
                gp.dma_start(
                    out=V_dram.ap()[:, :, kb * 260:(kb + 1) * 260],
                    in_=V_stage[kb % 2][:, :].rearrange("p (h c) -> p h c", h=HEADS),
                ).then_inc(s_stg, 16)
            gp.wait_ge(s_stg, 32 * KB)
            gp.sem_inc(s_ph, 1)

        @block.vector
        def _(v):
            v.memset(biasC[:, :], -CSH)
            v.memset(epsc[:, :], EPS)
            v.memset(ones64[64:65, :], 1.0)
            v.memset(onesbc[:, :], 1.0)
            v.memset(scrq[:, :], 1.0)
            v.memset(junk0[:, :], 1.0)
            for _w in range(3):
                v.reduce_sum(junk0[:, 0:1], scrq[:, 0:D], axis=AX.X)
                v.tensor_tensor(scrq[:, 0:D], scrq[:, 0:D], scrq[:, 0:D], ALU.mult)
                v.tensor_tensor(junk0[:, 4:8], junk0[:, 0:4], junk0[:, 0:4],
                                ALU.subtract)
                v.tensor_scalar(junk0[:, 0:4], junk0[:, 4:8], 0.5, None, ALU.mult)
                v.tensor_copy(junk0[:, 4:8], junk0[:, 0:4])
                v.reciprocal(junk0[:, 0:4], scrq[:, 0:4])
            for kb in range(KB):
                v.wait_ge(s_x, 16 * (kb + 2))
                for t in range(4):
                    xt = xblk[kb % 2][:, t * D:(t + 1) * D]
                    sq = scrq[:, t * D:(t + 1) * D]
                    v.reduce_sum(st_s1[:, t:t + 1], xt, axis=AX.X)
                    v.tensor_tensor(sq, xt, xt, ALU.mult)
                    v.reduce_sum(st_s2[:, t:t + 1], sq, axis=AX.X)
                v.tensor_scalar(st_m[:, :], st_s1[:, :], 1.0 / D, None, ALU.mult)
                v.tensor_scalar(st_q[:, :], st_s2[:, :], 1.0 / D, None, ALU.mult)
                v.tensor_copy(junk0[:, 0:4], st_m[:, :])
                v.tensor_copy(junk0[:, 4:8], st_m[:, :])
                v.tensor_tensor(st_t[:, :], st_m[:, :], st_m[:, :], ALU.mult)
                v.tensor_copy(junk0[:, 0:4], st_t[:, :])
                v.tensor_copy(junk0[:, 4:8], st_t[:, :])
                v.tensor_tensor(st_v[:, :], st_q[:, :], st_t[:, :],
                                ALU.subtract).then_inc(s_ln, 1)
                v.wait_ge(s_rinv, kb + 1)
                for t in range(4):
                    T = 4 * kb + t
                    if T >= 4:
                        v.wait_ge(s_tp, 8 * (T - 3))
                    xt = xblk[kb % 2][:, t * D:(t + 1) * D]
                    v.tensor_scalar(xn_t[t][:, :], xt, st_m[:, t:t + 1],
                                    st_r[:, t:t + 1], ALU.subtract,
                                    ALU.mult).then_inc(s_ln, 1)
                for t in range(4):
                    T = 4 * kb + t
                    if kb >= 2 and t == 0:
                        v.wait_ge(s_kmm, kb_end_count[kb - 2])
                    dst = xnT[kb % 2][:, :].rearrange("p (cc c) -> p cc c", cc=CC)
                    for c in range(CC):
                        gt = T * 8 + c
                        v.wait_ge(s_tp, gt + 1)
                        v.tensor_copy(dst[:, c, t * 128:(t + 1) * 128],
                                      ps_T[gt % 4][:, :]).then_inc(s_xnb, 1)
            v.sem_inc(s_ph, 1)

        @block.scalar
        def _(sc):
            gidx = 0
            for kb in range(KB):
                sc.wait_ge(s_ln, 5 * kb + 1)
                sc.activation(st_lt[:, :], st_v[:, :], AF.Ln, bias=epsc[:, :], scale=1.0)
                sc.activation(st_r[:, :], st_lt[:, :], AF.Exp, bias=0.0,
                              scale=-0.5).then_inc(s_rinv, 1)
                for _ in range(kb_end_count[kb] - kb_start[kb]):
                    kbx, kind, argv = kvq_sched[gidx]
                    sc.wait_ge(s_kmm, gidx + 1)
                    if kb >= 2 and gidx == kb_start[kb]:
                        sc.wait_ge(s_stg, 32 * (kb - 1))
                    src = ps_M[gidx % 4]
                    if kind == "K":
                        sc.copy(KT_stage[kb % 2][:, argv * 512:(argv + 1) * 512],
                                src[:, :]).then_inc(s_kev, 1)
                    elif kind == "V":
                        kt, half = argv
                        dst = V_stage[kb % 2][:, :].rearrange(
                            "p (h c) -> p h c", h=HEADS)
                        sc.copy(
                            dst[:, half * 8:(half + 1) * 8, kt * 65:kt * 65 + 64],
                            src[:, :].rearrange("p (h c) -> p h c", h=8),
                        ).then_inc(s_kev, 1)
                    else:
                        sc.copy(QT_all[:, argv * NQ + kb * 512: argv * NQ + (kb + 1) * 512],
                                src[:, :]).then_inc(s_kev, 1)
                    gidx += 1
            sc.sem_inc(s_ph, 1)

        @block.tensor
        def _(te):
            te.wait_ge(s_x, 16)
            te.wait_ge(s_wc, 48)
            gidx = 0
            for kb in range(KB):
                for t in range(4):
                    T = 4 * kb + t
                    te.wait_ge(s_ln, 5 * kb + t + 2)
                    for c in range(CC):
                        gt = T * 8 + c
                        if gt >= 4:
                            te.wait_ge(s_xnb, gt - 3)
                        te.transpose(ps_T[gt % 4][:, :],
                                     xn_t[t][:, c * 128:(c + 1) * 128],
                                     ident[:, :]).then_inc(s_tp, 1)
                te.wait_ge(s_xnb, 32 * (kb + 1))
                xn = xnT[kb % 2][:, :].rearrange("p (cc c) -> p cc c", cc=CC)
                for _ in range(kb_end_count[kb] - kb_start[kb]):
                    kbx, kind, argv = kvq_sched[gidx]
                    if gidx >= 4:
                        te.wait_ge(s_kev, gidx - 3)
                    dstp = ps_M[gidx % 4]
                    for c in range(CC):
                        if kind == "K":
                            ins = te.matmul(dstp[:, :],
                                            WkT_sb[:, c * D + argv * 128: c * D + (argv + 1) * 128],
                                            xn[:, c, :], start=(c == 0), stop=(c == CC - 1))
                        elif kind == "V":
                            kt, half = argv
                            ins = te.matmul(dstp[:, :],
                                            xn[:, c, kt * 128:(kt + 1) * 128],
                                            WvT_sb[:, c * D + half * 512: c * D + (half + 1) * 512],
                                            start=(c == 0), stop=(c == CC - 1))
                        else:
                            ins = te.matmul(dstp[:, :],
                                            WqT_sb[:, c * D + argv * 128: c * D + (argv + 1) * 128],
                                            xn[:, c, :], start=(c == 0), stop=(c == CC - 1))
                        if c == CC - 1:
                            ins.then_inc(s_kmm, 1)
                    gidx += 1
            te.wait_ge(s_kev, len(kvq_sched))
            te.sem_inc(s_ph, 1)

        PH[0] += 5

        if dump == "A":
            _emit_dump(block, PH[0], s_ph, s_od, [
                (dumps["d_qt"].ap(), QT_all[:, :]),
                (dumps["d_kt"].ap(), KT_dram.ap()),
                (dumps["d_v"].ap(), V_dram.ap()),
            ])
            a.close()
            s_ab.close()
            return nc, dumps

        a.close()
        b = ExitStack()
        bsb = lambda n, sh, d: b.enter_context(nc.sbuf_tensor(n, sh, d))
        bps = lambda n, sh, d: b.enter_context(nc.psum_tensor(n, sh, d))
        KT_buf = [bsb(f"KT_b{i}", [128, LK], BF16) for i in range(2)]
        V_buf = [bsb(f"V_b{i}", [128, 2 * NKT * VSLOT], BF16) for i in range(2)]
        P_buf = bsb("P_b", [128, 4 * 1024], BF16)
        OT_stage = [bsb(f"OT_st{i}", [64, 2 * NQ], BF16) for i in range(2)]
        rinv = bsb("rinv", [128, 1024], F32)
        rb_sbA = bsb("rb_sbA", [64, 512], F32)
        rb_sbB = bsb("rb_sbB", [64, 512], F32)
        ps_S = [bps(f"ps_S{i}", [128, 1024], F32) for i in range(2)]
        ps_OA = bps("ps_OA", [VSLOT, 512], F32)
        ps_OB = bps("ps_OB", [VSLOT, 512], F32)
        ps_RA = bps("ps_RA", [64, 512], F32)
        ps_RB = bps("ps_RB", [64, 512], F32)
        NKTG = NPQ * NKT

        @block.gpsimd
        def _(gp):
            gp.wait_ge(s_ph, PH[0])

            def load(p):
                if p >= 2:
                    gp.wait_ge(s_rb, QB * (p - 1))
                gp.dma_start(out=KT_buf[p % 2][:, :],
                             in_=KT_dram.ap()[:, p, :]).then_inc(s_kvld, 16)
                gp.dma_start(
                    out=V_buf[p % 2][:, :].rearrange("p (h c) -> p h c", h=2),
                    in_=V_dram.ap()[:, 2 * p:2 * p + 2, :],
                ).then_inc(s_kvld, 16)

            def spill(p):
                gp.wait_ge(s_nrm, 2 * QB * (p + 1))
                gp.dma_start(out=OT_dram.ap()[2 * p, :, :],
                             in_=OT_stage[p % 2][:, 0:NQ]).then_inc(s_ot, 16)
                gp.dma_start(out=OT_dram.ap()[2 * p + 1, :, :],
                             in_=OT_stage[p % 2][:, NQ:2 * NQ]).then_inc(s_ot, 16)

            load(0)
            load(1)
            W1src = [W1.ap()[ei] for ei in range(8)] + [Ws1.ap()]
            W2src = [W2.ap()[ei] for ei in range(8)] + [Ws2.ap()]
            for p in range(2, PAIRS):
                load(p)
                spill(p - 2)
            for ei in range(NE):
                gp.dma_start(
                    out=Wf1_dram.ap()[ei].rearrange("p (cc f) -> p cc f", cc=CC),
                    in_=W1src[ei].rearrange("(cc p) f -> p cc f", p=128),
                ).then_inc(s_wc, 16)
                gp.dma_start(
                    out=Wf2_dram.ap()[ei].rearrange("p (fc d) -> p fc d", fc=FC),
                    in_=W2src[ei].rearrange("(fc p) d -> p fc d", p=128),
                ).then_inc(s_wc, 16)
            spill(PAIRS - 2)
            spill(PAIRS - 1)
            gp.wait_ge(s_ot, 32 * PAIRS)
            gp.wait_ge(s_wc, 48 + 18 * 16)
            gp.sem_inc(s_ph, 1)

        def _score(te, g, kt_b):
            p, rem = divmod(g, QB * NKT)
            q, kt = divmod(rem, NKT)
            te.matmul(ps_S[g % 2][:, 0:512],
                      kt_b[0:64, kt * 128:(kt + 1) * 128],
                      QT_all[0:64, p * NQ + q * 512: p * NQ + (q + 1) * 512],
                      start=True, stop=True, tile_position=(0, 0))
            te.matmul(ps_S[g % 2][:, 512:1024],
                      kt_b[64:128, kt * 128:(kt + 1) * 128],
                      QT_all[64:128, p * NQ + q * 512: p * NQ + (q + 1) * 512],
                      start=True, stop=True,
                      tile_position=(64, 0)).then_inc(s_sc, 1)

        def _av(te, g, v_b):
            p, rem = divmod(g, QB * NKT)
            q, kt = divmod(rem, NKT)
            pq = p * QB + q
            if kt == 0:
                te.wait_ge(s_nrm, 2 * pq)
            sl = P_buf[:, (g % 4) * 1024:(g % 4 + 1) * 1024]
            te.matmul(ps_OA[:, :], v_b[:, 0, kt, :], sl[:, 0:512],
                      start=(kt == 0), stop=(kt == NKT - 1))
            te.matmul(ps_OB[:, :], v_b[:, 1, kt, :], sl[:, 512:1024],
                      start=(kt == 0), stop=(kt == NKT - 1)).then_inc(s_av, 1)
            if kt == NKT - 1:
                te.wait_ge(s_rinv, RIA + pq + 1)
                te.matmul(ps_RA[:, :], ones64[64:65, :], rinv[64:65, 0:512],
                          start=True, stop=True)
                te.matmul(ps_RB[:, :], ones64[64:65, :], rinv[64:65, 512:1024],
                          start=True, stop=True).then_inc(s_rb, 1)

        @block.tensor
        def _(te):
            te.wait_ge(s_ph, PH[0])
            for p in range(PAIRS):
                te.wait_ge(s_kvld, 32 * (p + 1))
                kt_b = KT_buf[p % 2]
                v_b = V_buf[p % 2][:, :].rearrange(
                    "p (h kt c) -> p h kt c", h=2, kt=NKT)
                for gl in range(QB * NKT):
                    g = p * QB * NKT + gl
                    if g >= 2:
                        te.wait_ge(s_exp, g - 1)
                    _score(te, g, kt_b)
                    if gl > 0:
                        te.wait_ge(s_exp, g)
                        _av(te, g - 1, v_b)
                g = (p + 1) * QB * NKT
                te.wait_ge(s_exp, g)
                _av(te, g - 1, v_b)
            te.wait_ge(s_nrm, 2 * NPQ)
            te.sem_inc(s_ph, 1)

        @block.scalar
        def _(sc):
            sc.wait_ge(s_ph, PH[0])
            for g in range(NKTG):
                sc.wait_ge(s_sc, g + 1)
                if g >= 4:
                    sc.wait_ge(s_av, g - 3)
                sc.activation(P_buf[:, (g % 4) * 1024:(g % 4 + 1) * 1024],
                              ps_S[g % 2][:, :], AF.Exp,
                              bias=biasC[:, :], scale=0.125).then_inc(s_exp, 1)
            sc.sem_inc(s_ph, 1)

        @block.vector
        def _(v):
            v.wait_ge(s_ph, PH[0])
            for pq in range(NPQ):
                p, q = divmod(pq, QB)
                v.wait_ge(s_av, NKT * (pq + 1))
                if p >= 2 and q == 0:
                    v.wait_ge(s_ot, 32 * (p - 1))
                v.reciprocal(rinv[64:65, 0:512], ps_OA[64:65, :])
                v.reciprocal(rinv[64:65, 512:1024],
                             ps_OB[64:65, :]).then_inc(s_rinv, 1)
                v.wait_ge(s_rb, pq + 1)
                v.tensor_copy(rb_sbA[:, :], ps_RA[:, :])
                v.tensor_copy(rb_sbB[:, :], ps_RB[:, :]).then_inc(s_nrm, 1)
                v.tensor_tensor(OT_stage[p % 2][:, q * 512:(q + 1) * 512],
                                ps_OA[0:64, :], rb_sbA[:, :], ALU.mult)
                v.tensor_tensor(OT_stage[p % 2][:, NQ + q * 512: NQ + (q + 1) * 512],
                                ps_OB[0:64, :], rb_sbB[:, :],
                                ALU.mult).then_inc(s_nrm, 1)
            v.sem_inc(s_ph, 1)

        @block.sync
        def _(sync):
            sync.sem_inc(s_ph, 1)

        PH[0] += 5

        if dump == "B":
            _emit_dump(block, PH[0], s_ph, s_od, [
                (dumps["d_ot"].ap(), OT_dram.ap()),
            ])
            b.close()
            s_ab.close()
            return nc, dumps

        b.close()
        s_ab.close()
        c_x = ExitStack()
        x1 = c_x.enter_context(nc.sbuf_tensor("x1", [128, 16 * D], F32))
        c = ExitStack()
        OT_pack = c.enter_context(nc.sbuf_tensor("OT_pack", [128, PAIRS * NQ], BF16))
        WoT_sb = c.enter_context(nc.sbuf_tensor("WoT_sb", [128, PAIRS * D], BF16))
        bobc = c.enter_context(nc.sbuf_tensor("bobc", [128, D], F32))
        ps_OP = [c.enter_context(nc.psum_tensor(f"ps_OP{i}", [128, 512], F32))
                 for i in range(2)]
        NCLD = 19 * 16

        @block.sync
        def _(sync):
            sync.wait_ge(s_ph, PH[0])
            sync.dma_start(
                out=x1[:, :].rearrange("p (a d) -> p a d", a=16),
                in_=xb.ap().rearrange("(a p) d -> p a d", p=128)[:, 0:16, :],
            ).then_inc(s_cld, 16)
            sync.sem_inc(s_ph, 1)

        @block.gpsimd
        def _(gp):
            gp.wait_ge(s_ph, PH[0])
            gp.dma_start(out=bobc[:, :],
                         in_=bass.AP(bo, 0, [[0, 128], [1, D]])).then_inc(s_cld, 16)
            gp.dma_start(
                out=WoT_sb[:, :].rearrange("p (pr d) -> p pr d", pr=PAIRS),
                in_=WoT.ap().rearrange("(pr p) d -> p pr d", p=128),
            ).then_inc(s_cld, 16)
            for pr in range(PAIRS):
                gp.dma_start(out=OT_pack[0:64, pr * NQ:(pr + 1) * NQ],
                             in_=OT_dram.ap()[2 * pr, :, :]).then_inc(s_cld, 16)
                gp.dma_start(out=OT_pack[64:128, pr * NQ:(pr + 1) * NQ],
                             in_=OT_dram.ap()[2 * pr + 1, :, :]).then_inc(s_cld, 16)
            gp.sem_inc(s_ph, 1)

        @block.vector
        def _(v):
            v.wait_ge(s_ph, PH[0])
            v.wait_ge(s_cld, NCLD)
            for qt in range(16):
                ins = v.tensor_tensor(x1[:, qt * D:(qt + 1) * D],
                                      x1[:, qt * D:(qt + 1) * D], bobc[:, :], ALU.add)
            ins.then_inc(s_cev, 1)
            for blk in range(32):
                qt, dmb = divmod(blk, 2)
                v.wait_ge(s_cmm, blk + 1)
                sl = x1[:, qt * D + dmb * 512: qt * D + (dmb + 1) * 512]
                v.tensor_tensor(sl, ps_OP[blk % 2][:, :], sl,
                                ALU.add).then_inc(s_cev, 1)
            v.sem_inc(s_ph, 1)

        @block.tensor
        def _(te):
            te.wait_ge(s_ph, PH[0])
            te.wait_ge(s_cld, NCLD)
            for blk in range(32):
                qt, dmb = divmod(blk, 2)
                if blk >= 2:
                    te.wait_ge(s_cev, blk)
                for pr in range(PAIRS):
                    ins = te.matmul(ps_OP[blk % 2][:, :],
                                    OT_pack[:, pr * NQ + qt * 128: pr * NQ + (qt + 1) * 128],
                                    WoT_sb[:, pr * D + dmb * 512: pr * D + (dmb + 1) * 512],
                                    start=(pr == 0), stop=(pr == PAIRS - 1))
                    if pr == PAIRS - 1:
                        ins.then_inc(s_cmm, 1)
            te.wait_ge(s_cev, 33)
            te.sem_inc(s_ph, 1)

        @block.scalar
        def _(sc):
            sc.wait_ge(s_ph, PH[0])
            sc.sem_inc(s_ph, 1)

        PH[0] += 5

        if dump == "C":
            _emit_dump(block, PH[0], s_ph, s_od, [
                (dumps["d_x1"].ap(), x1[:, :]),
            ])
            c.close()
            c_x.close()
            return nc, dumps

        c.close()
        dk = ExitStack()
        xn2T = dk.enter_context(nc.sbuf_tensor("xn2T", [128, CC * NQ], BF16))
        probT = dk.enter_context(nc.sbuf_tensor("probT", [8, NQ], BF16))
        sel_bf = dk.enter_context(nc.sbuf_tensor("sel_bf", [8, 1024], BF16))
        dt_ = ExitStack()
        dsb = lambda n, s, ddt: dt_.enter_context(nc.sbuf_tensor(n, s, ddt))
        dps = lambda n, s, ddt: dt_.enter_context(nc.psum_tensor(n, s, ddt))
        xn2_t = [dsb(f"xn2_t{i}", [128, D], BF16) for i in range(2)]
        d2_s1 = dsb("d2_s1", [128, 4], F32)
        d2_s2 = dsb("d2_s2", [128, 4], F32)
        d2_m = dsb("d2_m", [128, 4], F32)
        d2_q = dsb("d2_q", [128, 4], F32)
        d2_lt = dsb("d2_lt", [128, 4], F32)
        d2_t = dsb("d2_t", [128, 4], F32)
        d2_v = dsb("d2_v", [128, 4], F32)
        d2_r = dsb("d2_r", [128, 4], F32)
        Wr_sb = dsb("Wr_sb", [128, CC * 8], BF16)
        logits = dsb("logits", [128, 128], F32)
        ismax = dsb("ismax", [128, 16 * 8], F32)
        maskd = dsb("maskd", [128, 16 * 8], F32)
        nmax = dsb("nmax", [128, 16], F32)
        m1c = dsb("m1c", [128, 16], F32)
        v2c = dsb("v2c", [128, 16], F32)
        dcol = dsb("dcol", [128, 16], F32)
        ecol = dsb("ecol", [128, 16], F32)
        e2col = dsb("e2col", [128, 16], F32)
        e3col = dsb("e3col", [128, 16], F32)
        gcol = dsb("gcol", [128, 16], F32)
        g2col = dsb("g2col", [128, 16], F32)
        g3col = dsb("g3col", [128, 16], F32)
        prob_all = dsb("prob_all", [128, 16 * 8], BF16)
        sel_i = dsb("sel_i", [8, 1024], I32)
        ps_T2 = [dps(f"ps_T2{i}", [128, 128], BF16) for i in range(4)]
        ps_LG = [dps(f"ps_LG{i}", [128, 8], F32) for i in range(2)]
        ps_PT = dps("ps_PT", [8, 128], BF16)

        x1r = x1[:, :].rearrange("p (qt d) -> p qt d", qt=16)

        @block.gpsimd
        def _(gp):
            gp.wait_ge(s_ph, PH[0])
            gp.dma_start(
                out=Wr_sb[:, :].rearrange("p (cc e) -> p cc e", cc=CC),
                in_=Wr.ap().rearrange("(cc p) e -> p cc e", p=128),
            ).then_inc(s_wc, 16)
            gp.iota(sel_i[:, :], [[1, 8], [0, 128]], channel_multiplier=-1,
                    allow_small_or_imprecise_dtypes=True).then_inc(s_wc, 1)
            gp.sem_inc(s_ph, 1)

        @block.sync
        def _(sync):
            sync.wait_ge(s_ph, PH[0])
            sync.sem_inc(s_ph, 1)

        @block.vector
        def _(v):
            v.wait_ge(s_ph, PH[0])
            v.wait_ge(s_wc, 48 + 288 + 17)
            v.tensor_scalar(sel_bf[:, :], sel_i[:, :], 0.0, None, ALU.is_equal)
            for g4 in range(4):
                for t in range(4):
                    xt = x1r[:, g4 * 4 + t, :]
                    sq = scrq[:, t * D:(t + 1) * D]
                    v.reduce_sum(d2_s1[:, t:t + 1], xt, axis=AX.X)
                    v.tensor_tensor(sq, xt, xt, ALU.mult)
                    v.reduce_sum(d2_s2[:, t:t + 1], sq, axis=AX.X)
                v.tensor_scalar(d2_m[:, :], d2_s1[:, :], 1.0 / D, None, ALU.mult)
                v.tensor_scalar(d2_q[:, :], d2_s2[:, :], 1.0 / D, None, ALU.mult)
                v.tensor_copy(junk0[:, 0:4], d2_m[:, :])
                v.tensor_copy(junk0[:, 4:8], d2_m[:, :])
                v.tensor_tensor(d2_t[:, :], d2_m[:, :], d2_m[:, :], ALU.mult)
                v.tensor_copy(junk0[:, 0:4], d2_t[:, :])
                v.tensor_copy(junk0[:, 4:8], d2_t[:, :])
                v.tensor_tensor(d2_v[:, :], d2_q[:, :], d2_t[:, :],
                                ALU.subtract).then_inc(s_ln, 1)
                v.wait_ge(s_rinv, RIB + g4 + 1)
                for t in range(4):
                    ti = g4 * 4 + t
                    if ti >= 2:
                        v.wait_ge(s_tp, TPA + 8 * (ti - 1))
                    v.tensor_scalar(xn2_t[ti % 2][:, :], x1r[:, ti, :],
                                    d2_m[:, t:t + 1], d2_r[:, t:t + 1],
                                    ALU.subtract, ALU.mult).then_inc(s_ln, 1)
                    dst = xn2T[:, :].rearrange("p (cc c) -> p cc c", cc=CC)
                    for cc in range(CC):
                        gt = ti * 8 + cc
                        v.wait_ge(s_tp, TPA + gt + 1)
                        v.tensor_copy(dst[:, cc, ti * 128:(ti + 1) * 128],
                                      ps_T2[gt % 4][:, :]).then_inc(s_xnb, 1)
            for qt in range(16):
                v.wait_ge(s_lg, qt + 1)
                v.tensor_copy(logits[:, qt * 8:(qt + 1) * 8],
                              ps_LG[qt % 2][:, :]).then_inc(s_lge, 1)
            for qt in range(16):
                v.tensor_reduce(m1c[:, qt:qt + 1],
                                logits[:, qt * 8:(qt + 1) * 8], AX.X, ALU.max)
            for qt in range(16):
                v.tensor_scalar(ismax[:, qt * 8:(qt + 1) * 8],
                                logits[:, qt * 8:(qt + 1) * 8],
                                m1c[:, qt:qt + 1], None, ALU.is_equal)
            for qt in range(16):
                v.tensor_reduce(nmax[:, qt:qt + 1],
                                ismax[:, qt * 8:(qt + 1) * 8], AX.X, ALU.add)
            for qt in range(16):
                v.tensor_scalar(maskd[:, qt * 8:(qt + 1) * 8],
                                ismax[:, qt * 8:(qt + 1) * 8], -1e30, None,
                                ALU.mult)
            for qt in range(16):
                msl = maskd[:, qt * 8:(qt + 1) * 8]
                v.tensor_tensor(msl, msl, logits[:, qt * 8:(qt + 1) * 8], ALU.add)
            for qt in range(16):
                v.tensor_reduce(v2c[:, qt:qt + 1],
                                maskd[:, qt * 8:(qt + 1) * 8], AX.X, ALU.max)
            v.tensor_tensor(dcol[:, :], v2c[:, :], m1c[:, :],
                            ALU.subtract).then_inc(s_rt, 1)
            v.wait_ge(s_rt, 2)
            v.tensor_scalar(e2col[:, :], ecol[:, :], 1.0, None, ALU.add)
            v.tensor_scalar(g2col[:, :], nmax[:, :], 1.5, None, ALU.is_le)
            v.tensor_copy(junk0[:, 0:8], e2col[:, 0:8])
            v.reciprocal(e3col[:, :], e2col[:, :])
            v.tensor_copy(junk0[:, 0:8], e3col[:, 0:8])
            v.tensor_copy(junk0[:, 0:8], e3col[:, 8:16])
            v.tensor_scalar(gcol[:, :], e3col[:, :], 1.0, None, ALU.is_ge)
            v.tensor_copy(junk0[:, 0:8], gcol[:, 0:8])
            v.tensor_copy(junk0[:, 0:8], gcol[:, 8:16])
            v.tensor_tensor(g3col[:, :], gcol[:, :], g2col[:, :], ALU.mult)
            v.tensor_copy(junk0[:, 0:8], g3col[:, 0:8])
            v.tensor_copy(junk0[:, 0:8], g3col[:, 8:16])
            for qt in range(16):
                ins = v.tensor_scalar(prob_all[:, qt * 8:(qt + 1) * 8],
                                      ismax[:, qt * 8:(qt + 1) * 8],
                                      g3col[:, qt:qt + 1], None, ALU.mult)
            ins.then_inc(s_rt, 1)
            for qt in range(16):
                v.wait_ge(s_lg, 16 + qt + 1)
                v.tensor_copy(probT[:, qt * 128:(qt + 1) * 128],
                              ps_PT[:, :]).then_inc(s_xnb, 1)
            v.sem_inc(s_ph, 1)

        @block.scalar
        def _(sc):
            sc.wait_ge(s_ph, PH[0])
            for g4 in range(4):
                sc.wait_ge(s_ln, LNA + 5 * g4 + 1)
                sc.activation(d2_lt[:, :], d2_v[:, :], AF.Ln, bias=epsc[:, :], scale=1.0)
                sc.activation(d2_r[:, :], d2_lt[:, :], AF.Exp, bias=0.0,
                              scale=-0.5).then_inc(s_rinv, 1)
            sc.wait_ge(s_rt, 1)
            sc.activation(ecol[:, :], dcol[:, :], AF.Exp, bias=0.0,
                          scale=1.0).then_inc(s_rt, 1)
            sc.sem_inc(s_ph, 1)

        @block.tensor
        def _(te):
            te.wait_ge(s_ph, PH[0])
            for ti in range(16):
                te.wait_ge(s_ln, LNA + 5 * (ti // 4) + (ti % 4) + 2)
                for cc in range(CC):
                    gt = ti * 8 + cc
                    if gt >= 4:
                        te.wait_ge(s_xnb, XNA + gt - 3)
                    te.transpose(ps_T2[gt % 4][:, :],
                                 xn2_t[ti % 2][:, cc * 128:(cc + 1) * 128],
                                 ident[:, :]).then_inc(s_tp, 1)
            te.wait_ge(s_xnb, XNA + 128)
            te.wait_ge(s_wc, 48 + 288 + 17)
            xn2 = xn2T[:, :].rearrange("p (cc c) -> p cc c", cc=CC)
            for qt in range(16):
                if qt >= 2:
                    te.wait_ge(s_lge, qt - 1)
                for cc in range(CC):
                    ins = te.matmul(ps_LG[qt % 2][:, :],
                                    xn2[:, cc, qt * 128:(qt + 1) * 128],
                                    Wr_sb[:, cc * 8:(cc + 1) * 8],
                                    start=(cc == 0), stop=(cc == CC - 1))
                    if cc == CC - 1:
                        ins.then_inc(s_lg, 1)
            te.wait_ge(s_rt, 3)
            for qt in range(16):
                if qt >= 1:
                    te.wait_ge(s_xnb, XNA + 128 + qt)
                te.transpose(ps_PT[:, :], prob_all[:, qt * 8:(qt + 1) * 8],
                             ident[:, :]).then_inc(s_lg, 1)
            te.wait_ge(s_xnb, XNA + 128 + 16)
            te.sem_inc(s_ph, 1)

        PH[0] += 5

        if dump == "D":
            _emit_dump(block, PH[0], s_ph, s_od, [
                (dumps["d_xn2"].ap(), xn2T[:, :]),
                (dumps["d_prob"].ap(), prob_all[:, :]),
                (dumps["d_probT"].ap(), probT[:, :]),
            ])
            dt_.close()
            dk.close()
            c_x.close()
            return nc, dumps

        dt_.close()
        e = ExitStack()
        esb = lambda n, sh, ddt: e.enter_context(nc.sbuf_tensor(n, sh, ddt))
        TW = NQ // 2
        W1_sb = [esb(f"W1_sb{i}", [128, CC * 512], BF16) for i in range(2)]
        W2_sb = [esb(f"W2_sb{i}", [128, FC * D], BF16) for i in range(2)]
        H_sb = [esb(f"H_sb{i}", [128, FC * TW], BF16) for i in range(2)]
        pb_sb = esb("pb_sb", [128, 8 * TW], BF16)
        r_f = [esb(f"r_f{i}", [128, 512], F32) for i in range(4)]
        r2_f = [esb(f"r2_f{i}", [128, 512], F32) for i in range(2)]
        ps_H = [e.enter_context(nc.psum_tensor(f"ps_H{i}", [128, 512], F32))
                for i in range(2)]
        ps_O = [e.enter_context(nc.psum_tensor(f"ps_O{i}", [128, 512], F32))
                for i in range(2)]
        ps_PB = e.enter_context(nc.psum_tensor("ps_PB", [128, 512], F32))

        out_r = out_d.ap().rearrange("(tt p) d -> p tt d", p=128)
        NBLK = 16

        @block.sync
        def _(sync):
            sync.wait_ge(s_ph, PH[0])
            for tq in range(2):
                for ei in range(NE):
                    gx = tq * NE + ei
                    if gx >= 2:
                        sync.wait_ge(s_hmm, (gx - 1) * 8)
                    sync.dma_start(out=W1_sb[gx % 2][:, :],
                                   in_=Wf1_dram.ap()[ei]).then_inc(s_wld, 16)

        @block.gpsimd
        def _(gp):
            gp.wait_ge(s_ph, PH[0])
            for tq in range(2):
                for ei in range(NE):
                    gx = tq * NE + ei
                    if gx >= 2:
                        gp.wait_ge(s_eomm, (gx - 1) * NBLK)
                    gp.dma_start(out=W2_sb[gx % 2][:, :],
                                 in_=Wf2_dram.ap()[ei]).then_inc(s_w2l, 16)
                gp.wait_ge(s_eoadd, NBLK * NE * (tq + 1))
                for tb in range(8):
                    tt = tq * 8 + tb
                    gp.dma_start(out=out_r[:, tt, :],
                                 in_=x1[:, tt * D:(tt + 1) * D]).then_inc(s_od, 16)
            gp.wait_ge(s_od, 16 * 16)

        @block.tensor
        def _(te):
            te.wait_ge(s_ph, PH[0])
            xn2 = xn2T[:, :].rearrange("p (cc c) -> p cc c", cc=CC)
            for tq in range(2):
                for j in range(4):
                    pass
                for ei in range(8):
                    for half in range(2):
                        pj = tq * 16 + ei * 2 + half
                        te.wait_ge(s_pbe, pj)
                        te.matmul(ps_PB[:, :], sel_bf[:, ei * 128:(ei + 1) * 128],
                                  probT[:, tq * TW + half * 512: tq * TW + (half + 1) * 512],
                                  start=True, stop=True).then_inc(s_pb, 1)
                for ei in range(NE):
                    gx = tq * NE + ei
                    te.wait_ge(s_wld, 16 * (gx + 1))
                    if gx >= 2:
                        te.wait_ge(s_eoadd, (gx - 1) * NBLK)
                    for fc in range(FC):
                        for half in range(2):
                            hg = gx * 8 + fc * 2 + half
                            if hg >= 2:
                                te.wait_ge(s_hact, hg - 1)
                            for cc in range(CC):
                                ins = te.matmul(
                                    ps_H[hg % 2][:, :],
                                    W1_sb[gx % 2][:, cc * 512 + fc * 128:
                                                  cc * 512 + (fc + 1) * 128],
                                    xn2[:, cc, tq * TW + half * 512:
                                        tq * TW + (half + 1) * 512],
                                    start=(cc == 0), stop=(cc == CC - 1))
                            ins.then_inc(s_hmm, 1)
                    te.wait_ge(s_w2l, 16 * (gx + 1))
                    te.wait_ge(s_he, gx * 8 + 8)
                    for blk in range(NBLK):
                        og = gx * NBLK + blk
                        tb, dmb = divmod(blk, 2)
                        if og >= 2:
                            te.wait_ge(s_eoadd, og - 1)
                        for fc in range(FC):
                            ins = te.matmul(
                                ps_O[og % 2][:, :],
                                H_sb[gx % 2][:, fc * TW + tb * 128:
                                             fc * TW + (tb + 1) * 128],
                                W2_sb[gx % 2][:, fc * D + dmb * 512:
                                              fc * D + (dmb + 1) * 512],
                                start=(fc == 0), stop=(fc == FC - 1))
                        ins.then_inc(s_eomm, 1)

        @block.scalar
        def _(sc):
            sc.wait_ge(s_ph, PH[0])
            for hg in range(2 * NE * 8):
                sc.wait_ge(s_hmm, hg + 1)
                if hg >= 2:
                    sc.wait_ge(s_he, hg - 1)
                sc.activation(r_f[hg % 4][:, :], ps_H[hg % 2][:, :],
                              AF.Relu, bias=0.0, scale=1.0)
                sc.activation(r2_f[hg % 2][:, :], r_f[hg % 4][:, :],
                              AF.Square, bias=0.0, scale=1.0).then_inc(s_hact, 1)

        @block.vector
        def _(v):
            v.wait_ge(s_ph, PH[0])
            for tq in range(2):
                for ei in range(8):
                    for half in range(2):
                        pj = tq * 16 + ei * 2 + half
                        v.wait_ge(s_pb, pj + 1)
                        v.tensor_copy(pb_sb[:, ei * TW + half * 512:
                                            ei * TW + (half + 1) * 512],
                                      ps_PB[:, :]).then_inc(s_pbe, 1)
                for ei in range(NE):
                    gx = tq * NE + ei
                    for fc in range(FC):
                        for half in range(2):
                            hg = gx * 8 + fc * 2 + half
                            v.wait_ge(s_hact, hg + 1)
                            if gx >= 2 and fc == 0 and half == 0:
                                v.wait_ge(s_eomm, (gx - 2) * NBLK + NBLK)
                            pbs = (onesbc[:, :] if ei == 8 else
                                   pb_sb[:, ei * TW + half * 512:
                                         ei * TW + (half + 1) * 512])
                            v.tensor_tensor(
                                H_sb[gx % 2][:, fc * TW + half * 512:
                                             fc * TW + (half + 1) * 512],
                                r2_f[hg % 2][:, :], pbs,
                                ALU.mult).then_inc(s_he, 1)
                    for blk in range(NBLK):
                        og = gx * NBLK + blk
                        tb, dmb = divmod(blk, 2)
                        v.wait_ge(s_eomm, og + 1)
                        tt = tq * 8 + tb
                        sl = x1[:, tt * D + dmb * 512: tt * D + (dmb + 1) * 512]
                        v.tensor_tensor(sl, ps_O[og % 2][:, :], sl,
                                        ALU.add).then_inc(s_eoadd, 1)

        e.close()
        dk.close()
        c_x.close()
    return nc, dumps


def _emit_dump(block, ph_val, s_ph, s_od, pairs):
    @block.gpsimd
    def _(gp):
        gp.wait_ge(s_ph, ph_val)
        n = 0
        for dst, src in pairs:
            gp.dma_start(out=dst, in_=src).then_inc(s_od, 16)
            n += 16
        gp.wait_ge(s_od, n)


_PROG_CACHE = {}


def _get_program(dump=None):
    if dump not in _PROG_CACHE:
        _PROG_CACHE[dump] = build_program(dump)
    return _PROG_CACHE[dump]


def make_in_maps(x, WqT, WkT, WvT, WoT, bo, Ws1, Ws2, W1, W2, Wr):
    in_maps = []
    for core in range(8):
        bidx, qidx = divmod(core, 4)
        xb_c = np.ascontiguousarray(np.roll(x[bidx], -qidx * NQ, axis=0))
        in_maps.append({
            "xb": xb_c, "WqT": WqT, "WkT": WkT, "WvT": WvT, "WoT": WoT,
            "bo": bo, "Ws1": Ws1, "Ws2": Ws2, "W1": W1, "W2": W2, "Wr": Wr,
        })
    return in_maps


def prep_inputs(L0_attn_x, L0_attn_W_q, L0_attn_W_k, L0_attn_W_v, L0_attn_W_o,
                L0_attn_b_o, L0_ffn_Ws1, L0_ffn_Ws2, L0_ffn_W1, L0_ffn_W2,
                L0_ffn_Wr):
    f = lambda v: np.ascontiguousarray(np.asarray(v, dtype=np.float32))
    ft = lambda v: np.ascontiguousarray(np.asarray(v, dtype=np.float32).T)
    return (f(L0_attn_x), ft(L0_attn_W_q), ft(L0_attn_W_k), ft(L0_attn_W_v),
            ft(L0_attn_W_o), f(L0_attn_b_o), f(L0_ffn_Ws1), f(L0_ffn_Ws2),
            f(L0_ffn_W1), f(L0_ffn_W2), f(L0_ffn_Wr))


def kernel(**inputs):
    arrs = prep_inputs(**inputs)
    nc, _ = _get_program()
    in_maps = make_in_maps(*arrs)
    res = run_bass_kernel_spmd(nc, in_maps, core_ids=list(range(8)))
    out = np.empty((2, LK, D), dtype=np.float32)
    for core in range(8):
        bidx, qidx = divmod(core, 4)
        out[bidx, qidx * NQ:(qidx + 1) * NQ] = np.asarray(
            res.results[core]["out"], dtype=np.float32)
    return out


# version 04
# speedup vs baseline: 1.7851x; 1.0519x over previous
import numpy as np
import ml_dtypes
from contextlib import ExitStack

import concourse.bass as bass
import concourse.mybir as mybir
from concourse.bass_utils import run_bass_kernel_spmd

F32 = mybir.dt.float32
BF16 = mybir.dt.bfloat16
I32 = mybir.dt.int32
AF = mybir.ActivationFunctionType
ALU = mybir.AluOpType
AX = mybir.AxisListType

D = 1024
LK = 8192
NQ = 2048
CC = 8
KB = 16
QKB = 4
PAIRS = 8
HEADS = 16
QB = 4
NKT = 64
GRP = 2
CH = 8
GPC = CH // GRP
NCH = NKT // CH
NPQ = PAIRS * QB
NGRP = NPQ * NKT // GRP
CSH = 10.0
EPS = 1e-5
FC = 4
NE = 9
TQ = 4
VSLOT = 65

LNA = 5 * KB
TPA = 8 * 4 * KB
XNA = 8 * 4 * KB
RIA = KB
RIB = RIA + NPQ


def build_program(dump=None):
    nc = bass.Bass("TRN2", target_bir_lowering=False,
                   detect_race_conditions=False)

    xb = nc.dram_tensor("xb", [LK, D], F32, kind="ExternalInput")
    WqT = nc.dram_tensor("WqT", [D, D], F32, kind="ExternalInput")
    WkT = nc.dram_tensor("WkT", [D, D], F32, kind="ExternalInput")
    WvT = nc.dram_tensor("WvT", [D, D], F32, kind="ExternalInput")
    WoT = nc.dram_tensor("WoT", [D, D], F32, kind="ExternalInput")
    bo = nc.dram_tensor("bo", [D], F32, kind="ExternalInput")
    Ws1 = nc.dram_tensor("Ws1", [D, 512], F32, kind="ExternalInput")
    Ws2 = nc.dram_tensor("Ws2", [512, D], F32, kind="ExternalInput")
    W1 = nc.dram_tensor("W1", [8, D, 512], F32, kind="ExternalInput")
    W2 = nc.dram_tensor("W2", [8, 512, D], F32, kind="ExternalInput")
    Wr = nc.dram_tensor("Wr", [D, 8], F32, kind="ExternalInput")
    out_d = nc.dram_tensor("out", [NQ, D], F32, kind="ExternalOutput")

    KT_dram = nc.dram_tensor("KT_scr", [128, PAIRS, LK], BF16)
    V_dram = nc.dram_tensor("V_scr", [128, HEADS, NKT * VSLOT], BF16)
    OT_dram = nc.dram_tensor("OT_scr", [HEADS, 64, NQ], BF16)
    Wf1_dram = nc.dram_tensor("Wf1_scr", [NE, 128, CC * 512], BF16)
    Wf2_dram = nc.dram_tensor("Wf2_scr", [NE, 128, FC * D], BF16)

    ident_dram = nc.inline_tensor(np.eye(128, dtype=ml_dtypes.bfloat16), "ident_c")

    dumps = {}
    if dump == "A":
        dumps["d_qt"] = nc.dram_tensor("d_qt", [128, PAIRS * NQ], BF16, kind="ExternalOutput")
        dumps["d_kt"] = nc.dram_tensor("d_kt", [128, PAIRS, LK], BF16, kind="ExternalOutput")
        dumps["d_v"] = nc.dram_tensor("d_v", [128, HEADS, NKT * VSLOT], BF16, kind="ExternalOutput")
    if dump == "B":
        dumps["d_ot"] = nc.dram_tensor("d_ot", [HEADS, 64, NQ], BF16, kind="ExternalOutput")
    if dump == "C":
        dumps["d_x1"] = nc.dram_tensor("d_x1", [128, 16 * D], F32, kind="ExternalOutput")
    if dump == "D":
        dumps["d_xn2"] = nc.dram_tensor("d_xn2", [128, CC * NQ], BF16, kind="ExternalOutput")
        dumps["d_prob"] = nc.dram_tensor("d_prob", [128, 16 * 8], BF16, kind="ExternalOutput")
        dumps["d_probT"] = nc.dram_tensor("d_probT", [8, NQ], BF16, kind="ExternalOutput")

    g = ExitStack()
    with g:
        sem = lambda n: g.enter_context(nc.semaphore(n))

        ident = g.enter_context(nc.sbuf_tensor("ident", [128, 128], BF16))
        biasC = g.enter_context(nc.sbuf_tensor("biasC", [128, 1], F32))
        epsc = g.enter_context(nc.sbuf_tensor("epsc", [128, 1], F32))
        ones64 = g.enter_context(nc.sbuf_tensor("ones64", [128, 64], F32))
        onesbc = g.enter_context(nc.sbuf_tensor("onesbc", [128, 512], BF16))
        scrq = g.enter_context(nc.sbuf_tensor("scrq", [128, 4 * D], F32))
        junk0 = g.enter_context(nc.sbuf_tensor("junk0", [128, 8], F32))

        s_ph = sem("s_ph")
        s_x = sem("s_x")
        s_wc = sem("s_wc")
        s_ln = sem("s_ln")
        s_tp = sem("s_tp")
        s_xnb = sem("s_xnb")
        s_kmm = sem("s_kmm")
        s_kev = sem("s_kev")
        s_stg = sem("s_stg")
        s_kvld = sem("s_kvld")
        s_sc = sem("s_sc")
        s_exp = sem("s_exp")
        s_av = sem("s_av")
        s_rinv = sem("s_rinv")
        s_rb = sem("s_rb")
        s_nrm = sem("s_nrm")
        s_nst = sem("s_nst")
        s_ot = sem("s_ot")
        s_cld = sem("s_cld")
        s_cmm = sem("s_cmm")
        s_cev = sem("s_cev")
        s_lg = sem("s_lg")
        s_lge = sem("s_lge")
        s_rt = sem("s_rt")
        s_wld = sem("s_wld")
        s_w2l = sem("s_w2l")
        s_pb = sem("s_pb")
        s_pbe = sem("s_pbe")
        s_hmm = sem("s_hmm")
        s_hact = sem("s_hact")
        s_he = sem("s_he")
        s_eomm = sem("s_eomm")
        s_eoadd = sem("s_eoadd")
        s_oev = sem("s_oev")
        s_od = sem("s_od")

        PH = [0]
        block = g.enter_context(nc.Block())

        s_ab = ExitStack()
        QT_all = s_ab.enter_context(nc.sbuf_tensor("QT_all", [128, PAIRS * NQ], BF16))

        a = ExitStack()
        asb = lambda n, s, d: a.enter_context(nc.sbuf_tensor(n, s, d))
        aps = lambda n, s, d: a.enter_context(nc.psum_tensor(n, s, d))
        WqT_sb = asb("WqT_sb", [128, CC * D], BF16)
        WkT_sb = asb("WkT_sb", [128, CC * D], BF16)
        WvT_sb = asb("WvT_sb", [128, CC * D], BF16)
        xblk = [asb(f"xblk{i}", [128, 4 * D], F32) for i in range(2)]
        xn_t = [asb(f"xn_t{i}", [128, D], BF16) for i in range(4)]
        st_s1 = asb("st_s1", [128, 4], F32)
        st_s2 = asb("st_s2", [128, 4], F32)
        st_m = asb("st_m", [128, 4], F32)
        st_q = asb("st_q", [128, 4], F32)
        st_lt = asb("st_lt", [128, 4], F32)
        st_t = asb("st_t", [128, 4], F32)
        st_v = asb("st_v", [128, 4], F32)
        st_r = asb("st_r", [128, 4], F32)
        xnT = [asb(f"xnT{i}", [128, CC * 512], BF16) for i in range(2)]
        KT_stage = [asb(f"KT_st{i}", [128, PAIRS * 512], BF16) for i in range(2)]
        V_stage = [asb(f"V_st{i}", [128, HEADS * 260], BF16) for i in range(2)]
        ps_T = [aps(f"ps_T{i}", [128, 128], BF16) for i in range(4)]
        ps_M = [aps(f"ps_M{i}", [128, 512], F32) for i in range(4)]

        kvq_sched = []
        for kb in range(KB):
            for pr in range(PAIRS):
                kvq_sched.append((kb, "K", pr))
            for kt in range(4):
                for half in range(2):
                    kvq_sched.append((kb, "V", (kt, half)))
            if kb < QKB:
                for pr in range(PAIRS):
                    kvq_sched.append((kb, "Q", pr))
        kb_end_count = {}
        for i, (kb, kind, argv) in enumerate(kvq_sched):
            kb_end_count[kb] = i + 1
        kb_start = {kb: (kb_end_count[kb - 1] if kb else 0) for kb in range(KB)}

        @block.sync
        def _(sync):
            sync.dma_start(out=ident[:, :], in_=ident_dram.ap()).then_inc(s_x, 16)
            for kb in range(KB):
                if kb >= 2:
                    sync.wait_ge(s_ln, 5 * (kb - 1))
                sync.dma_start(
                    out=xblk[kb % 2][:, :].rearrange("p (s d) -> p s d", s=4),
                    in_=xb.ap().rearrange("(kb s p) d -> kb p s d", p=128, s=4)[kb],
                ).then_inc(s_x, 16)
            sync.sem_inc(s_ph, 1)

        @block.gpsimd
        def _(gp):
            for W, Wsb in ((WqT, WqT_sb), (WkT, WkT_sb), (WvT, WvT_sb)):
                gp.dma_start(
                    out=Wsb[:, :].rearrange("p (cc d) -> p cc d", cc=CC),
                    in_=W.ap().rearrange("(cc p) d -> p cc d", p=128),
                ).then_inc(s_wc, 16)
            for i in range(2):
                vst = V_stage[i][:, :].rearrange("p (h kt c) -> p h kt c", h=HEADS, kt=4)
                gp.memset(vst[:, :, :, 64:65], 1.0)
            for kb in range(KB):
                gp.wait_ge(s_kev, kb_end_count[kb])
                gp.dma_start(
                    out=KT_dram.ap()[:, :, kb * 512:(kb + 1) * 512],
                    in_=KT_stage[kb % 2][:, :].rearrange("p (pr c) -> p pr c", pr=PAIRS),
                ).then_inc(s_stg, 16)
                gp.dma_start(
                    out=V_dram.ap()[:, :, kb * 260:(kb + 1) * 260],
                    in_=V_stage[kb % 2][:, :].rearrange("p (h c) -> p h c", h=HEADS),
                ).then_inc(s_stg, 16)
            gp.wait_ge(s_stg, 32 * KB)
            gp.sem_inc(s_ph, 1)

        @block.vector
        def _(v):
            v.memset(biasC[:, :], -CSH)
            v.memset(epsc[:, :], EPS)
            v.memset(ones64[64:65, :], 1.0)
            v.memset(onesbc[:, :], 1.0)
            v.memset(scrq[:, :], 1.0)
            v.memset(junk0[:, :], 1.0)
            for _w in range(3):
                v.reduce_sum(junk0[:, 0:1], scrq[:, 0:D], axis=AX.X)
                v.tensor_tensor(scrq[:, 0:D], scrq[:, 0:D], scrq[:, 0:D], ALU.mult)
                v.tensor_tensor(junk0[:, 4:8], junk0[:, 0:4], junk0[:, 0:4],
                                ALU.subtract)
                v.tensor_scalar(junk0[:, 0:4], junk0[:, 4:8], 0.5, None, ALU.mult)
                v.tensor_copy(junk0[:, 4:8], junk0[:, 0:4])
                v.reciprocal(junk0[:, 0:4], scrq[:, 0:4])
            for kb in range(KB):
                v.wait_ge(s_x, 16 * (kb + 2))
                for t in range(4):
                    xt = xblk[kb % 2][:, t * D:(t + 1) * D]
                    sq = scrq[:, t * D:(t + 1) * D]
                    v.reduce_sum(st_s1[:, t:t + 1], xt, axis=AX.X)
                    v.tensor_tensor(sq, xt, xt, ALU.mult)
                    v.reduce_sum(st_s2[:, t:t + 1], sq, axis=AX.X)
                v.tensor_scalar(st_m[:, :], st_s1[:, :], 1.0 / D, None, ALU.mult)
                v.tensor_scalar(st_q[:, :], st_s2[:, :], 1.0 / D, None, ALU.mult)
                v.tensor_copy(junk0[:, 0:4], st_m[:, :])
                v.tensor_copy(junk0[:, 4:8], st_m[:, :])
                v.tensor_tensor(st_t[:, :], st_m[:, :], st_m[:, :], ALU.mult)
                v.tensor_copy(junk0[:, 0:4], st_t[:, :])
                v.tensor_copy(junk0[:, 4:8], st_t[:, :])
                v.tensor_tensor(st_v[:, :], st_q[:, :], st_t[:, :],
                                ALU.subtract).then_inc(s_ln, 1)
                v.wait_ge(s_rinv, kb + 1)
                for t in range(4):
                    T = 4 * kb + t
                    if T >= 4:
                        v.wait_ge(s_tp, 8 * (T - 3))
                    xt = xblk[kb % 2][:, t * D:(t + 1) * D]
                    v.tensor_scalar(xn_t[t][:, :], xt, st_m[:, t:t + 1],
                                    st_r[:, t:t + 1], ALU.subtract,
                                    ALU.mult).then_inc(s_ln, 1)
                for t in range(4):
                    T = 4 * kb + t
                    if kb >= 2 and t == 0:
                        v.wait_ge(s_kmm, kb_end_count[kb - 2])
                    dst = xnT[kb % 2][:, :].rearrange("p (cc c) -> p cc c", cc=CC)
                    for c in range(CC):
                        gt = T * 8 + c
                        v.wait_ge(s_tp, gt + 1)
                        v.tensor_copy(dst[:, c, t * 128:(t + 1) * 128],
                                      ps_T[gt % 4][:, :]).then_inc(s_xnb, 1)
            v.sem_inc(s_ph, 1)

        @block.scalar
        def _(sc):
            gidx = 0
            for kb in range(KB):
                sc.wait_ge(s_ln, 5 * kb + 1)
                sc.activation(st_lt[:, :], st_v[:, :], AF.Ln, bias=epsc[:, :], scale=1.0)
                sc.activation(st_r[:, :], st_lt[:, :], AF.Exp, bias=0.0,
                              scale=-0.5).then_inc(s_rinv, 1)
                for _ in range(kb_end_count[kb] - kb_start[kb]):
                    kbx, kind, argv = kvq_sched[gidx]
                    sc.wait_ge(s_kmm, gidx + 1)
                    if kb >= 2 and gidx == kb_start[kb]:
                        sc.wait_ge(s_stg, 32 * (kb - 1))
                    src = ps_M[gidx % 4]
                    if kind == "K":
                        sc.copy(KT_stage[kb % 2][:, argv * 512:(argv + 1) * 512],
                                src[:, :]).then_inc(s_kev, 1)
                    elif kind == "V":
                        kt, half = argv
                        dst = V_stage[kb % 2][:, :].rearrange(
                            "p (h c) -> p h c", h=HEADS)
                        sc.copy(
                            dst[:, half * 8:(half + 1) * 8, kt * 65:kt * 65 + 64],
                            src[:, :].rearrange("p (h c) -> p h c", h=8),
                        ).then_inc(s_kev, 1)
                    else:
                        sc.copy(QT_all[:, argv * NQ + kb * 512: argv * NQ + (kb + 1) * 512],
                                src[:, :]).then_inc(s_kev, 1)
                    gidx += 1
            sc.sem_inc(s_ph, 1)

        @block.tensor
        def _(te):
            te.wait_ge(s_x, 16)
            te.wait_ge(s_wc, 48)
            gidx = 0
            for kb in range(KB):
                for t in range(4):
                    T = 4 * kb + t
                    te.wait_ge(s_ln, 5 * kb + t + 2)
                    for c in range(CC):
                        gt = T * 8 + c
                        if gt >= 4:
                            te.wait_ge(s_xnb, gt - 3)
                        te.transpose(ps_T[gt % 4][:, :],
                                     xn_t[t][:, c * 128:(c + 1) * 128],
                                     ident[:, :]).then_inc(s_tp, 1)
                te.wait_ge(s_xnb, 32 * (kb + 1))
                xn = xnT[kb % 2][:, :].rearrange("p (cc c) -> p cc c", cc=CC)
                for _ in range(kb_end_count[kb] - kb_start[kb]):
                    kbx, kind, argv = kvq_sched[gidx]
                    if gidx >= 4:
                        te.wait_ge(s_kev, gidx - 3)
                    dstp = ps_M[gidx % 4]
                    for c in range(CC):
                        if kind == "K":
                            ins = te.matmul(dstp[:, :],
                                            WkT_sb[:, c * D + argv * 128: c * D + (argv + 1) * 128],
                                            xn[:, c, :], start=(c == 0), stop=(c == CC - 1))
                        elif kind == "V":
                            kt, half = argv
                            ins = te.matmul(dstp[:, :],
                                            xn[:, c, kt * 128:(kt + 1) * 128],
                                            WvT_sb[:, c * D + half * 512: c * D + (half + 1) * 512],
                                            start=(c == 0), stop=(c == CC - 1))
                        else:
                            ins = te.matmul(dstp[:, :],
                                            WqT_sb[:, c * D + argv * 128: c * D + (argv + 1) * 128],
                                            xn[:, c, :], start=(c == 0), stop=(c == CC - 1))
                        if c == CC - 1:
                            ins.then_inc(s_kmm, 1)
                    gidx += 1
            te.wait_ge(s_kev, len(kvq_sched))
            te.sem_inc(s_ph, 1)

        PH[0] += 5

        if dump == "A":
            _emit_dump(block, PH[0], s_ph, s_od, [
                (dumps["d_qt"].ap(), QT_all[:, :]),
                (dumps["d_kt"].ap(), KT_dram.ap()),
                (dumps["d_v"].ap(), V_dram.ap()),
            ])
            a.close()
            s_ab.close()
            return nc, dumps

        a.close()
        b = ExitStack()
        bsb = lambda n, sh, d: b.enter_context(nc.sbuf_tensor(n, sh, d))
        bps = lambda n, sh, d: b.enter_context(nc.psum_tensor(n, sh, d))
        KT_buf = [bsb(f"KT_b{i}", [128, LK], BF16) for i in range(2)]
        V_buf = [bsb(f"V_b{i}", [128, 2 * NKT * VSLOT], BF16) for i in range(2)]
        P_buf = bsb("P_b", [128, 4 * 1024], BF16)
        OT_stage = [bsb(f"OT_st{i}", [64, 2 * NQ], BF16) for i in range(2)]
        OTF32 = bsb("OTF32", [VSLOT, 2048], F32)
        rinv = bsb("rinv", [128, 1024], F32)
        rb_sbA = bsb("rb_sbA", [64, 512], F32)
        rb_sbB = bsb("rb_sbB", [64, 512], F32)
        ps_S = [bps(f"ps_S{i}", [128, 1024], F32) for i in range(2)]
        ps_OA = bps("ps_OA", [VSLOT, 512], F32)
        ps_OB = bps("ps_OB", [VSLOT, 512], F32)
        ps_RA = bps("ps_RA", [64, 512], F32)
        ps_RB = bps("ps_RB", [64, 512], F32)
        NKTG = NPQ * NKT

        @block.gpsimd
        def _(gp):
            gp.wait_ge(s_ph, PH[0])

            def load(p):
                if p >= 2:
                    gp.wait_ge(s_rb, QB * (p - 1))
                gp.dma_start(out=KT_buf[p % 2][:, :],
                             in_=KT_dram.ap()[:, p, :]).then_inc(s_kvld, 16)
                gp.dma_start(
                    out=V_buf[p % 2][:, :].rearrange("p (h c) -> p h c", h=2),
                    in_=V_dram.ap()[:, 2 * p:2 * p + 2, :],
                ).then_inc(s_kvld, 16)

            def spill(p):
                gp.wait_ge(s_nst, QB * (p + 1))
                gp.dma_start(out=OT_dram.ap()[2 * p, :, :],
                             in_=OT_stage[p % 2][:, 0:NQ]).then_inc(s_ot, 16)
                gp.dma_start(out=OT_dram.ap()[2 * p + 1, :, :],
                             in_=OT_stage[p % 2][:, NQ:2 * NQ]).then_inc(s_ot, 16)

            load(0)
            load(1)
            W1src = [W1.ap()[ei] for ei in range(8)] + [Ws1.ap()]
            W2src = [W2.ap()[ei] for ei in range(8)] + [Ws2.ap()]
            for p in range(2, PAIRS):
                load(p)
                spill(p - 2)
            for ei in range(NE):
                gp.dma_start(
                    out=Wf1_dram.ap()[ei].rearrange("p (cc f) -> p cc f", cc=CC),
                    in_=W1src[ei].rearrange("(cc p) f -> p cc f", p=128),
                ).then_inc(s_wc, 16)
                gp.dma_start(
                    out=Wf2_dram.ap()[ei].rearrange("p (fc d) -> p fc d", fc=FC),
                    in_=W2src[ei].rearrange("(fc p) d -> p fc d", p=128),
                ).then_inc(s_wc, 16)
            spill(PAIRS - 2)
            spill(PAIRS - 1)
            gp.wait_ge(s_ot, 32 * PAIRS)
            gp.wait_ge(s_wc, 48 + 18 * 16)
            gp.sem_inc(s_ph, 1)

        def _score(te, g, kt_b):
            p, rem = divmod(g, QB * NKT)
            q, kt = divmod(rem, NKT)
            te.matmul(ps_S[g % 2][:, 0:512],
                      kt_b[0:64, kt * 128:(kt + 1) * 128],
                      QT_all[0:64, p * NQ + q * 512: p * NQ + (q + 1) * 512],
                      start=True, stop=True, tile_position=(0, 0))
            te.matmul(ps_S[g % 2][:, 512:1024],
                      kt_b[64:128, kt * 128:(kt + 1) * 128],
                      QT_all[64:128, p * NQ + q * 512: p * NQ + (q + 1) * 512],
                      start=True, stop=True,
                      tile_position=(64, 0)).then_inc(s_sc, 1)

        def _av(te, g, v_b):
            p, rem = divmod(g, QB * NKT)
            q, kt = divmod(rem, NKT)
            pq = p * QB + q
            if kt == 0:
                te.wait_ge(s_nrm, pq)
            sl = P_buf[:, (g % 4) * 1024:(g % 4 + 1) * 1024]
            te.matmul(ps_OA[:, :], v_b[:, 0, kt, :], sl[:, 0:512],
                      start=(kt == 0), stop=(kt == NKT - 1))
            te.matmul(ps_OB[:, :], v_b[:, 1, kt, :], sl[:, 512:1024],
                      start=(kt == 0), stop=(kt == NKT - 1)).then_inc(s_av, 1)
            if kt == 8 and pq > 0:
                te.wait_ge(s_rinv, RIA + pq)
                te.matmul(ps_RA[:, :], ones64[64:65, :], rinv[64:65, 0:512],
                          start=True, stop=True)
                te.matmul(ps_RB[:, :], ones64[64:65, :], rinv[64:65, 512:1024],
                          start=True, stop=True).then_inc(s_rb, 1)

        @block.tensor
        def _(te):
            te.wait_ge(s_ph, PH[0])
            for p in range(PAIRS):
                te.wait_ge(s_kvld, 32 * (p + 1))
                kt_b = KT_buf[p % 2]
                v_b = V_buf[p % 2][:, :].rearrange(
                    "p (h kt c) -> p h kt c", h=2, kt=NKT)
                for gl in range(QB * NKT):
                    g = p * QB * NKT + gl
                    if g >= 2:
                        te.wait_ge(s_exp, g - 1)
                    _score(te, g, kt_b)
                    if gl > 0:
                        te.wait_ge(s_exp, g)
                        _av(te, g - 1, v_b)
                g = (p + 1) * QB * NKT
                te.wait_ge(s_exp, g)
                _av(te, g - 1, v_b)
            te.wait_ge(s_rinv, RIA + NPQ)
            te.matmul(ps_RA[:, :], ones64[64:65, :], rinv[64:65, 0:512],
                      start=True, stop=True)
            te.matmul(ps_RB[:, :], ones64[64:65, :], rinv[64:65, 512:1024],
                      start=True, stop=True).then_inc(s_rb, 1)
            te.wait_ge(s_nst, NPQ)
            te.sem_inc(s_ph, 1)

        @block.scalar
        def _(sc):
            sc.wait_ge(s_ph, PH[0])
            for g in range(NKTG):
                sc.wait_ge(s_sc, g + 1)
                if g >= 4:
                    sc.wait_ge(s_av, g - 3)
                sc.activation(P_buf[:, (g % 4) * 1024:(g % 4 + 1) * 1024],
                              ps_S[g % 2][:, :], AF.Exp,
                              bias=biasC[:, :], scale=0.125).then_inc(s_exp, 1)
            sc.sem_inc(s_ph, 1)

        @block.vector
        def _(v):
            v.wait_ge(s_ph, PH[0])
            for pq in range(NPQ):
                p, q = divmod(pq, QB)
                h = (pq % 2) * 1024
                v.wait_ge(s_av, NKT * (pq + 1))
                v.tensor_copy(OTF32[:, h:h + 512], ps_OA[:, :])
                v.tensor_copy(OTF32[:, h + 512:h + 1024],
                              ps_OB[:, :]).then_inc(s_nrm, 1)
                v.reciprocal(rinv[64:65, 0:512], OTF32[64:65, h:h + 512])
                v.reciprocal(rinv[64:65, 512:1024],
                             OTF32[64:65, h + 512:h + 1024]).then_inc(s_rinv, 1)
                v.wait_ge(s_rb, pq + 1)
                if p >= 2 and q == 0:
                    v.wait_ge(s_ot, 32 * (p - 1))
                v.tensor_copy(rb_sbA[:, :], ps_RA[:, :])
                v.tensor_copy(rb_sbB[:, :], ps_RB[:, :])
                v.tensor_tensor(OT_stage[p % 2][:, q * 512:(q + 1) * 512],
                                OTF32[0:64, h:h + 512], rb_sbA[:, :], ALU.mult)
                v.tensor_tensor(OT_stage[p % 2][:, NQ + q * 512: NQ + (q + 1) * 512],
                                OTF32[0:64, h + 512:h + 1024], rb_sbB[:, :],
                                ALU.mult).then_inc(s_nst, 1)
            v.sem_inc(s_ph, 1)

        @block.sync
        def _(sync):
            sync.sem_inc(s_ph, 1)

        PH[0] += 5

        if dump == "B":
            _emit_dump(block, PH[0], s_ph, s_od, [
                (dumps["d_ot"].ap(), OT_dram.ap()),
            ])
            b.close()
            s_ab.close()
            return nc, dumps

        b.close()
        s_ab.close()
        c_x = ExitStack()
        x1 = c_x.enter_context(nc.sbuf_tensor("x1", [128, 16 * D], F32))
        c = ExitStack()
        OT_pack = c.enter_context(nc.sbuf_tensor("OT_pack", [128, PAIRS * NQ], BF16))
        WoT_sb = c.enter_context(nc.sbuf_tensor("WoT_sb", [128, PAIRS * D], BF16))
        bobc = c.enter_context(nc.sbuf_tensor("bobc", [128, D], F32))
        ps_OP = [c.enter_context(nc.psum_tensor(f"ps_OP{i}", [128, 512], F32))
                 for i in range(2)]
        NCLD = 19 * 16

        @block.sync
        def _(sync):
            sync.wait_ge(s_ph, PH[0])
            sync.dma_start(
                out=x1[:, :].rearrange("p (a d) -> p a d", a=16),
                in_=xb.ap().rearrange("(a p) d -> p a d", p=128)[:, 0:16, :],
            ).then_inc(s_cld, 16)
            sync.sem_inc(s_ph, 1)

        @block.gpsimd
        def _(gp):
            gp.wait_ge(s_ph, PH[0])
            gp.dma_start(out=bobc[:, :],
                         in_=bass.AP(bo, 0, [[0, 128], [1, D]])).then_inc(s_cld, 16)
            gp.dma_start(
                out=WoT_sb[:, :].rearrange("p (pr d) -> p pr d", pr=PAIRS),
                in_=WoT.ap().rearrange("(pr p) d -> p pr d", p=128),
            ).then_inc(s_cld, 16)
            for pr in range(PAIRS):
                gp.dma_start(out=OT_pack[0:64, pr * NQ:(pr + 1) * NQ],
                             in_=OT_dram.ap()[2 * pr, :, :]).then_inc(s_cld, 16)
                gp.dma_start(out=OT_pack[64:128, pr * NQ:(pr + 1) * NQ],
                             in_=OT_dram.ap()[2 * pr + 1, :, :]).then_inc(s_cld, 16)
            gp.sem_inc(s_ph, 1)

        @block.vector
        def _(v):
            v.wait_ge(s_ph, PH[0])
            v.wait_ge(s_cld, NCLD)
            for qt in range(16):
                ins = v.tensor_tensor(x1[:, qt * D:(qt + 1) * D],
                                      x1[:, qt * D:(qt + 1) * D], bobc[:, :], ALU.add)
            ins.then_inc(s_cev, 1)
            for blk in range(32):
                qt, dmb = divmod(blk, 2)
                v.wait_ge(s_cmm, blk + 1)
                sl = x1[:, qt * D + dmb * 512: qt * D + (dmb + 1) * 512]
                v.tensor_tensor(sl, ps_OP[blk % 2][:, :], sl,
                                ALU.add).then_inc(s_cev, 1)
            v.sem_inc(s_ph, 1)

        @block.tensor
        def _(te):
            te.wait_ge(s_ph, PH[0])
            te.wait_ge(s_cld, NCLD)
            for blk in range(32):
                qt, dmb = divmod(blk, 2)
                if blk >= 2:
                    te.wait_ge(s_cev, blk)
                for pr in range(PAIRS):
                    ins = te.matmul(ps_OP[blk % 2][:, :],
                                    OT_pack[:, pr * NQ + qt * 128: pr * NQ + (qt + 1) * 128],
                                    WoT_sb[:, pr * D + dmb * 512: pr * D + (dmb + 1) * 512],
                                    start=(pr == 0), stop=(pr == PAIRS - 1))
                    if pr == PAIRS - 1:
                        ins.then_inc(s_cmm, 1)
            te.wait_ge(s_cev, 33)
            te.sem_inc(s_ph, 1)

        @block.scalar
        def _(sc):
            sc.wait_ge(s_ph, PH[0])
            sc.sem_inc(s_ph, 1)

        PH[0] += 5

        if dump == "C":
            _emit_dump(block, PH[0], s_ph, s_od, [
                (dumps["d_x1"].ap(), x1[:, :]),
            ])
            c.close()
            c_x.close()
            return nc, dumps

        c.close()
        dk = ExitStack()
        xn2T = dk.enter_context(nc.sbuf_tensor("xn2T", [128, CC * NQ], BF16))
        probT = dk.enter_context(nc.sbuf_tensor("probT", [8, NQ], BF16))
        sel_bf = dk.enter_context(nc.sbuf_tensor("sel_bf", [8, 1024], BF16))
        dt_ = ExitStack()
        dsb = lambda n, s, ddt: dt_.enter_context(nc.sbuf_tensor(n, s, ddt))
        dps = lambda n, s, ddt: dt_.enter_context(nc.psum_tensor(n, s, ddt))
        xn2_t = [dsb(f"xn2_t{i}", [128, D], BF16) for i in range(2)]
        d2_s1 = dsb("d2_s1", [128, 4], F32)
        d2_s2 = dsb("d2_s2", [128, 4], F32)
        d2_m = dsb("d2_m", [128, 4], F32)
        d2_q = dsb("d2_q", [128, 4], F32)
        d2_lt = dsb("d2_lt", [128, 4], F32)
        d2_t = dsb("d2_t", [128, 4], F32)
        d2_v = dsb("d2_v", [128, 4], F32)
        d2_r = dsb("d2_r", [128, 4], F32)
        Wr_sb = dsb("Wr_sb", [128, CC * 8], BF16)
        logits = dsb("logits", [128, 128], F32)
        ismax = dsb("ismax", [128, 16 * 8], F32)
        maskd = dsb("maskd", [128, 16 * 8], F32)
        nmax = dsb("nmax", [128, 16], F32)
        m1c = dsb("m1c", [128, 16], F32)
        v2c = dsb("v2c", [128, 16], F32)
        dcol = dsb("dcol", [128, 16], F32)
        ecol = dsb("ecol", [128, 16], F32)
        e2col = dsb("e2col", [128, 16], F32)
        e3col = dsb("e3col", [128, 16], F32)
        gcol = dsb("gcol", [128, 16], F32)
        g2col = dsb("g2col", [128, 16], F32)
        g3col = dsb("g3col", [128, 16], F32)
        prob_all = dsb("prob_all", [128, 16 * 8], BF16)
        sel_i = dsb("sel_i", [8, 1024], I32)
        ps_T2 = [dps(f"ps_T2{i}", [128, 128], BF16) for i in range(4)]
        ps_LG = [dps(f"ps_LG{i}", [128, 8], F32) for i in range(2)]
        ps_PT = dps("ps_PT", [8, 128], BF16)

        x1r = x1[:, :].rearrange("p (qt d) -> p qt d", qt=16)

        @block.gpsimd
        def _(gp):
            gp.wait_ge(s_ph, PH[0])
            gp.dma_start(
                out=Wr_sb[:, :].rearrange("p (cc e) -> p cc e", cc=CC),
                in_=Wr.ap().rearrange("(cc p) e -> p cc e", p=128),
            ).then_inc(s_wc, 16)
            gp.iota(sel_i[:, :], [[1, 8], [0, 128]], channel_multiplier=-1,
                    allow_small_or_imprecise_dtypes=True).then_inc(s_wc, 1)
            gp.sem_inc(s_ph, 1)

        @block.sync
        def _(sync):
            sync.wait_ge(s_ph, PH[0])
            sync.sem_inc(s_ph, 1)

        @block.vector
        def _(v):
            v.wait_ge(s_ph, PH[0])
            v.wait_ge(s_wc, 48 + 288 + 17)
            v.tensor_scalar(sel_bf[:, :], sel_i[:, :], 0.0, None, ALU.is_equal)
            for g4 in range(4):
                for t in range(4):
                    xt = x1r[:, g4 * 4 + t, :]
                    sq = scrq[:, t * D:(t + 1) * D]
                    v.reduce_sum(d2_s1[:, t:t + 1], xt, axis=AX.X)
                    v.tensor_tensor(sq, xt, xt, ALU.mult)
                    v.reduce_sum(d2_s2[:, t:t + 1], sq, axis=AX.X)
                v.tensor_scalar(d2_m[:, :], d2_s1[:, :], 1.0 / D, None, ALU.mult)
                v.tensor_scalar(d2_q[:, :], d2_s2[:, :], 1.0 / D, None, ALU.mult)
                v.tensor_copy(junk0[:, 0:4], d2_m[:, :])
                v.tensor_copy(junk0[:, 4:8], d2_m[:, :])
                v.tensor_tensor(d2_t[:, :], d2_m[:, :], d2_m[:, :], ALU.mult)
                v.tensor_copy(junk0[:, 0:4], d2_t[:, :])
                v.tensor_copy(junk0[:, 4:8], d2_t[:, :])
                v.tensor_tensor(d2_v[:, :], d2_q[:, :], d2_t[:, :],
                                ALU.subtract).then_inc(s_ln, 1)
                v.wait_ge(s_rinv, RIB + g4 + 1)
                for t in range(4):
                    ti = g4 * 4 + t
                    if ti >= 2:
                        v.wait_ge(s_tp, TPA + 8 * (ti - 1))
                    v.tensor_scalar(xn2_t[ti % 2][:, :], x1r[:, ti, :],
                                    d2_m[:, t:t + 1], d2_r[:, t:t + 1],
                                    ALU.subtract, ALU.mult).then_inc(s_ln, 1)
                    dst = xn2T[:, :].rearrange("p (cc c) -> p cc c", cc=CC)
                    for cc in range(CC):
                        gt = ti * 8 + cc
                        v.wait_ge(s_tp, TPA + gt + 1)
                        v.tensor_copy(dst[:, cc, ti * 128:(ti + 1) * 128],
                                      ps_T2[gt % 4][:, :]).then_inc(s_xnb, 1)
            for qt in range(16):
                v.wait_ge(s_lg, qt + 1)
                v.tensor_copy(logits[:, qt * 8:(qt + 1) * 8],
                              ps_LG[qt % 2][:, :]).then_inc(s_lge, 1)
            for qt in range(16):
                v.tensor_reduce(m1c[:, qt:qt + 1],
                                logits[:, qt * 8:(qt + 1) * 8], AX.X, ALU.max)
            for qt in range(16):
                v.tensor_scalar(ismax[:, qt * 8:(qt + 1) * 8],
                                logits[:, qt * 8:(qt + 1) * 8],
                                m1c[:, qt:qt + 1], None, ALU.is_equal)
            for qt in range(16):
                v.tensor_reduce(nmax[:, qt:qt + 1],
                                ismax[:, qt * 8:(qt + 1) * 8], AX.X, ALU.add)
            for qt in range(16):
                v.tensor_scalar(maskd[:, qt * 8:(qt + 1) * 8],
                                ismax[:, qt * 8:(qt + 1) * 8], -1e30, None,
                                ALU.mult)
            for qt in range(16):
                msl = maskd[:, qt * 8:(qt + 1) * 8]
                v.tensor_tensor(msl, msl, logits[:, qt * 8:(qt + 1) * 8], ALU.add)
            for qt in range(16):
                v.tensor_reduce(v2c[:, qt:qt + 1],
                                maskd[:, qt * 8:(qt + 1) * 8], AX.X, ALU.max)
            v.tensor_tensor(dcol[:, :], v2c[:, :], m1c[:, :],
                            ALU.subtract).then_inc(s_rt, 1)
            v.wait_ge(s_rt, 2)
            v.tensor_scalar(e2col[:, :], ecol[:, :], 1.0, None, ALU.add)
            v.tensor_scalar(g2col[:, :], nmax[:, :], 1.5, None, ALU.is_le)
            v.tensor_copy(junk0[:, 0:8], e2col[:, 0:8])
            v.reciprocal(e3col[:, :], e2col[:, :])
            v.tensor_copy(junk0[:, 0:8], e3col[:, 0:8])
            v.tensor_copy(junk0[:, 0:8], e3col[:, 8:16])
            v.tensor_scalar(gcol[:, :], e3col[:, :], 1.0, None, ALU.is_ge)
            v.tensor_copy(junk0[:, 0:8], gcol[:, 0:8])
            v.tensor_copy(junk0[:, 0:8], gcol[:, 8:16])
            v.tensor_tensor(g3col[:, :], gcol[:, :], g2col[:, :], ALU.mult)
            v.tensor_copy(junk0[:, 0:8], g3col[:, 0:8])
            v.tensor_copy(junk0[:, 0:8], g3col[:, 8:16])
            for qt in range(16):
                ins = v.tensor_scalar(prob_all[:, qt * 8:(qt + 1) * 8],
                                      ismax[:, qt * 8:(qt + 1) * 8],
                                      g3col[:, qt:qt + 1], None, ALU.mult)
            ins.then_inc(s_rt, 1)
            for qt in range(16):
                v.wait_ge(s_lg, 16 + qt + 1)
                v.tensor_copy(probT[:, qt * 128:(qt + 1) * 128],
                              ps_PT[:, :]).then_inc(s_xnb, 1)
            v.sem_inc(s_ph, 1)

        @block.scalar
        def _(sc):
            sc.wait_ge(s_ph, PH[0])
            for g4 in range(4):
                sc.wait_ge(s_ln, LNA + 5 * g4 + 1)
                sc.activation(d2_lt[:, :], d2_v[:, :], AF.Ln, bias=epsc[:, :], scale=1.0)
                sc.activation(d2_r[:, :], d2_lt[:, :], AF.Exp, bias=0.0,
                              scale=-0.5).then_inc(s_rinv, 1)
            sc.wait_ge(s_rt, 1)
            sc.activation(ecol[:, :], dcol[:, :], AF.Exp, bias=0.0,
                          scale=1.0).then_inc(s_rt, 1)
            sc.sem_inc(s_ph, 1)

        @block.tensor
        def _(te):
            te.wait_ge(s_ph, PH[0])
            for ti in range(16):
                te.wait_ge(s_ln, LNA + 5 * (ti // 4) + (ti % 4) + 2)
                for cc in range(CC):
                    gt = ti * 8 + cc
                    if gt >= 4:
                        te.wait_ge(s_xnb, XNA + gt - 3)
                    te.transpose(ps_T2[gt % 4][:, :],
                                 xn2_t[ti % 2][:, cc * 128:(cc + 1) * 128],
                                 ident[:, :]).then_inc(s_tp, 1)
            te.wait_ge(s_xnb, XNA + 128)
            te.wait_ge(s_wc, 48 + 288 + 17)
            xn2 = xn2T[:, :].rearrange("p (cc c) -> p cc c", cc=CC)
            for qt in range(16):
                if qt >= 2:
                    te.wait_ge(s_lge, qt - 1)
                for cc in range(CC):
                    ins = te.matmul(ps_LG[qt % 2][:, :],
                                    xn2[:, cc, qt * 128:(qt + 1) * 128],
                                    Wr_sb[:, cc * 8:(cc + 1) * 8],
                                    start=(cc == 0), stop=(cc == CC - 1))
                    if cc == CC - 1:
                        ins.then_inc(s_lg, 1)
            te.wait_ge(s_rt, 3)
            for qt in range(16):
                if qt >= 1:
                    te.wait_ge(s_xnb, XNA + 128 + qt)
                te.transpose(ps_PT[:, :], prob_all[:, qt * 8:(qt + 1) * 8],
                             ident[:, :]).then_inc(s_lg, 1)
            te.wait_ge(s_xnb, XNA + 128 + 16)
            te.sem_inc(s_ph, 1)

        PH[0] += 5

        if dump == "D":
            _emit_dump(block, PH[0], s_ph, s_od, [
                (dumps["d_xn2"].ap(), xn2T[:, :]),
                (dumps["d_prob"].ap(), prob_all[:, :]),
                (dumps["d_probT"].ap(), probT[:, :]),
            ])
            dt_.close()
            dk.close()
            c_x.close()
            return nc, dumps

        dt_.close()
        e = ExitStack()
        esb = lambda n, sh, ddt: e.enter_context(nc.sbuf_tensor(n, sh, ddt))
        TW = NQ // 2
        W1_sb = [esb(f"W1_sb{i}", [128, CC * 512], BF16) for i in range(2)]
        W2_sb = [esb(f"W2_sb{i}", [128, FC * D], BF16) for i in range(2)]
        H_sb = [esb(f"H_sb{i}", [128, FC * TW], BF16) for i in range(2)]
        pb_sb = esb("pb_sb", [128, 8 * TW], BF16)
        r_f = [esb(f"r_f{i}", [128, 512], F32) for i in range(4)]
        r2_f = [esb(f"r2_f{i}", [128, 512], F32) for i in range(2)]
        ps_H = [e.enter_context(nc.psum_tensor(f"ps_H{i}", [128, 512], F32))
                for i in range(2)]
        ps_O = [e.enter_context(nc.psum_tensor(f"ps_O{i}", [128, 512], F32))
                for i in range(2)]
        ps_PB = e.enter_context(nc.psum_tensor("ps_PB", [128, 512], F32))

        out_r = out_d.ap().rearrange("(tt p) d -> p tt d", p=128)
        NBLK = 16

        @block.sync
        def _(sync):
            sync.wait_ge(s_ph, PH[0])
            for tq in range(2):
                for ei in range(NE):
                    gx = tq * NE + ei
                    if gx >= 2:
                        sync.wait_ge(s_hmm, (gx - 1) * 8)
                    sync.dma_start(out=W1_sb[gx % 2][:, :],
                                   in_=Wf1_dram.ap()[ei]).then_inc(s_wld, 16)

        @block.gpsimd
        def _(gp):
            gp.wait_ge(s_ph, PH[0])
            for tq in range(2):
                for ei in range(NE):
                    gx = tq * NE + ei
                    if gx >= 2:
                        gp.wait_ge(s_eomm, (gx - 1) * NBLK)
                    gp.dma_start(out=W2_sb[gx % 2][:, :],
                                 in_=Wf2_dram.ap()[ei]).then_inc(s_w2l, 16)
                gp.wait_ge(s_eoadd, NBLK * NE * (tq + 1))
                for tb in range(8):
                    tt = tq * 8 + tb
                    gp.dma_start(out=out_r[:, tt, :],
                                 in_=x1[:, tt * D:(tt + 1) * D]).then_inc(s_od, 16)
            gp.wait_ge(s_od, 16 * 16)

        @block.tensor
        def _(te):
            te.wait_ge(s_ph, PH[0])
            xn2 = xn2T[:, :].rearrange("p (cc c) -> p cc c", cc=CC)
            for tq in range(2):
                for ei in range(8):
                    for half in range(2):
                        pj = tq * 16 + ei * 2 + half
                        te.wait_ge(s_pbe, pj)
                        te.matmul(ps_PB[:, :], sel_bf[:, ei * 128:(ei + 1) * 128],
                                  probT[:, tq * TW + half * 512: tq * TW + (half + 1) * 512],
                                  start=True, stop=True).then_inc(s_pb, 1)
                for ei in range(NE):
                    gx = tq * NE + ei
                    te.wait_ge(s_wld, 16 * (gx + 1))
                    if gx >= 2:
                        te.wait_ge(s_eoadd, (gx - 1) * NBLK)
                    for fc in range(FC):
                        for half in range(2):
                            hg = gx * 8 + fc * 2 + half
                            if hg >= 2:
                                te.wait_ge(s_hact, hg - 1)
                            for cc in range(CC):
                                ins = te.matmul(
                                    ps_H[hg % 2][:, :],
                                    W1_sb[gx % 2][:, cc * 512 + fc * 128:
                                                  cc * 512 + (fc + 1) * 128],
                                    xn2[:, cc, tq * TW + half * 512:
                                        tq * TW + (half + 1) * 512],
                                    start=(cc == 0), stop=(cc == CC - 1))
                            ins.then_inc(s_hmm, 1)
                    te.wait_ge(s_w2l, 16 * (gx + 1))
                    te.wait_ge(s_he, gx * 8 + 8)
                    for blk in range(NBLK):
                        og = gx * NBLK + blk
                        tb, dmb = divmod(blk, 2)
                        if og >= 2:
                            te.wait_ge(s_eoadd, og - 1)
                        for fc in range(FC):
                            ins = te.matmul(
                                ps_O[og % 2][:, :],
                                H_sb[gx % 2][:, fc * TW + tb * 128:
                                             fc * TW + (tb + 1) * 128],
                                W2_sb[gx % 2][:, fc * D + dmb * 512:
                                              fc * D + (dmb + 1) * 512],
                                start=(fc == 0), stop=(fc == FC - 1))
                        ins.then_inc(s_eomm, 1)

        @block.scalar
        def _(sc):
            sc.wait_ge(s_ph, PH[0])
            for hg in range(2 * NE * 8):
                sc.wait_ge(s_hmm, hg + 1)
                if hg >= 2:
                    sc.wait_ge(s_he, hg - 1)
                sc.activation(r_f[hg % 4][:, :], ps_H[hg % 2][:, :],
                              AF.Relu, bias=0.0, scale=1.0)
                sc.activation(r2_f[hg % 2][:, :], r_f[hg % 4][:, :],
                              AF.Square, bias=0.0, scale=1.0).then_inc(s_hact, 1)

        @block.vector
        def _(v):
            v.wait_ge(s_ph, PH[0])
            for tq in range(2):
                for ei in range(8):
                    for half in range(2):
                        pj = tq * 16 + ei * 2 + half
                        v.wait_ge(s_pb, pj + 1)
                        v.tensor_copy(pb_sb[:, ei * TW + half * 512:
                                            ei * TW + (half + 1) * 512],
                                      ps_PB[:, :]).then_inc(s_pbe, 1)
                for ei in range(NE):
                    gx = tq * NE + ei
                    for fc in range(FC):
                        for half in range(2):
                            hg = gx * 8 + fc * 2 + half
                            v.wait_ge(s_hact, hg + 1)
                            if gx >= 2 and fc == 0 and half == 0:
                                v.wait_ge(s_eomm, (gx - 2) * NBLK + NBLK)
                            pbs = (onesbc[:, :] if ei == 8 else
                                   pb_sb[:, ei * TW + half * 512:
                                         ei * TW + (half + 1) * 512])
                            v.tensor_tensor(
                                H_sb[gx % 2][:, fc * TW + half * 512:
                                             fc * TW + (half + 1) * 512],
                                r2_f[hg % 2][:, :], pbs,
                                ALU.mult).then_inc(s_he, 1)
                    for blk in range(NBLK):
                        og = gx * NBLK + blk
                        tb, dmb = divmod(blk, 2)
                        v.wait_ge(s_eomm, og + 1)
                        tt = tq * 8 + tb
                        sl = x1[:, tt * D + dmb * 512: tt * D + (dmb + 1) * 512]
                        v.tensor_tensor(sl, ps_O[og % 2][:, :], sl,
                                        ALU.add).then_inc(s_eoadd, 1)

        e.close()
        dk.close()
        c_x.close()
    return nc, dumps


def _emit_dump(block, ph_val, s_ph, s_od, pairs):
    @block.gpsimd
    def _(gp):
        gp.wait_ge(s_ph, ph_val)
        n = 0
        for dst, src in pairs:
            gp.dma_start(out=dst, in_=src).then_inc(s_od, 16)
            n += 16
        gp.wait_ge(s_od, n)


_PROG_CACHE = {}


def _get_program(dump=None):
    if dump not in _PROG_CACHE:
        _PROG_CACHE[dump] = build_program(dump)
    return _PROG_CACHE[dump]


def make_in_maps(x, WqT, WkT, WvT, WoT, bo, Ws1, Ws2, W1, W2, Wr):
    in_maps = []
    for core in range(8):
        bidx, qidx = divmod(core, 4)
        xb_c = np.ascontiguousarray(np.roll(x[bidx], -qidx * NQ, axis=0))
        in_maps.append({
            "xb": xb_c, "WqT": WqT, "WkT": WkT, "WvT": WvT, "WoT": WoT,
            "bo": bo, "Ws1": Ws1, "Ws2": Ws2, "W1": W1, "W2": W2, "Wr": Wr,
        })
    return in_maps


def prep_inputs(L0_attn_x, L0_attn_W_q, L0_attn_W_k, L0_attn_W_v, L0_attn_W_o,
                L0_attn_b_o, L0_ffn_Ws1, L0_ffn_Ws2, L0_ffn_W1, L0_ffn_W2,
                L0_ffn_Wr):
    f = lambda v: np.ascontiguousarray(np.asarray(v, dtype=np.float32))
    ft = lambda v: np.ascontiguousarray(np.asarray(v, dtype=np.float32).T)
    return (f(L0_attn_x), ft(L0_attn_W_q), ft(L0_attn_W_k), ft(L0_attn_W_v),
            ft(L0_attn_W_o), f(L0_attn_b_o), f(L0_ffn_Ws1), f(L0_ffn_Ws2),
            f(L0_ffn_W1), f(L0_ffn_W2), f(L0_ffn_Wr))


def kernel(**inputs):
    arrs = prep_inputs(**inputs)
    nc, _ = _get_program()
    in_maps = make_in_maps(*arrs)
    res = run_bass_kernel_spmd(nc, in_maps, core_ids=list(range(8)))
    out = np.empty((2, LK, D), dtype=np.float32)
    for core in range(8):
        bidx, qidx = divmod(core, 4)
        out[bidx, qidx * NQ:(qidx + 1) * NQ] = np.asarray(
            res.results[core]["out"], dtype=np.float32)
    return out
